# version 08
# baseline (speedup 1.0000x reference)
import numpy as np
import ml_dtypes
from contextlib import ExitStack

import concourse.bass as bass
import concourse.mybir as mybir
from concourse.bass_utils import run_bass_kernel_spmd

F32 = mybir.dt.float32
BF16 = mybir.dt.bfloat16
AF = mybir.ActivationFunctionType
ALU = mybir.AluOpType
AX = mybir.AxisListType.X

NCOR = 8
S = 8192
D = 2048
NBLK = 16
TOK = 1024
I0 = 5632
IC0 = I0 // 128
E = 8
HE = 1408
HC = HE // 128
SCALE = float(1.0 / np.sqrt(np.float32(128.0)))
EPS = 1e-5
SQ128 = float(np.sqrt(np.float64(128.0)))

ENGINES = ("sync", "tensor", "vector", "scalar", "gpsimd")
DBG = {"qkv_n": 6, "qkv_evac": "ts", "qkv_stash": True}
BF16NP = ml_dtypes.bfloat16


def blocks_of(core):
    return (core, 15 - core)


def tok_index(core):
    b0, b1 = blocks_of(core)
    return np.concatenate([np.arange(512 * b0, 512 * b0 + 512),
                           np.arange(512 * b1, 512 * b1 + 512)])


def shard_slot(b):
    return (b, 0) if b < 8 else (15 - b, 1)


class Sem:
    def __init__(self, h):
        self.h = h
        self.n = 0


class Bld:
    def __init__(self, nc):
        self.nc = nc
        self.q = {e: [] for e in ENGINES}
        self.rel = {}
        self.dma_sems = set()

    def emit(self):
        with self.nc.Block() as block:
            for e in ENGINES:
                ops = self.q[e]
                getattr(block, e)(lambda eng, ops=ops: [op(eng) for op in ops])

    def wait(self, eng, sem, val=None):
        v = sem.n if val is None else val
        if v <= 0:
            return
        g = lambda E, s=sem.h, v=v: E.wait_ge(s, v)
        g.is_op = False
        g.sem = None
        self.q[eng].append(g)

    def release(self, key, sem):
        self.rel[key] = (sem, sem.n)

    def wait_release(self, eng, key):
        if key in self.rel:
            sem, v = self.rel[key]
            self.wait(eng, sem, v)

    def op(self, eng, fn, sem=None, inc=1):
        def g(E, fn=fn, sem=sem, inc=inc):
            i = fn(E)
            if sem is not None:
                i.then_inc(sem.h, inc)
            return i
        g.is_op = True
        g.sem = sem
        self.q[eng].append(g)
        if sem is not None:
            sem.n += inc

    def inc_last(self, eng, sem, inc=1):
        prev = self.q[eng].pop()
        assert getattr(prev, "is_op", False), "inc_last needs a real op"
        assert getattr(prev, "sem", None) is None, \
            "inc_last on an op that already has a sem update"
        def g(E, prev=prev, sem=sem, inc=inc):
            i = prev(E)
            i.then_inc(sem.h, inc)
            return i
        g.is_op = True
        g.sem = sem
        self.q[eng].append(g)
        sem.n += inc

    def dma(self, eng, out, in_, sem, inc=16):
        self.dma_sems.add(sem)
        self.wait(eng, sem)
        self.op(eng, lambda E, o=out, i=in_: E.dma_start(out=o, in_=i),
                sem, inc)

    def mm(self, out, lh, rh, start, stop, sem=None, inc=1):
        self.op("tensor",
                lambda E, o=out, l=lh, r=rh, a=start, b=stop:
                E.matmul(o, l, r, start=a, stop=b), sem, inc)

    def tr(self, out, in_, ident, sem=None):
        self.op("tensor",
                lambda E, o=out, i=in_, d=ident: E.transpose(o, i, d), sem)

    def cp(self, eng, out, in_, sem=None):
        if eng == "scalar":
            self.op(eng, lambda E, o=out, i=in_: E.copy(out=o, in_=i), sem)
        else:
            self.op(eng, lambda E, o=out, i=in_: E.tensor_copy(out=o, in_=i),
                    sem)

    def tt(self, out, a, c, alu, sem=None, eng="vector"):
        self.op(eng, lambda E, o=out, x=a, y=c, u=alu:
                E.tensor_tensor(o, x, y, u), sem)

    def ts(self, out, in0, s1, op0, s2=None, op1=None, sem=None, eng="vector"):
        if op1 is None:
            self.op(eng, lambda E, o=out, i=in0, a=s1, u=op0:
                    E.tensor_scalar(out=o, in0=i, scalar1=a, scalar2=None,
                                    op0=u), sem)
        else:
            self.op(eng, lambda E, o=out, i=in0, a=s1, c=s2, u=op0, v=op1:
                    E.tensor_scalar(out=o, in0=i, scalar1=a, scalar2=c,
                                    op0=u, op1=v), sem)

    def stt(self, out, in0, sc, in1, op0, op1, sem=None, eng="vector"):
        self.op(eng, lambda E, o=out, i=in0, s=sc, j=in1, u=op0, v=op1:
                E.scalar_tensor_tensor(out=o, in0=i, scalar=s, in1=j,
                                       op0=u, op1=v), sem)

    def act(self, out, in_, func, bias=0.0, scale=1.0, accum=None, sem=None):
        self.op("scalar", lambda E, o=out, i=in_, f=func, c=bias, s=scale,
                a=accum:
                E.activation(out=o, in_=i, func=f, bias=c, scale=s,
                             accum_out=a), sem)

    def red(self, out, in_, alu, sem=None):
        self.op("vector", lambda E, o=out, i=in_, u=alu:
                E.tensor_reduce(o, i, AX, u), sem)

    def recip(self, out, in_, sem=None):
        self.op("vector", lambda E, o=out, i=in_: E.reciprocal(o, i), sem)

    def nopinc(self, eng, sem, inc=1):
        self.op(eng, lambda E: E.nop(), sem, inc)

    def memset(self, ap, v, sem=None):
        self.op("gpsimd", lambda E, o=ap, w=v: E.memset(o, w), sem)

    def barrier(self, sem):
        for e in ENGINES:
            for ds in self.dma_sems:
                self.wait(e, ds)
        for e in ENGINES:
            idx = None
            for j in range(len(self.q[e]) - 1, -1, -1):
                if getattr(self.q[e][j], "is_op", False):
                    idx = j
                    break
            if idx is None:
                self.nopinc(e, sem)
                continue
            last_op = self.q[e][idx]
            if last_op.sem is not None:
                self.wait(e, last_op.sem)
                self.nopinc(e, sem)
            else:
                def g(E, prev=last_op, sem=sem):
                    i = prev(E)
                    i.then_inc(sem.h, 1)
                    return i
                g.is_op = True
                g.sem = sem
                self.q[e][idx] = g
                sem.n += 1
        tgt = sem.n
        for e in ENGINES:
            self.wait(e, sem, tgt)



def csplit(ap2d, p=128):
    return ap2d.rearrange("(mj p) o -> p mj o", p=p)


def dsplit(sbuf_ap, owid):
    return sbuf_ap.rearrange("p (mj o) -> p mj o", o=owid)

def emit_ln_tile(b, xb, scr, xntm, st, s_ve, s_sc):
    def v(out, fn):
        b.op("vector", fn, s_ve)
        b.wait("vector", s_ve)
    b.red(st["ssum"][:], xb[:], ALU.add, sem=s_ve)
    b.act(scr[:], xb[:], AF.Square, accum=st["ssq"][:], sem=s_sc)
    b.wait("vector", s_ve)
    v(None, lambda E, o=st["m"][:], i=st["ssum"][:]:
      E.tensor_scalar(out=o, in0=i, scalar1=1.0 / D, scalar2=None,
                      op0=ALU.mult))
    v(None, lambda E, o=st["mneg"][:], i=st["m"][:]:
      E.tensor_scalar(out=o, in0=i, scalar1=-1.0, scalar2=None, op0=ALU.mult))
    v(None, lambda E, o=st["msq"][:], i=st["m"][:]:
      E.tensor_tensor(o, i, i, ALU.mult))
    b.wait("vector", s_sc)
    v(None, lambda E, o=st["var"][:], i=st["ssq"][:]:
      E.tensor_scalar(out=o, in0=i, scalar1=1.0 / D, scalar2=None,
                      op0=ALU.mult))
    v(None, lambda E, o=st["var"][:], i=st["var"][:], j=st["msq"][:]:
      E.tensor_tensor(o, i, j, ALU.subtract))
    v(None, lambda E, o=st["var"][:], i=st["var"][:]:
      E.tensor_scalar(out=o, in0=i, scalar1=EPS, scalar2=None, op0=ALU.add))
    b.wait("scalar", s_ve)
    b.act(st["r2"][:], st["var"][:], AF.Ln, sem=s_sc)
    b.wait("scalar", s_sc)
    b.act(st["rstd"][:], st["r2"][:], AF.Exp, scale=-0.5, sem=s_sc)
    b.wait("scalar", s_sc)
    b.wait("vector", s_sc)
    v(None, lambda E, o=st["nb"][:], i=st["mneg"][:], j=st["rstd"][:]:
      E.tensor_tensor(o, i, j, ALU.mult))
    b.wait("scalar", s_ve)
    b.act(xntm[:], xb[:], AF.Identity, bias=st["nb"][:], scale=st["rstd"][:],
          sem=s_sc)
    b.release(("xnt",), s_sc)



def emit_p1(nc, b, sems, id32, stop_after=None):
    x = nc.dram_tensor("x", [S, D], F32, kind="ExternalInput")
    wq = nc.dram_tensor("wq", [128, 4096], F32, kind="ExternalInput")
    wk = nc.dram_tensor("wk", [128, 4096], F32, kind="ExternalInput")
    wv = nc.dram_tensor("wv", [128, 4096], F32, kind="ExternalInput")
    bqkv = nc.dram_tensor("bqkv", [128, 6], F32, kind="ExternalInput")
    wo = nc.dram_tensor("wo", [128, 2 * D], F32, kind="ExternalInput")
    biask = nc.dram_tensor("biask", [128, 128], F32, kind="ExternalInput")
    mrow = nc.dram_tensor("mrow", [2, S], BF16, kind="ExternalInput")
    mstrip = nc.dram_tensor("mstrip", [128, 896], F32, kind="ExternalInput")
    idbf = nc.dram_tensor("idbf", [128, 128], BF16, kind="ExternalInput")
    onesbf = nc.dram_tensor("onesbf", [1, 128], BF16, kind="ExternalInput")

    part_dump = (nc.dram_tensor("part_dump", [8 * D, TOK], F32,
                                kind="ExternalOutput")
                 if DBG.get("dump_all") else None)
    if DBG.get("dump_ln"):
        ln_dump = nc.dram_tensor("ln_dump", [128, 16], F32, kind="ExternalOutput")
        xn_dump = nc.dram_tensor("xn_dump", [128, D], F32, kind="ExternalOutput")
        xnh_dump = nc.dram_tensor("xnh_dump", [128, 16 * 512], BF16, kind="ExternalOutput")
        xnl_dump = nc.dram_tensor("xnl_dump", [128, 16 * 512], BF16, kind="ExternalOutput")
    if DBG.get("dump_mid"):
        qd_dump = nc.dram_tensor("qd_dump", [512, S], BF16, kind="ExternalOutput")
        kd_dump = nc.dram_tensor("kd_dump", [512, S], BF16, kind="ExternalOutput")
        vd_dump = nc.dram_tensor("vd_dump", [512, 64 * 132], BF16, kind="ExternalOutput")

    qd = nc.dram_tensor("qd", [512, S], BF16)
    kd = nc.dram_tensor("kd", [512, S], BF16)
    vd = nc.dram_tensor("vd", [512, 64 * 132], BF16)
    part_d = nc.dram_tensor("part_d", [16 * D, 512], F32)
    rs_d = nc.dram_tensor("rs_d", [2 * D, 512], F32)

    with ExitStack() as X:
        sb = X.enter_context
        id32_s = sb(nc.sbuf_tensor("id32s", [128, 128], F32))
        idbf_s = sb(nc.sbuf_tensor("idbfs", [128, 128], BF16))
        ones_s = sb(nc.sbuf_tensor("oness", [1, 128], BF16))
        bias_s = sb(nc.sbuf_tensor("biass", [128, 128], F32))
        mstr_s = sb(nc.sbuf_tensor("mstrs", [128, 896], F32))
        bqkv_s = sb(nc.sbuf_tensor("bqkvs", [128, 6], F32))
        st = {k: sb(nc.sbuf_tensor("st_" + k, [128, 1], F32))
              for k in ("ssum", "ssq", "m", "mneg", "msq", "var", "r2",
                        "rstd", "nb")}
        ctxf = sb(nc.sbuf_tensor("ctxf", [128, 2 * S], F32))

        p512 = [sb(nc.psum_tensor("p512a", [128, 512], F32)),
                sb(nc.psum_tensor("p512b", [128, 512], F32))]
        pT = [sb(nc.psum_tensor("pT0", [128, 128], F32)),
              sb(nc.psum_tensor("pT1", [128, 128], F32))]
        XPC = ExitStack()
        pc = [XPC.enter_context(nc.psum_tensor(f"pc{i}", [128, 132], F32))
              for i in range(4)]
        pT6 = [pT[0][:], pT[1][:]] + [pc[i][:, 0:128] for i in range(4)]
        assert nc.psum_banks_remaining >= 0
        assert nc.sbuf_bytes_remaining >= 0

        s_ld, s_x0, s_x1 = sems["s_ld"], sems["s_x0"], sems["s_x1"]
        s_cast, s_te, s_ve, s_sc = (sems["s_cast"], sems["s_te"],
                                    sems["s_ve"], sems["s_sc"])
        s_stash, s_qb, s_kv = sems["s_stash"], sems["s_qb"], sems["s_kv"]
        s_pd, s_cc, s_out, s_bar = (sems["s_pd"], sems["s_cc"],
                                    sems["s_out"], sems["s_bar"])
        s_x = [s_x0, s_x1]

        XA = ExitStack()
        sa = XA.enter_context
        wf32 = sa(nc.sbuf_tensor("wf32", [128, 4096], F32))
        wh = sa(nc.sbuf_tensor("wh", [128, 3 * 4096], BF16))
        wl = sa(nc.sbuf_tensor("wl", [128, 3 * 4096], BF16))
        xb = sa(nc.sbuf_tensor("xb", [128, 2 * D], F32))
        xntm = sa(nc.sbuf_tensor("xntm", [128, D], F32))
        xnh = sa(nc.sbuf_tensor("xnh", [128, 16 * 512], BF16))
        xnl = sa(nc.sbuf_tensor("xnl", [128, 16 * 512], BF16))
        qtmp = sa(nc.sbuf_tensor("qtmp", [128, 2 * 512], F32))
        sth = sa(nc.sbuf_tensor("sth", [128, 2 * 512], BF16))
        stl = sa(nc.sbuf_tensor("stl", [128, 2 * 512], BF16))
        vsgh = sa(nc.sbuf_tensor("vsgh", [128, 4 * 132], BF16))
        vsgl = sa(nc.sbuf_tensor("vsgl", [128, 4 * 132], BF16))
        scr = xntm

        b.dma("sync", id32_s[:], id32.ap(), s_ld)
        b.dma("sync", idbf_s[:], idbf.ap(), s_ld)
        b.dma("sync", ones_s[:], onesbf.ap(), s_ld)
        b.dma("sync", bias_s[:], biask.ap(), s_ld)
        b.dma("sync", mstr_s[:], mstrip.ap(), s_ld)
        b.dma("sync", bqkv_s[:], bqkv.ap(), s_ld)
        for i3, w3 in enumerate((wq, wk, wv)):
            sl = slice(4096 * i3, 4096 * (i3 + 1))
            b.dma("sync", wf32[:], w3.ap(), s_ld)
            b.wait("gpsimd", s_ld)
            b.dma("gpsimd", wh[:, sl], wf32[:], s_cast)
            b.wait("vector", s_cast)
            b.tt(wl[:, sl], wf32[:], wh[:, sl], ALU.subtract, sem=s_ve)
            b.wait("sync", s_ve)
        b.memset(vsgl[:], 0.0)
        b.memset(vsgh[:], 0.0)
        for i in range(4):
            b.memset(vsgh[:, 132 * i + 128:132 * i + 129], 1.0)
        b.barrier(s_bar)

        _ntile = {'prep': 0, 'A1t': 1, 'A1': 4, 'A2': 8}.get(stop_after, S // 128)
        for t in range(_ntile):
            blk, stn = t // 4, t % 4
            p = t % 2
            xt = xb[:, D * p:D * (p + 1)]
            b.wait_release("sync", ("xb", p))
            b.dma("sync", xt, x.ap()[128 * t:128 * (t + 1), :], s_x[p])
            b.wait("vector", s_x[p])
            b.wait("scalar", s_x[p])
            emit_ln_tile(b, xt, scr, xntm, st, s_ve, s_sc)
            b.release(("xb", p), s_sc)
            if stn == 0:
                b.wait_release("scalar", ("xn",))
                b.wait_release("vector", ("xn",))
            if DBG.get("dump_ln") and t == DBG.get("dump_ln_tile", 0):
                b.wait("sync", s_sc)
                for ki, kk in enumerate(("ssum", "ssq", "m", "var", "r2",
                                         "rstd", "nb")):
                    b.op("sync", lambda E, o=ln_dump.ap()[:, ki:ki + 1],
                         i=st[kk][:]: E.dma_start(
                             out=o, in_=i, allow_slow_non_contiguous=True),
                         sems["s_out"], 16)
                b.dma("sync", xn_dump.ap(), xntm[:], sems["s_out"])
                b.wait("vector", sems["s_out"])
                b.wait("scalar", sems["s_out"])
                b.wait("tensor", sems["s_out"])
            for j in range(16):
                r = j % 6
                tp = pT6[r]
                b.wait_release("tensor", ("pT", r))
                b.wait_release("tensor", ("xnt",))
                b.tr(tp, xntm[:, 128 * j:128 * (j + 1)], id32_s[:], s_te)
                col = 512 * j + 128 * stn
                b.wait("scalar", s_te)
                b.cp("scalar", xnh[:, col:col + 128], tp, s_sc)
                b.wait("vector", s_sc)
                b.tt(xnl[:, col:col + 128], tp, xnh[:, col:col + 128],
                     ALU.subtract, sem=s_ve)
                b.release(("pT", r), s_ve)
            b.release(("xntm",), s_te)
            b.wait_release("scalar", ("xntm",))
            if DBG.get("dump_ln") and t == 3:
                b.wait("sync", s_ve)
                b.dma("sync", xnh_dump.ap(), xnh[:], sems["s_out"])
                b.dma("sync", xnl_dump.ap(), xnl[:], sems["s_out"])
            if stn != 3:
                continue
            for m3 in range(3):
                for oc in range(2):
                    if m3 * 2 + oc >= DBG["qkv_n"]:
                        continue
                    g = (m3 * 2 + oc) % 2
                    ps = p512[g]
                    b.wait_release("tensor", ("p512", g))
                    b.wait("tensor", s_ve)
                    nmj = DBG.get("qkv_mj", 16)
                    npass = DBG.get("qkv_pass", 3)
                    for mj in range(nmj):
                        wc = slice(4096 * m3 + 256 * mj + 128 * oc,
                                   4096 * m3 + 256 * mj + 128 * (oc + 1))
                        xc = slice(512 * mj, 512 * (mj + 1))
                        b.mm(ps[:], wh[:, wc], xnh[:, xc], mj == 0,
                             npass == 1 and mj == nmj - 1)
                        if npass >= 2:
                            b.mm(ps[:], wh[:, wc], xnl[:, xc], False,
                                 npass == 2 and mj == nmj - 1)
                        if npass >= 3:
                            b.mm(ps[:], wl[:, wc], xnh[:, xc], False,
                                 mj == nmj - 1)
                    b.inc_last("tensor", s_te)
                    if m3 == 2 and oc == 1:
                        b.release(("xn",), s_te)
                    qcol = slice(512 * g, 512 * (g + 1))
                    b.wait_release("vector", ("stage", g))
                    b.wait("vector", s_te)
                    if DBG["qkv_evac"] == "ts":
                        b.ts(qtmp[:, qcol], ps[:],
                             bqkv_s[:, 2 * m3 + oc:2 * m3 + oc + 1], ALU.add,
                             sem=s_ve)
                    else:
                        b.cp("vector", qtmp[:, qcol], ps[:])
                        b.inc_last("vector", s_ve)
                    b.release(("p512", g), s_ve)
                    b.wait("scalar", s_ve)
                    b.cp("scalar", sth[:, qcol], qtmp[:, qcol], s_sc)
                    b.wait("vector", s_sc)
                    b.tt(stl[:, qcol], qtmp[:, qcol], sth[:, qcol],
                         ALU.subtract, sem=s_ve)
                    if not DBG["qkv_stash"]:
                        b.release(("stage", g), s_ve)
                    elif m3 < 2:
                        dd = qd if m3 == 0 else kd
                        tsl = slice(512 * blk, 512 * (blk + 1))
                        b.wait("sync", s_ve)
                        b.dma("sync", dd.ap()[128 * oc:128 * (oc + 1), tsl],
                              sth[:, qcol], s_stash)
                        b.dma("sync",
                              dd.ap()[256 + 128 * oc:256 + 128 * (oc + 1),
                                      tsl],
                              stl[:, qcol], s_stash)
                        b.release(("stage", g), s_stash)
                    else:
                        b.wait_release("scalar", ("vsg",))
                        b.wait_release("vector", ("vsg",))
                        for i in range(4):
                            u = i % 2
                            tp = pT[u]
                            b.wait_release("tensor", ("pT", u))
                            b.wait("tensor", s_ve)
                            b.tr(tp[:],
                                 qtmp[:, 512 * g + 128 * i:
                                      512 * g + 128 * (i + 1)],
                                 id32_s[:], s_te)
                            b.wait("scalar", s_te)
                            b.cp("scalar", vsgh[:, 132 * i:132 * i + 128],
                                 tp[:], s_sc)
                            b.wait("vector", s_sc)
                            b.tt(vsgl[:, 132 * i:132 * i + 128], tp[:],
                                 vsgh[:, 132 * i:132 * i + 128],
                                 ALU.subtract, sem=s_ve)
                            b.release(("pT", u), s_ve)
                        b.wait("sync", s_ve)
                        vsl = slice(132 * 4 * blk, 132 * 4 * (blk + 1))
                        b.dma("sync", vd.ap()[128 * oc:128 * (oc + 1), vsl],
                              vsgh[:], s_stash)
                        b.dma("sync",
                              vd.ap()[256 + 128 * oc:256 + 128 * (oc + 1),
                                      vsl],
                              vsgl[:], s_stash)
                        b.release(("stage", g), s_stash)
                        b.release(("vsg",), s_stash)

        b.barrier(s_bar)
        if DBG.get("dump_mid"):
            b.dma("gpsimd", qd_dump.ap(), qd.ap(), sems["s_out"])
            b.dma("gpsimd", kd_dump.ap(), kd.ap(), sems["s_out"])
            b.dma("gpsimd", vd_dump.ap(), vd.ap(), sems["s_out"])
            b.wait("gpsimd", sems["s_out"])
        XA.close()
        XC = ExitStack()
        sc_ = XC.enter_context
        kh = sc_(nc.sbuf_tensor("khb", [128, S], BF16))
        kl = sc_(nc.sbuf_tensor("klb", [128, S], BF16))
        vh = sc_(nc.sbuf_tensor("vhb", [128, 64 * 132], BF16))
        vl = sc_(nc.sbuf_tensor("vlb", [128, 64 * 132], BF16))
        qhb = sc_(nc.sbuf_tensor("qhb", [128, 2 * 512], BF16))
        qlb = sc_(nc.sbuf_tensor("qlb", [128, 2 * 512], BF16))
        ptb = sc_(nc.sbuf_tensor("ptb", [128, 2 * 512], F32))
        phb = sc_(nc.sbuf_tensor("phb", [128, 2 * 512], BF16))
        plb = sc_(nc.sbuf_tensor("plb", [128, 2 * 512], BF16))
        ctmp = sc_(nc.sbuf_tensor("ctmp", [128, 2 * 128], F32))
        linv = sc_(nc.sbuf_tensor("linv", [128, 2], F32))
        mrow_s = sc_(nc.sbuf_tensor("mrows", [1, S], BF16))

        for hh in range(DBG.get('c_heads', 2) if stop_after in (None, 'C', 'Dnc') else 0):
            b.wait_release("sync", ("kvbuf",))
            b.dma("sync", kh[:], kd.ap()[128 * hh:128 * (hh + 1), :], s_kv)
            b.dma("sync", kl[:], kd.ap()[256 + 128 * hh:256 + 128 * (hh + 1),
                                         :], s_kv)
            b.dma("sync", vh[:], vd.ap()[128 * hh:128 * (hh + 1), :], s_kv)
            b.dma("sync", vl[:], vd.ap()[256 + 128 * hh:256 + 128 * (hh + 1),
                                         :], s_kv)
            b.dma("sync", mrow_s[:], mrow.ap()[hh:hh + 1, :], s_kv)
            b.wait("tensor", s_kv)
            for qb in range(DBG.get('c_qb', NBLK)):
                g2 = qb % 2
                qsl = slice(512 * g2, 512 * (g2 + 1))
                b.wait_release("sync", ("qblk", g2))
                b.dma("sync", qhb[:, qsl],
                      qd.ap()[128 * hh:128 * (hh + 1),
                              512 * qb:512 * (qb + 1)], s_qb)
                b.dma("sync", qlb[:, qsl],
                      qd.ap()[256 + 128 * hh:256 + 128 * (hh + 1),
                              512 * qb:512 * (qb + 1)], s_qb)
                b.wait("tensor", s_qb)
                b.wait_release("tensor", ("pcx",))
                nkt = 4 * qb + 4

                def emit_scores(kt, qb=qb, qsl=qsl, hh=hh):
                    u = kt % 2
                    sp = p512[u]
                    b.wait_release("tensor", ("sp", u))
                    ksl = slice(128 * kt, 128 * (kt + 1))
                    b.mm(sp[:], kh[:, ksl], qhb[:, qsl], True, False)
                    b.mm(sp[:], kh[:, ksl], qlb[:, qsl], False, False)
                    b.mm(sp[:], kl[:, ksl], qhb[:, qsl], False, False)
                    b.mm(sp[:], ones_s[:],
                         mrow_s[0:1, 512 * qb:512 * (qb + 1)], False, True,
                         sem=s_te)
                    if kt >= 4 * qb:
                        off = 384 - (128 * kt - 512 * qb)
                        b.wait("vector", s_te)
                        b.tt(sp[:], sp[:], mstr_s[:, off:off + 512], ALU.add,
                             sem=s_ve)
                        b.wait("scalar", s_ve)
                    b.wait_release("scalar", ("pt", u))
                    b.wait("scalar", s_te)
                    ptc = ptb[:, 512 * u:512 * (u + 1)]
                    b.act(ptc, sp[:], AF.Exp,
                          bias=bias_s[:, 2 * kt + hh:2 * kt + hh + 1],
                          scale=SCALE, sem=s_sc)
                    b.release(("sp", u), s_sc)
                    b.wait("scalar", s_sc)
                    phc = phb[:, 512 * u:512 * (u + 1)]
                    plc = plb[:, 512 * u:512 * (u + 1)]
                    b.wait_release("scalar", ("ph", u))
                    b.cp("scalar", phc, ptc, s_sc)
                    b.wait("vector", s_sc)
                    b.tt(plc, ptc, phc, ALU.subtract, sem=s_ve)
                    b.release(("pt", u), s_ve)
                    b.release(("phrdy", u), s_ve)

                def emit_pv(kt, nkt=nkt):
                    u = kt % 2
                    b.wait_release("tensor", ("phrdy", u))
                    vhc = vh[:, 132 * kt:132 * kt + 129]
                    vlc = vl[:, 132 * kt:132 * kt + 129]
                    for sq in range(4):
                        phs = phb[:, 512 * u + 128 * sq:
                                  512 * u + 128 * (sq + 1)]
                        pls = plb[:, 512 * u + 128 * sq:
                                  512 * u + 128 * (sq + 1)]
                        b.mm(pc[sq][:, 0:129], phs, vhc, kt == 0, False)
                        b.mm(pc[sq][:, 0:129], phs, vlc, False, False)
                        b.mm(pc[sq][:, 0:129], pls, vhc, False,
                             kt == nkt - 1)
                    b.inc_last("tensor", s_te)
                    b.release(("ph", u), s_te)

                for kt in range(nkt):
                    emit_scores(kt)
                    if kt >= 1:
                        emit_pv(kt - 1)
                emit_pv(nkt - 1)
                b.wait("vector", s_te)
                for sq in range(4):
                    b.recip(linv[:, sq % 2:sq % 2 + 1], pc[sq][:, 128:129],
                            sem=s_ve)
                    b.wait("vector", s_ve)
                    cts = ctmp[:, 128 * (sq % 2):128 * (sq % 2 + 1)]
                    b.wait_release("vector", ("ctmp", sq % 2))
                    b.ts(cts, pc[sq][:, 0:128], linv[:, sq % 2:sq % 2 + 1],
                         ALU.mult, sem=s_ve)
                    tp = pT[sq % 2]
                    b.wait_release("tensor", ("pT", sq % 2))
                    b.wait("tensor", s_ve)
                    b.tr(tp[:], cts, id32_s[:], s_te)
                    b.release(("ctmp", sq % 2), s_te)
                    b.wait("scalar", s_te)
                    ccol = S * hh + 512 * qb + 128 * sq
                    b.cp("scalar", ctxf[:, ccol:ccol + 128], tp[:], s_sc)
                    b.release(("pT", sq % 2), s_sc)
                b.release(("pcx",), s_ve)
                b.release(("qblk", g2), s_te)
            b.release(("kvbuf",), s_sc)

        b.barrier(s_bar)
        XC.close()
        XPC.close()
        XD = ExitStack()
        sd = XD.enter_context
        ctxh = sd(nc.sbuf_tensor("ctxh", [128, 2 * S], BF16))
        ctxl = sd(nc.sbuf_tensor("ctxl", [128, 2 * S], BF16))
        wof = sd(nc.sbuf_tensor("wof", [128, 2 * D], F32))
        woh = sd(nc.sbuf_tensor("woh", [128, 2 * D], BF16))
        wol = sd(nc.sbuf_tensor("wol", [128, 2 * D], BF16))
        ptm = sd(nc.sbuf_tensor("ptm", [128, 4 * 512], F32))
        pdx = [p512[0], p512[1],
               sd(nc.psum_tensor("pd2", [128, 512], F32)),
               sd(nc.psum_tensor("pd3", [128, 512], F32))]

        b.dma("sync", wof[:], wo.ap(), s_ld)
        b.wait("gpsimd", s_ld)
        b.dma("gpsimd", woh[:], wof[:], s_cast)
        b.wait("vector", s_cast)
        b.tt(wol[:], wof[:], woh[:], ALU.subtract, sem=s_ve)
        for uu in range(2 * S // 512):
            usl = slice(512 * uu, 512 * (uu + 1))
            b.cp("vector", ctxh[:, usl], ctxf[:, usl], sem=s_ve)
            b.wait("vector", s_ve)
            b.tt(ctxl[:, usl], ctxf[:, usl], ctxh[:, usl], ALU.subtract,
                 sem=s_ve)
        b.wait("tensor", s_ve)
        for blk in range(DBG.get('d_nblk', NBLK)):
            sh, slot = shard_slot(blk)
            for oc in range(16):
                g = (blk * 16 + oc) % 4
                ps = pdx[g]
                b.wait_release("tensor", ("p512d", g))
                for dh in range(2):
                    wcs = slice(2048 * dh + 128 * oc,
                                2048 * dh + 128 * (oc + 1))
                    csl = slice(S * dh + 512 * blk, S * dh + 512 * (blk + 1))
                    b.mm(ps[:], woh[:, wcs], ctxh[:, csl], dh == 0, False)
                    b.mm(ps[:], woh[:, wcs], ctxl[:, csl], False, False)
                    b.mm(ps[:], wol[:, wcs], ctxh[:, csl], False, dh == 1)
                b.inc_last("tensor", s_te)
                b.wait_release("vector", ("ptm", g))
                b.wait("vector", s_te)
                b.cp("vector", ptm[:, 512 * g:512 * (g + 1)], ps[:])
                b.inc_last("vector", s_ve)
                b.release(("p512d", g), s_ve)
                prow = (2 * sh + slot) * D + 128 * oc
                dq, ds_ = (("sync", s_pd) if g % 2 == 0 else
                           ("scalar", s_stash))
                b.wait(dq, s_ve)
                b.dma(dq, part_d.ap()[prow:prow + 128, :],
                      ptm[:, 512 * g:512 * (g + 1)], ds_)
                b.release(("ptm", g), ds_)
        XD.close()
        b.barrier(s_bar)
        b.op("gpsimd", lambda E: E.collective_compute(
            "ReduceScatter", ALU.add,
            replica_groups=[list(range(NCOR))],
            ins=[part_d.ap()], outs=[rs_d.ap()]), sems["s_cc"])
        b.nopinc("gpsimd", sems["s_np"])
    return rs_d



def emit_p2(nc, b, sems, id32, rs_d):
    x_own = nc.dram_tensor("x_own", [TOK, D], F32, kind="ExternalInput")
    bo_r = nc.dram_tensor("bo_r", [128, 16], F32, kind="ExternalInput")
    gwT = nc.dram_tensor("gwT", [IC0 * 128, D], F32, kind="ExternalInput")
    uwT = nc.dram_tensor("uwT", [IC0 * 128, D], F32, kind="ExternalInput")
    dwT = nc.dram_tensor("dwT", [16 * 128, I0], F32, kind="ExternalInput")
    gsc = nc.dram_tensor("gsc", [128, IC0], F32, kind="ExternalInput")
    usc = nc.dram_tensor("usc", [128, IC0], F32, kind="ExternalInput")
    dsc = nc.dram_tensor("dsc", [128, 16], F32, kind="ExternalInput")
    vwT = nc.dram_tensor("vwT", [4 * 128, D], F32, kind="ExternalInput")
    woT1 = nc.dram_tensor("woT1", [16 * 128, D], F32, kind="ExternalInput")
    gwT1 = nc.dram_tensor("gwT1", [128, 128], F32, kind="ExternalInput")
    w1 = nc.dram_tensor("w1", [E * HC * 128, D], F32, kind="ExternalInput")
    w2 = nc.dram_tensor("w2", [E * 16 * 128, HC * 128], F32,
                        kind="ExternalInput")
    w3 = nc.dram_tensor("w3", [E * HC * 128, D], F32, kind="ExternalInput")
    sw1 = nc.dram_tensor("sw1", [HC * 128, D], F32, kind="ExternalInput")
    sw2 = nc.dram_tensor("sw2", [16 * 128, HC * 128], F32,
                         kind="ExternalInput")
    sw3 = nc.dram_tensor("sw3", [HC * 128, D], F32, kind="ExternalInput")
    ones32 = nc.dram_tensor("ones32", [1, 128], F32, kind="ExternalInput")

    y = nc.dram_tensor("y", [TOK, D], F32, kind="ExternalOutput")

    with ExitStack() as X:
        sb = X.enter_context
        id32_s = sb(nc.sbuf_tensor("id32sB", [128, 128], F32))
        ones_s = sb(nc.sbuf_tensor("onessB", [1, 128], F32))
        bo_s = sb(nc.sbuf_tensor("bos", [128, 16], F32))
        gsc_s = sb(nc.sbuf_tensor("gscs", [128, IC0], F32))
        usc_s = sb(nc.sbuf_tensor("uscs", [128, IC0], F32))
        dsc_s = sb(nc.sbuf_tensor("dscs", [128, 16], F32))

        resid = sb(nc.sbuf_tensor("resid", [128, 16 * TOK], F32))
        xnh = sb(nc.sbuf_tensor("xnhB", [128, 16 * TOK], BF16))
        xnl = sb(nc.sbuf_tensor("xnlB", [128, 16 * TOK], BF16))
        st = {k: sb(nc.sbuf_tensor("stB_" + k, [128, 1], F32))
              for k in ("ssum", "ssq", "m", "mneg", "msq", "var", "r2",
                        "rstd", "nb")}

        pA = sb(nc.psum_tensor("pA", [128, 512], F32))
        pB = sb(nc.psum_tensor("pB", [128, 512], F32))
        pC = sb(nc.psum_tensor("pC", [128, 512], F32))
        pD = sb(nc.psum_tensor("pD", [128, 512], F32))
        pE = sb(nc.psum_tensor("pE", [128, 512], F32))
        pF = sb(nc.psum_tensor("pF", [128, 512], F32))
        pT = [sb(nc.psum_tensor("pU0", [128, 128], F32)),
              sb(nc.psum_tensor("pU1", [128, 128], F32))]
        pG = pT[1][:, 0:8]
        pT6 = [pT[0][:], pT[1][:], pA[:, 0:128], pB[:, 0:128],
               pC[:, 0:128], pD[:, 0:128]]
        assert nc.psum_banks_remaining >= 0
        assert nc.sbuf_bytes_remaining >= 0

        s_ld, s_x0, s_x1 = sems["s_ld"], sems["s_x0"], sems["s_x1"]
        s_cast, s_te, s_ve, s_sc = (sems["s_cast"], sems["s_te"],
                                    sems["s_ve"], sems["s_sc"])
        s_w, s_out, s_bar = sems["s_w"], sems["s_out"], sems["s_bar"]
        s_gp = sems["s_gp"]
        s_x = [s_x0, s_x1]

        b.dma("sync", id32_s[:], id32.ap(), s_ld)
        b.dma("sync", ones_s[:], ones32.ap(), s_ld)
        b.dma("sync", bo_s[:], bo_r.ap(), s_ld)
        b.dma("sync", gsc_s[:], gsc.ap(), s_ld)
        b.dma("sync", usc_s[:], usc.ap(), s_ld)
        b.dma("sync", dsc_s[:], dsc.ap(), s_ld)
        for e_ in ENGINES:
            b.wait(e_, s_ld)
        b.barrier(s_bar)

        X1 = ExitStack()
        s1 = X1.enter_context
        xb = s1(nc.sbuf_tensor("xbB", [128, 2 * D], F32))
        xntm = s1(nc.sbuf_tensor("xntmB", [128, D], F32))
        scr = xntm
        for t in range(8):
            p = t % 2
            xt = xb[:, D * p:D * (p + 1)]
            b.wait_release("sync", ("xb", p))
            b.dma("sync", xt, x_own.ap()[128 * t:128 * (t + 1), :], s_x[p])
            b.wait("vector", s_x[p])
            b.wait("scalar", s_x[p])
            emit_ln_tile(b, xt, scr, xntm, st, s_ve, s_sc)
            for j in range(16):
                col = TOK * j + 128 * t
                r0, r1 = (2 * j) % 6, (2 * j + 1) % 6
                tp0 = pT6[r0]
                b.wait_release("tensor", ("pT", r0))
                b.wait("tensor", s_x[p])
                b.tr(tp0, xt[:, 128 * j:128 * (j + 1)], id32_s[:], s_te)
                b.wait("vector", s_te)
                b.cp("vector", resid[:, col:col + 128], tp0)
                b.inc_last("vector", s_ve)
                b.release(("pT", r0), s_ve)
                tp1 = pT6[r1]
                b.wait_release("tensor", ("pT", r1))
                b.wait_release("tensor", ("xnt",))
                b.tr(tp1, xntm[:, 128 * j:128 * (j + 1)], id32_s[:], s_te)
                b.wait("scalar", s_te)
                b.cp("scalar", xnh[:, col:col + 128], tp1, s_sc)
                b.wait("vector", s_sc)
                b.tt(xnl[:, col:col + 128], tp1, xnh[:, col:col + 128],
                     ALU.subtract, sem=s_ve)
                b.release(("pT", r1), s_ve)
            b.release(("xb", p), s_te)
            b.release(("xntm",), s_te)
            b.wait_release("scalar", ("xntm",))
        b.barrier(s_bar)
        X1.close()

        X2 = ExitStack()
        s2_ = X2.enter_context
        huh = s2_(nc.sbuf_tensor("huh", [128, IC0 * 256], BF16))
        hul = s2_(nc.sbuf_tensor("hul", [128, IC0 * 256], BF16))
        gact = s2_(nc.sbuf_tensor("gact", [128, 256], F32))
        uact = s2_(nc.sbuf_tensor("uact", [128, 256], F32))
        humul = s2_(nc.sbuf_tensor("humul", [128, 256], F32))
        wbh = s2_(nc.sbuf_tensor("wbh", [128, 2 * 2048], BF16))
        wbh2 = s2_(nc.sbuf_tensor("wbh2", [128, 2 * 2048], BF16))
        dwh = s2_(nc.sbuf_tensor("dwh", [128, IC0 * 128], BF16))
        for qt in range(4):
            for ic in range(IC0):
                g = ic % 2
                wgh = wbh[:, 2048 * g:2048 * (g + 1)]
                wuh = wbh2[:, 2048 * g:2048 * (g + 1)]
                b.wait_release("gpsimd", ("fw", g))
                b.dma("gpsimd", wgh,
                      gwT.ap()[128 * ic:128 * (ic + 1), :], s_cast)
                b.dma("gpsimd", wuh,
                      uwT.ap()[128 * ic:128 * (ic + 1), :], s_cast)
                b.wait("tensor", s_cast)
                b.wait_release("tensor", ("pAB",))
                b.wait_release("tensor", ("pAB_sc",))
                for mj in range(16):
                    xsl = slice(TOK * mj + 256 * qt,
                                TOK * mj + 256 * (qt + 1))
                    wc = slice(128 * mj, 128 * (mj + 1))
                    b.mm(pA[:, 0:256], wgh[:, wc], xnh[:, xsl], mj == 0,
                         False)
                    b.mm(pA[:, 0:256], wgh[:, wc], xnl[:, xsl], False,
                         mj == 15)
                for mj in range(16):
                    xsl = slice(TOK * mj + 256 * qt,
                                TOK * mj + 256 * (qt + 1))
                    wc = slice(128 * mj, 128 * (mj + 1))
                    b.mm(pB[:, 0:256], wuh[:, wc], xnh[:, xsl], mj == 0,
                         False)
                    b.mm(pB[:, 0:256], wuh[:, wc], xnl[:, xsl], False,
                         mj == 15)
                b.inc_last("tensor", s_te)
                b.release(("fw", g), s_te)
                b.wait("scalar", s_te)
                b.wait_release("scalar", ("gact",))
                b.act(gact[:], pA[:, 0:256], AF.Silu,
                      scale=gsc_s[:, ic:ic + 1], sem=s_sc)
                b.release(("pAB_sc",), s_sc)
                b.wait("vector", s_te)
                b.ts(uact[:], pB[:, 0:256], usc_s[:, ic:ic + 1], ALU.mult,
                     sem=s_ve)
                b.release(("pAB",), s_ve)
                b.wait("vector", s_ve)
                b.wait("vector", s_sc)
                b.tt(humul[:], gact[:], uact[:], ALU.mult, sem=s_ve)
                b.release(("gact",), s_ve)
                hucol = 256 * ic
                b.wait("vector", s_ve)
                b.cp("vector", huh[:, hucol:hucol + 256], humul[:], sem=s_ve)
                b.wait("vector", s_ve)
                b.tt(hul[:, hucol:hucol + 256], humul[:],
                     huh[:, hucol:hucol + 256], ALU.subtract, sem=s_ve)
            for oc in range(16):
                b.wait_release("gpsimd", ("dw",))
                b.dma("gpsimd", dwh[:],
                      dwT.ap()[128 * oc:128 * (oc + 1), :], s_cast)
                b.wait("tensor", s_cast)
                b.wait("tensor", s_ve)
                b.wait_release("tensor", ("pC",))
                for ij in range(IC0):
                    wc = slice(128 * ij, 128 * (ij + 1))
                    b.mm(pC[:, 0:256], dwh[:, wc],
                         huh[:, 256 * ij:256 * (ij + 1)], ij == 0, False)
                    b.mm(pC[:, 0:256], dwh[:, wc],
                         hul[:, 256 * ij:256 * (ij + 1)], False,
                         ij == IC0 - 1)
                b.inc_last("tensor", s_te)
                b.release(("dw",), s_te)
                b.wait("vector", s_te)
                rsl = resid[:, TOK * oc + 256 * qt:TOK * oc + 256 * (qt + 1)]
                b.stt(rsl, pC[:, 0:256], dsc_s[:, oc:oc + 1], rsl, ALU.mult,
                      ALU.add, sem=s_ve)
                b.release(("pC",), s_ve)
        b.barrier(s_bar)
        X2.close()

        X3 = ExitStack()
        abuf = X3.enter_context(nc.sbuf_tensor("abuf", [128, TOK], F32))
        b.wait("sync", sems["s_cc"], 1)
        for fc in range(16):
            b.wait_release("sync", ("abuf",))
            b.dma("sync", abuf[:, 0:512],
                  rs_d.ap()[128 * fc:128 * (fc + 1), :], s_x0)
            b.dma("sync", abuf[:, 512:1024],
                  rs_d.ap()[D + 128 * fc:D + 128 * (fc + 1), :], s_x0)
            b.wait("vector", s_x0)
            rsl = resid[:, TOK * fc:TOK * (fc + 1)]
            b.stt(rsl, abuf[:], bo_s[:, fc:fc + 1], rsl, ALU.add, ALU.add,
                  sem=s_ve)
            b.release(("abuf",), s_ve)
        b.barrier(s_bar)
        X3.close()

        X4 = ExitStack()
        s4 = X4.enter_context
        xb = s4(nc.sbuf_tensor("xb4", [128, D], F32))
        xntm = s4(nc.sbuf_tensor("xntm4", [128, D], F32))
        scr = xntm
        for t in range(8):
            xt = xb[:]
            b.wait_release("vector", ("xbD4",))
            for j in range(16):
                r = j % 6
                tp = pT6[r]
                b.wait_release("tensor", ("pT", r))
                b.tr(tp,
                     resid[:, TOK * j + 128 * t:TOK * j + 128 * (t + 1)],
                     id32_s[:], s_te)
                b.wait("vector", s_te)
                b.cp("vector", xt[:, 128 * j:128 * (j + 1)], tp)
                b.inc_last("vector", s_ve)
                b.release(("pT", r), s_ve)
            b.wait("vector", s_ve)
            b.wait("scalar", s_ve)
            emit_ln_tile(b, xt, scr, xntm, st, s_ve, s_sc)
            for j in range(16):
                col = TOK * j + 128 * t
                r = j % 6
                tp = pT6[r]
                b.wait_release("tensor", ("pT", r))
                b.wait_release("tensor", ("xnt",))
                b.tr(tp, xntm[:, 128 * j:128 * (j + 1)], id32_s[:], s_te)
                b.wait("scalar", s_te)
                b.cp("scalar", xnh[:, col:col + 128], tp, s_sc)
                b.wait("vector", s_sc)
                b.tt(xnl[:, col:col + 128], tp, xnh[:, col:col + 128],
                     ALU.subtract, sem=s_ve)
                b.release(("pT", r), s_ve)
            b.release(("xbD4",), s_sc)
            b.release(("xntm",), s_te)
            b.wait_release("scalar", ("xntm",))
        b.barrier(s_bar)
        X4.close()

        X5 = ExitStack()
        s5 = X5.enter_context
        wbf = s5(nc.sbuf_tensor("wbf5", [128, 2 * 2048], F32))
        wpr = [(s5(nc.sbuf_tensor("wbh5", [128, 2048], BF16)),
                s5(nc.sbuf_tensor("wbl5", [128, 2048], BF16))),
               (s5(nc.sbuf_tensor("wbh25", [128, 2048], BF16)),
                s5(nc.sbuf_tensor("wbl25", [128, 2048], BF16)))]
        PWH = [(pA, pB), (pE, pF)]

        def wchunk2(col_ap, ncols, pair, hb=None, lb=None):
            hbuf, lbuf = wpr[pair] if hb is None else (hb, lb)
            wsl = wbf[:, 2048 * pair:2048 * pair + ncols]
            b.wait_release("sync", ("wbf", pair))
            b.dma("sync", wsl, col_ap, s_w)
            if hb is None:
                b.wait_release("gpsimd", ("wpair", pair))
            b.dma("gpsimd", hbuf[:, 0:ncols], col_ap, s_cast)
            b.wait("gpsimd", s_w)
            b.wait("gpsimd", s_cast)
            b.tt(lbuf[:, 0:ncols], wsl, hbuf[:, 0:ncols], ALU.subtract,
                 sem=s_gp, eng="gpsimd")
            b.release(("wbf", pair), s_gp)

        def mm3(ps, whb, wlb, xh_of, xl_of, nmj):
            for mj in range(nmj):
                wc = slice(128 * mj, 128 * (mj + 1))
                xh, xl = xh_of(mj), xl_of(mj)
                b.mm(ps, whb[:, wc], xh, mj == 0, False)
                b.mm(ps, whb[:, wc], xl, False, False)
                b.mm(ps, wlb[:, wc], xh, False, mj == nmj - 1)

        def xn_of(half):
            return (lambda mj, h=half: xnh[:, TOK * mj + 512 * h:
                                           TOK * mj + 512 * (h + 1)],
                    lambda mj, h=half: xnl[:, TOK * mj + 512 * h:
                                           TOK * mj + 512 * (h + 1)])

        X5b = ExitStack()
        s5b = X5b.enter_context
        v1h = s5b(nc.sbuf_tensor("v1h", [128, 4 * TOK], BF16))
        v1l = s5b(nc.sbuf_tensor("v1l", [128, 4 * TOK], BF16))
        for vc in range(4):
            pr = vc % 2
            wchunk2(vwT.ap()[128 * vc:128 * (vc + 1), :], 2048, pr)
            whb, wlb = wpr[pr]
            b.wait("tensor", s_gp)
            for half in range(2):
                ps = PWH[pr][half]
                b.wait_release("tensor", ("pw", pr, half))
                fh, fl = xn_of(half)
                mm3(ps[:], whb, wlb, fh, fl, 16)
                b.inc_last("tensor", s_te)
                if half == 1:
                    b.release(("wpair", pr), s_te)
                col = TOK * vc + 512 * half
                b.wait("scalar", s_te)
                b.cp("scalar", v1h[:, col:col + 512], ps[:], s_sc)
                b.wait("vector", s_sc)
                b.tt(v1l[:, col:col + 512], ps[:], v1h[:, col:col + 512],
                     ALU.subtract, sem=s_ve)
                b.release(("pw", pr, half), s_ve)
        for oc in range(16):
            pr = oc % 2
            wchunk2(woT1.ap()[128 * oc:128 * (oc + 1), :], 2048, pr)
            whb, wlb = wpr[pr]
            b.wait("tensor", s_gp)
            for half in range(2):
                ps = PWH[pr][half]
                b.wait_release("tensor", ("pw", pr, half))
                mm3(ps[:], whb, wlb,
                    lambda mj, h=half: v1h[:, TOK * (mj // 4) + 512 * h:
                                           TOK * (mj // 4) + 512 * (h + 1)],
                    lambda mj, h=half: v1l[:, TOK * (mj // 4) + 512 * h:
                                           TOK * (mj // 4) + 512 * (h + 1)],
                    16)
                b.inc_last("tensor", s_te)
                if half == 1:
                    b.release(("wpair", pr), s_te)
                b.wait("vector", s_te)
                rsl = resid[:, TOK * oc + 512 * half:
                            TOK * oc + 512 * (half + 1)]
                b.tt(rsl, rsl, ps[:], ALU.add, sem=s_ve)
                b.release(("pw", pr, half), s_ve)
        b.barrier(s_bar)
        X5b.close()

        X6 = ExitStack()
        s6 = X6.enter_context
        comb = s6(nc.sbuf_tensor("comb", [8, TOK], F32))
        g1h = s6(nc.sbuf_tensor("g1h", [128, 16 * 8], BF16))
        g1l = s6(nc.sbuf_tensor("g1l", [128, 16 * 8], BF16))
        sg8 = s6(nc.sbuf_tensor("sg8", [128, 8], F32))
        t8a = s6(nc.sbuf_tensor("t8a", [128, 8], F32))
        t8b = s6(nc.sbuf_tensor("t8b", [128, 8], F32))
        m81 = s6(nc.sbuf_tensor("m81", [128, 1], F32))
        m82 = s6(nc.sbuf_tensor("m82", [128, 1], F32))
        wchunk2(gwT1.ap(), 128, 0, hb=g1h, lb=g1l)
        b.wait("tensor", s_gp)
        for t in range(8):
            b.wait("tensor", s_ve)
            b.wait_release("tensor", ("pg8",))
            for mj in range(16):
                xh = xnh[:, TOK * mj + 128 * t:TOK * mj + 128 * (t + 1)]
                xl = xnl[:, TOK * mj + 128 * t:TOK * mj + 128 * (t + 1)]
                gc = slice(8 * mj, 8 * (mj + 1))
                b.mm(pG, xh, g1h[:, gc], mj == 0, False)
                b.mm(pG, xl, g1h[:, gc], False, False)
                b.mm(pG, xh, g1l[:, gc], False, mj == 15)
            b.inc_last("tensor", s_te)
            b.wait("scalar", s_te)
            b.act(sg8[:], pG, AF.Sigmoid, sem=s_sc)
            b.wait("vector", s_sc)
            def _vs(fn):
                b.op("vector", fn, s_ve)
                b.wait("vector", s_ve)
            b.red(m81[:], sg8[:], ALU.max, sem=s_ve)
            b.wait("vector", s_ve)
            _vs(lambda E, o=t8a[:], i=sg8[:], s=m81[:]:
                E.tensor_scalar(out=o, in0=i, scalar1=s, scalar2=None,
                                op0=ALU.subtract))
            _vs(lambda E, o=t8a[:], i=t8a[:]:
                E.tensor_scalar(out=o, in0=i, scalar1=0.0, scalar2=None,
                                op0=ALU.is_ge))
            _vs(lambda E, o=t8b[:], i=t8a[:], j=sg8[:]:
                E.scalar_tensor_tensor(out=o, in0=i, scalar=-1e30, in1=j,
                                       op0=ALU.mult, op1=ALU.add))
            b.red(m82[:], t8b[:], ALU.max, sem=s_ve)
            b.wait("vector", s_ve)
            _vs(lambda E, o=t8a[:], i=sg8[:], s=m82[:]:
                E.tensor_scalar(out=o, in0=i, scalar1=s, scalar2=None,
                                op0=ALU.subtract))
            _vs(lambda E, o=t8a[:], i=t8a[:]:
                E.tensor_scalar(out=o, in0=i, scalar1=0.0, scalar2=None,
                                op0=ALU.is_ge))
            _vs(lambda E, o=t8a[:], i=t8a[:], j=sg8[:]:
                E.tensor_tensor(o, i, j, ALU.mult))
            _vs(lambda E, o=m82[:], i=m82[:], j=m81[:]:
                E.tensor_tensor(o, i, j, ALU.add))
            _vs(lambda E, o=m82[:], i=m82[:]:
                E.tensor_scalar(out=o, in0=i, scalar1=1e-20, scalar2=None,
                                op0=ALU.add))
            _vs(lambda E, o=m81[:], i=m82[:]: E.reciprocal(o, i))
            b.ts(t8b[:], t8a[:], m81[:], ALU.mult, s2=2.5, op1=ALU.mult,
                 sem=s_ve)
            b.release(("pg8",), s_ve)
            tp = pT[0]
            b.wait_release("tensor", ("pT", 0))
            b.wait("tensor", s_ve)
            b.tr(tp[0:8, :], t8b[:], id32_s[:], s_te)
            b.wait("vector", s_te)
            b.cp("vector", comb[0:8, 128 * t:128 * (t + 1)], tp[0:8, 0:128])
            b.inc_last("vector", s_ve)
            b.release(("pT", 0), s_ve)
        b.barrier(s_bar)

        X7 = ExitStack()
        s7 = X7.enter_context
        GRPS = [list(range(0, 6)), list(range(6, HC))]
        GMAX = max(len(g) for g in GRPS)
        huh = s7(nc.sbuf_tensor("huh7", [128, GMAX * TOK], BF16))
        hul = s7(nc.sbuf_tensor("hul7", [128, GMAX * TOK], BF16))
        gact = s7(nc.sbuf_tensor("gact7", [128, 512], F32))
        uact = s7(nc.sbuf_tensor("uact7", [128, 512], F32))
        humul = s7(nc.sbuf_tensor("humul7", [128, 512], F32))
        cbc = s7(nc.sbuf_tensor("cbc", [128, TOK], F32))
        crow = s7(nc.sbuf_tensor("crow", [1, TOK], F32))
        evb = s7(nc.sbuf_tensor("evb", [128, 512], F32))

        def silu_mul_split2(pa, pb, hucol, kpa, kpb):
            b.wait("scalar", s_te)
            b.wait_release("scalar", ("gact",))
            b.act(gact[:], pa[:], AF.Silu, sem=s_sc)
            b.release(kpa, s_sc)
            b.wait("vector", s_te)
            b.wait_release("vector", ("uact",))
            b.cp("vector", uact[:], pb[:], sem=s_ve)
            b.release(kpb, s_ve)
            b.wait("vector", s_ve)
            b.wait("vector", s_sc)
            b.wait_release("vector", ("humul",))
            b.tt(humul[:], gact[:], uact[:], ALU.mult, sem=s_ve)
            b.release(("gact",), s_ve)
            b.release(("uact",), s_ve)
            b.wait("scalar", s_ve)
            b.wait_release("scalar", ("hu",))
            b.cp("scalar", huh[:, hucol:hucol + 512], humul[:], s_sc)
            b.release(("humul",), s_sc)
            b.wait("vector", s_sc)
            b.wait_release("vector", ("hu",))
            b.tt(hul[:, hucol:hucol + 512], humul[:],
                 huh[:, hucol:hucol + 512], ALU.subtract, sem=s_ve)

        def expert_hu(w1ap, w3ap, grp):
            for i_hc, hc in enumerate(grp):
                par = i_hc % 2
                wchunk2(w1ap(hc), 2048, 0)
                whb, wlb = wpr[0]
                b.wait("tensor", s_gp)
                for half in range(2):
                    b.wait_release("tensor", ("pw", par, half))
                    fh, fl = xn_of(half)
                    mm3(PWH[par][half][:], whb, wlb, fh, fl, 16)
                    b.inc_last("tensor", s_te)
                b.release(("wpair", 0), s_te)
                wchunk2(w3ap(hc), 2048, 1)
                whb2, wlb2 = wpr[1]
                b.wait("tensor", s_gp)
                for half in range(2):
                    b.wait_release("tensor", ("pw3", half))
                    fh, fl = xn_of(half)
                    mm3((pC, pD)[half][:], whb2, wlb2, fh, fl, 16)
                    b.inc_last("tensor", s_te)
                b.release(("wpair", 1), s_te)
                for half in range(2):
                    silu_mul_split2(PWH[par][half], (pC, pD)[half],
                                    TOK * i_hc + 512 * half,
                                    ("pw", par, half), ("pw3", half))

        def expert_down(w2colap, grp, use_cbc):
            nij = len(grp)
            for oc in range(16):
                par = oc % 2
                wchunk2(w2colap(oc), nij * 128, par)
                whb, wlb = wpr[par]
                b.wait("tensor", s_gp)
                b.wait("tensor", s_ve)
                b.wait("tensor", s_sc)
                for half in range(2):
                    ps = PWH[par][half]
                    b.wait_release("tensor", ("pw", par, half))
                    mm3(ps[:], whb, wlb,
                        lambda ij, h=half: huh[:, TOK * ij + 512 * h:
                                               TOK * ij + 512 * (h + 1)],
                        lambda ij, h=half: hul[:, TOK * ij + 512 * h:
                                               TOK * ij + 512 * (h + 1)],
                        nij)
                    b.inc_last("tensor", s_te)
                    if half == 1:
                        b.release(("wpair", par), s_te)
                    b.wait("vector", s_te)
                    rsl = resid[:, TOK * oc + 512 * half:
                                TOK * oc + 512 * (half + 1)]
                    if use_cbc:
                        b.wait_release("vector", ("evb",))
                        b.tt(evb[:], ps[:],
                             cbc[:, 512 * half:512 * (half + 1)],
                             ALU.mult, sem=s_ve)
                        b.release(("pw", par, half), s_ve)
                        b.wait("vector", s_ve)
                        b.tt(rsl, rsl, evb[:], ALU.add, sem=s_ve)
                        b.release(("evb",), s_ve)
                    else:
                        b.tt(rsl, rsl, ps[:], ALU.add, sem=s_ve)
                        b.release(("pw", par, half), s_ve)
            b.release(("hu",), s_te)

        for e in range(E):
            b.wait_release("sync", ("crow",))
            b.dma("sync", crow[:], comb[e:e + 1, :], s_w)
            b.wait("tensor", s_w)
            for half in range(2):
                ps = PWH[0][half]
                b.wait_release("tensor", ("pw", 0, half))
                b.mm(ps[:], ones_s[:],
                     crow[0:1, 512 * half:512 * (half + 1)], True, True)
                b.inc_last("tensor", s_te)
            b.release(("crow",), s_te)
            b.wait("vector", s_te)
            for half in range(2):
                ps = PWH[0][half]
                b.cp("vector", cbc[:, 512 * half:512 * (half + 1)], ps[:])
                b.inc_last("vector", s_ve)
                b.release(("pw", 0, half), s_ve)
            for grp in GRPS:
                g0, ng = grp[0], len(grp)
                expert_hu(
                    lambda hc, e=e: w1.ap()[(e * HC + hc) * 128:
                                            (e * HC + hc + 1) * 128, :],
                    lambda hc, e=e: w3.ap()[(e * HC + hc) * 128:
                                            (e * HC + hc + 1) * 128, :],
                    grp)
                expert_down(
                    lambda oc, e=e, g0=g0, n=ng:
                    w2.ap()[(e * 16 + oc) * 128:(e * 16 + oc + 1) * 128,
                            128 * g0:128 * (g0 + n)],
                    grp, True)

        for grp in GRPS:
            g0, ng = grp[0], len(grp)
            expert_hu(
                lambda hc: sw1.ap()[128 * hc:128 * (hc + 1), :],
                lambda hc: sw3.ap()[128 * hc:128 * (hc + 1), :],
                grp)
            expert_down(
                lambda oc, g0=g0, n=ng:
                sw2.ap()[128 * oc:128 * (oc + 1), 128 * g0:128 * (g0 + n)],
                grp, False)
        b.barrier(s_bar)
        X7.close()
        X6.close()
        X5.close()

        X9 = ExitStack()
        xb = X9.enter_context(nc.sbuf_tensor("xb9", [128, D], F32))
        for t in range(8):
            b.wait_release("vector", ("yst",))
            for j in range(16):
                r = j % 6
                tp = pT6[r]
                b.wait_release("tensor", ("pT", r))
                b.tr(tp,
                     resid[:, TOK * j + 128 * t:TOK * j + 128 * (t + 1)],
                     id32_s[:], s_te)
                b.wait("vector", s_te)
                b.cp("vector", xb[:, 128 * j:128 * (j + 1)], tp)
                b.inc_last("vector", s_ve)
                b.release(("pT", r), s_ve)
            b.wait("sync", s_ve)
            b.dma("sync", y.ap()[128 * t:128 * (t + 1), :], xb[:], s_out)
            b.release(("yst",), s_out)
        b.wait("sync", s_out)
        X9.close()


def build_fused():
    nc = bass.Bass("TRN2", target_bir_lowering=False,
                   detect_race_conditions=False)
    id32 = nc.dram_tensor("id32", [128, 128], F32, kind="ExternalInput")
    with ExitStack() as XS:
        sems = {k: Sem(XS.enter_context(nc.semaphore(k))) for k in
                ("s_ld", "s_x0", "s_x1", "s_cast", "s_te", "s_ve", "s_sc",
                 "s_stash", "s_qb", "s_kv", "s_pd", "s_cc", "s_out", "s_bar",
                 "s_w", "s_np", "s_gp")}
        b = Bld(nc)
        rs_d = emit_p1(nc, b, sems, id32)
        emit_p2(nc, b, sems, id32, rs_d)
        b.emit()
    return nc



def _t128(a):
    a = np.ascontiguousarray(np.asarray(a, np.float32))
    k, n = a.shape
    return np.ascontiguousarray(
        a.reshape(k // 128, 128, n).swapaxes(0, 1).reshape(128, -1))


def _tcol(a, cw=128):
    a = np.ascontiguousarray(np.asarray(a, np.float32))
    k, n = a.shape
    return np.ascontiguousarray(
        a.reshape(k // 128, 128, n // cw, cw).transpose(2, 1, 0, 3)
        .reshape((n // cw) * 128, (k // 128) * cw))


def _alibi_slopes():
    ratio = 2.0 ** (-8.0 / 16.0)
    return np.array([ratio ** (h + 1) for h in range(16)], dtype=np.float64)


def _p1_inputs(x, Wq, Wk, Wv, Wo, bq, bk, bv):
    f32 = np.float32
    slopes = _alibi_slopes()
    pcol = np.arange(128, dtype=np.float64)[:, None]
    ucol = np.arange(896, dtype=np.float64)[None, :]
    mstrip = np.where(ucol >= pcol + 384, 0.0, -1e9).astype(f32)
    id32 = np.eye(128, dtype=f32)
    idbf = np.eye(128, dtype=BF16NP)
    onesbf = np.ones((1, 128), dtype=BF16NP)
    qi = np.arange(S, dtype=np.float64)
    kt = np.arange(64, dtype=np.float64)[None, :]
    base = 128.0 * kt + pcol - 8191.0

    ins = []
    for c in range(NCOR):
        hsl = slice(256 * c, 256 * (c + 1))
        bqkv = np.zeros((128, 6), f32)
        for i3, bb in enumerate((bq, bk, bv)):
            for oc in range(2):
                bqkv[:, 2 * i3 + oc] = bb[hsl].reshape(2, 128)[oc]
        biask = np.zeros((128, 128), np.float64)
        for oc in range(2):
            biask[:, 2 * np.arange(64) + oc] = slopes[2 * c + oc] * base
        mr = np.stack([-SQ128 * slopes[2 * c + oc] * (qi - 8191.0)
                       for oc in range(2)])
        ins.append({
            "x": x,
            "wq": _t128(Wq[:, hsl]),
            "wk": _t128(Wk[:, hsl]),
            "wv": _t128(Wv[:, hsl]),
            "bqkv": bqkv,
            "wo": _t128(Wo[hsl, :]),
            "biask": biask.astype(f32),
            "mrow": mr.astype(BF16NP),
            "mstrip": mstrip,
            "id32": id32, "idbf": idbf, "onesbf": onesbf,
        })
    return ins


def _p2_shared(L0_attn_bo, L0_ffn_gate_w, L0_ffn_up_w, L0_ffn_down_w,
               L0_ffn_gate_s, L0_ffn_up_s, L0_ffn_down_s,
               L1_attn_wqkv, L1_attn_wo, L1_ffn_gate_w,
               L1_ffn_w1, L1_ffn_w2, L1_ffn_w3,
               L1_ffn_sw1, L1_ffn_sw2, L1_ffn_sw3):
    f32 = np.float32
    wqkv = np.asarray(L1_attn_wqkv, f32)
    v_rows = np.concatenate([np.arange(2176 + 256 * g, 2304 + 256 * g)
                             for g in range(4)])
    w1a = np.asarray(L1_ffn_w1, f32)
    w2a = np.asarray(L1_ffn_w2, f32)
    w3a = np.asarray(L1_ffn_w3, f32)
    return {
        "bo_r": np.ascontiguousarray(
            np.asarray(L0_attn_bo, f32).reshape(16, 128).T),
        "gwT": _tcol(np.asarray(L0_ffn_gate_w, f32).T),
        "uwT": _tcol(np.asarray(L0_ffn_up_w, f32).T),
        "dwT": _tcol(np.asarray(L0_ffn_down_w, f32).T),
        "gsc": np.ascontiguousarray(
            np.asarray(L0_ffn_gate_s, f32).reshape(IC0, 128).T),
        "usc": np.ascontiguousarray(
            np.asarray(L0_ffn_up_s, f32).reshape(IC0, 128).T),
        "dsc": np.ascontiguousarray(
            np.asarray(L0_ffn_down_s, f32).reshape(16, 128).T),
        "vwT": _tcol(wqkv[v_rows, :].T),
        "woT1": _tcol(np.asarray(L1_attn_wo, f32).T),
        "gwT1": _t128(np.asarray(L1_ffn_gate_w, f32).T),
        "w1": np.ascontiguousarray(
            w1a.reshape(E, 16, 128, HC, 128).transpose(0, 3, 2, 1, 4)
            .reshape(E * HC * 128, D)),
        "w2": np.ascontiguousarray(
            w2a.reshape(E, HC, 128, 16, 128).transpose(0, 3, 2, 1, 4)
            .reshape(E * 16 * 128, HC * 128)),
        "w3": np.ascontiguousarray(
            w3a.reshape(E, 16, 128, HC, 128).transpose(0, 3, 2, 1, 4)
            .reshape(E * HC * 128, D)),
        "sw1": _tcol(np.asarray(L1_ffn_sw1, f32)),
        "sw2": _tcol(np.asarray(L1_ffn_sw2, f32)),
        "sw3": _tcol(np.asarray(L1_ffn_sw3, f32)),
        "ones32": np.ones((1, 128), dtype=f32),
    }


def kernel(L0_attn_x, L0_attn_Wq, L0_attn_bq, L0_attn_Wk, L0_attn_bk,
           L0_attn_Wv, L0_attn_bv, L0_attn_Wo, L0_attn_bo,
           L0_ffn_gate_w, L0_ffn_up_w, L0_ffn_down_w,
           L0_ffn_gate_s, L0_ffn_up_s, L0_ffn_down_s,
           L1_attn_wqkv, L1_attn_wo,
           L1_ffn_gate_w, L1_ffn_w1, L1_ffn_w2, L1_ffn_w3,
           L1_ffn_sw1, L1_ffn_sw2, L1_ffn_sw3):
    f32 = np.float32
    x = np.ascontiguousarray(np.asarray(L0_attn_x, dtype=f32).reshape(S, D))

    in1 = _p1_inputs(x, np.asarray(L0_attn_Wq, f32), np.asarray(L0_attn_Wk, f32),
                     np.asarray(L0_attn_Wv, f32), np.asarray(L0_attn_Wo, f32),
                     np.asarray(L0_attn_bq, f32), np.asarray(L0_attn_bk, f32),
                     np.asarray(L0_attn_bv, f32))
    shared = _p2_shared(L0_attn_bo, L0_ffn_gate_w, L0_ffn_up_w,
                        L0_ffn_down_w, L0_ffn_gate_s, L0_ffn_up_s,
                        L0_ffn_down_s, L1_attn_wqkv, L1_attn_wo,
                        L1_ffn_gate_w, L1_ffn_w1, L1_ffn_w2, L1_ffn_w3,
                        L1_ffn_sw1, L1_ffn_sw2, L1_ffn_sw3)
    ins = []
    for c in range(NCOR):
        d = dict(in1[c])
        d.update(shared)
        d["x_own"] = np.ascontiguousarray(x[tok_index(c)])
        ins.append(d)
    nc = build_fused()
    res = run_bass_kernel_spmd(nc, ins, core_ids=list(range(NCOR)))

    out = np.zeros((S, D), dtype=f32)
    for c in range(NCOR):
        out[tok_index(c)] = res.results[c]["y"]
    return out.reshape(1, S, D)


# version 09
# speedup vs baseline: 1.1505x; 1.1505x over previous
import numpy as np
import ml_dtypes
from contextlib import ExitStack

import concourse.bass as bass
import concourse.mybir as mybir
from concourse.bass_utils import run_bass_kernel_spmd

F32 = mybir.dt.float32
BF16 = mybir.dt.bfloat16
AF = mybir.ActivationFunctionType
ALU = mybir.AluOpType
AX = mybir.AxisListType.X

NCOR = 8
S = 8192
D = 2048
NBLK = 16
TOK = 1024
I0 = 5632
IC0 = I0 // 128
E = 8
HE = 1408
HC = HE // 128
SCALE = float(1.0 / np.sqrt(np.float32(128.0)))
EPS = 1e-5
SQ128 = float(np.sqrt(np.float64(128.0)))

ENGINES = ("sync", "tensor", "vector", "scalar", "gpsimd")
DBG = {"qkv_n": 6, "qkv_evac": "ts", "qkv_stash": True}
BF16NP = ml_dtypes.bfloat16


def blocks_of(core):
    return (core, 15 - core)


def tok_index(core):
    b0, b1 = blocks_of(core)
    return np.concatenate([np.arange(512 * b0, 512 * b0 + 512),
                           np.arange(512 * b1, 512 * b1 + 512)])


def shard_slot(b):
    return (b, 0) if b < 8 else (15 - b, 1)


class Sem:
    def __init__(self, h):
        self.h = h
        self.n = 0


class Bld:
    def __init__(self, nc):
        self.nc = nc
        self.q = {e: [] for e in ENGINES}
        self.rel = {}
        self.dma_sems = set()

    def emit(self):
        with self.nc.Block() as block:
            for e in ENGINES:
                ops = self.q[e]
                getattr(block, e)(lambda eng, ops=ops: [op(eng) for op in ops])

    def wait(self, eng, sem, val=None):
        v = sem.n if val is None else val
        if v <= 0:
            return
        g = lambda E, s=sem.h, v=v: E.wait_ge(s, v)
        g.is_op = False
        g.sem = None
        self.q[eng].append(g)

    def release(self, key, sem):
        self.rel[key] = (sem, sem.n)

    def wait_release(self, eng, key):
        if key in self.rel:
            sem, v = self.rel[key]
            self.wait(eng, sem, v)

    def op(self, eng, fn, sem=None, inc=1):
        def g(E, fn=fn, sem=sem, inc=inc):
            i = fn(E)
            if sem is not None:
                i.then_inc(sem.h, inc)
            return i
        g.is_op = True
        g.sem = sem
        self.q[eng].append(g)
        if sem is not None:
            sem.n += inc

    def inc_last(self, eng, sem, inc=1):
        prev = self.q[eng].pop()
        assert getattr(prev, "is_op", False), "inc_last needs a real op"
        assert getattr(prev, "sem", None) is None, \
            "inc_last on an op that already has a sem update"
        def g(E, prev=prev, sem=sem, inc=inc):
            i = prev(E)
            i.then_inc(sem.h, inc)
            return i
        g.is_op = True
        g.sem = sem
        self.q[eng].append(g)
        sem.n += inc

    def dma(self, eng, out, in_, sem, inc=16):
        self.dma_sems.add(sem)
        self.wait(eng, sem)
        self.op(eng, lambda E, o=out, i=in_: E.dma_start(out=o, in_=i),
                sem, inc)

    def mm(self, out, lh, rh, start, stop, sem=None, inc=1):
        self.op("tensor",
                lambda E, o=out, l=lh, r=rh, a=start, b=stop:
                E.matmul(o, l, r, start=a, stop=b), sem, inc)

    def tr(self, out, in_, ident, sem=None):
        self.op("tensor",
                lambda E, o=out, i=in_, d=ident: E.transpose(o, i, d), sem)

    def cp(self, eng, out, in_, sem=None):
        if eng == "scalar":
            self.op(eng, lambda E, o=out, i=in_: E.copy(out=o, in_=i), sem)
        else:
            self.op(eng, lambda E, o=out, i=in_: E.tensor_copy(out=o, in_=i),
                    sem)

    def tt(self, out, a, c, alu, sem=None, eng="vector"):
        self.op(eng, lambda E, o=out, x=a, y=c, u=alu:
                E.tensor_tensor(o, x, y, u), sem)

    def ts(self, out, in0, s1, op0, s2=None, op1=None, sem=None, eng="vector"):
        if op1 is None:
            self.op(eng, lambda E, o=out, i=in0, a=s1, u=op0:
                    E.tensor_scalar(out=o, in0=i, scalar1=a, scalar2=None,
                                    op0=u), sem)
        else:
            self.op(eng, lambda E, o=out, i=in0, a=s1, c=s2, u=op0, v=op1:
                    E.tensor_scalar(out=o, in0=i, scalar1=a, scalar2=c,
                                    op0=u, op1=v), sem)

    def stt(self, out, in0, sc, in1, op0, op1, sem=None, eng="vector"):
        self.op(eng, lambda E, o=out, i=in0, s=sc, j=in1, u=op0, v=op1:
                E.scalar_tensor_tensor(out=o, in0=i, scalar=s, in1=j,
                                       op0=u, op1=v), sem)

    def act(self, out, in_, func, bias=0.0, scale=1.0, accum=None, sem=None):
        self.op("scalar", lambda E, o=out, i=in_, f=func, c=bias, s=scale,
                a=accum:
                E.activation(out=o, in_=i, func=f, bias=c, scale=s,
                             accum_out=a), sem)

    def red(self, out, in_, alu, sem=None):
        self.op("vector", lambda E, o=out, i=in_, u=alu:
                E.tensor_reduce(o, i, AX, u), sem)

    def recip(self, out, in_, sem=None):
        self.op("vector", lambda E, o=out, i=in_: E.reciprocal(o, i), sem)

    def nopinc(self, eng, sem, inc=1):
        self.op(eng, lambda E: E.nop(), sem, inc)

    def memset(self, ap, v, sem=None):
        self.op("gpsimd", lambda E, o=ap, w=v: E.memset(o, w), sem)

    def barrier(self, sem):
        for e in ENGINES:
            for ds in self.dma_sems:
                self.wait(e, ds)
        for e in ENGINES:
            idx = None
            for j in range(len(self.q[e]) - 1, -1, -1):
                if getattr(self.q[e][j], "is_op", False):
                    idx = j
                    break
            if idx is None:
                self.nopinc(e, sem)
                continue
            last_op = self.q[e][idx]
            if last_op.sem is not None:
                self.wait(e, last_op.sem)
                self.nopinc(e, sem)
            else:
                def g(E, prev=last_op, sem=sem):
                    i = prev(E)
                    i.then_inc(sem.h, 1)
                    return i
                g.is_op = True
                g.sem = sem
                self.q[e][idx] = g
                sem.n += 1
        tgt = sem.n
        for e in ENGINES:
            self.wait(e, sem, tgt)



def csplit(ap2d, p=128):
    return ap2d.rearrange("(mj p) o -> p mj o", p=p)


def dsplit(sbuf_ap, owid):
    return sbuf_ap.rearrange("p (mj o) -> p mj o", o=owid)

def emit_ln_tile(b, xb, scr, xntm, st, s_ve, s_sc):
    def v(out, fn):
        b.op("vector", fn, s_ve)
        b.wait("vector", s_ve)
    b.red(st["ssum"][:], xb[:], ALU.add, sem=s_ve)
    b.act(scr[:], xb[:], AF.Square, accum=st["ssq"][:], sem=s_sc)
    b.wait("vector", s_ve)
    v(None, lambda E, o=st["m"][:], i=st["ssum"][:]:
      E.tensor_scalar(out=o, in0=i, scalar1=1.0 / D, scalar2=None,
                      op0=ALU.mult))
    v(None, lambda E, o=st["mneg"][:], i=st["m"][:]:
      E.tensor_scalar(out=o, in0=i, scalar1=-1.0, scalar2=None, op0=ALU.mult))
    v(None, lambda E, o=st["msq"][:], i=st["m"][:]:
      E.tensor_tensor(o, i, i, ALU.mult))
    b.wait("vector", s_sc)
    v(None, lambda E, o=st["var"][:], i=st["ssq"][:]:
      E.tensor_scalar(out=o, in0=i, scalar1=1.0 / D, scalar2=None,
                      op0=ALU.mult))
    v(None, lambda E, o=st["var"][:], i=st["var"][:], j=st["msq"][:]:
      E.tensor_tensor(o, i, j, ALU.subtract))
    v(None, lambda E, o=st["var"][:], i=st["var"][:]:
      E.tensor_scalar(out=o, in0=i, scalar1=EPS, scalar2=None, op0=ALU.add))
    b.wait("scalar", s_ve)
    b.act(st["r2"][:], st["var"][:], AF.Ln, sem=s_sc)
    b.wait("scalar", s_sc)
    b.act(st["rstd"][:], st["r2"][:], AF.Exp, scale=-0.5, sem=s_sc)
    b.wait("scalar", s_sc)
    b.wait("vector", s_sc)
    v(None, lambda E, o=st["nb"][:], i=st["mneg"][:], j=st["rstd"][:]:
      E.tensor_tensor(o, i, j, ALU.mult))
    b.wait("scalar", s_ve)
    b.act(xntm[:], xb[:], AF.Identity, bias=st["nb"][:], scale=st["rstd"][:],
          sem=s_sc)
    b.release(("xnt",), s_sc)



def emit_p1(nc, b, sems, id32, stop_after=None):
    x = nc.dram_tensor("x", [S, D], F32, kind="ExternalInput")
    wq = nc.dram_tensor("wq", [128, 4096], F32, kind="ExternalInput")
    wk = nc.dram_tensor("wk", [128, 4096], F32, kind="ExternalInput")
    wv = nc.dram_tensor("wv", [128, 4096], F32, kind="ExternalInput")
    bqkv = nc.dram_tensor("bqkv", [128, 6], F32, kind="ExternalInput")
    wo = nc.dram_tensor("wo", [128, 2 * D], F32, kind="ExternalInput")
    biask = nc.dram_tensor("biask", [128, 128], F32, kind="ExternalInput")
    mrow = nc.dram_tensor("mrow", [2, S], BF16, kind="ExternalInput")
    mstrip = nc.dram_tensor("mstrip", [128, 896], F32, kind="ExternalInput")
    idbf = nc.dram_tensor("idbf", [128, 128], BF16, kind="ExternalInput")
    onesbf = nc.dram_tensor("onesbf", [1, 128], BF16, kind="ExternalInput")

    part_dump = (nc.dram_tensor("part_dump", [8 * D, TOK], F32,
                                kind="ExternalOutput")
                 if DBG.get("dump_all") else None)
    if DBG.get("dump_ln"):
        ln_dump = nc.dram_tensor("ln_dump", [128, 16], F32, kind="ExternalOutput")
        xn_dump = nc.dram_tensor("xn_dump", [128, D], F32, kind="ExternalOutput")
        xnh_dump = nc.dram_tensor("xnh_dump", [128, 16 * 512], BF16, kind="ExternalOutput")
        xnl_dump = nc.dram_tensor("xnl_dump", [128, 16 * 512], BF16, kind="ExternalOutput")
    if DBG.get("dump_mid"):
        qd_dump = nc.dram_tensor("qd_dump", [512, S], BF16, kind="ExternalOutput")
        kd_dump = nc.dram_tensor("kd_dump", [512, S], BF16, kind="ExternalOutput")
        vd_dump = nc.dram_tensor("vd_dump", [512, 64 * 132], BF16, kind="ExternalOutput")

    qd = nc.dram_tensor("qd", [512, S], BF16)
    kd = nc.dram_tensor("kd", [512, S], BF16)
    vd = nc.dram_tensor("vd", [512, 64 * 132], BF16)
    part_d = nc.dram_tensor("part_d", [16 * D, 512], F32)
    rs_d = nc.dram_tensor("rs_d", [2 * D, 512], F32)

    with ExitStack() as X:
        sb = X.enter_context
        id32_s = sb(nc.sbuf_tensor("id32s", [128, 128], F32))
        idbf_s = sb(nc.sbuf_tensor("idbfs", [128, 128], BF16))
        ones_s = sb(nc.sbuf_tensor("oness", [1, 128], BF16))
        bias_s = sb(nc.sbuf_tensor("biass", [128, 128], F32))
        mstr_s = sb(nc.sbuf_tensor("mstrs", [128, 896], F32))
        bqkv_s = sb(nc.sbuf_tensor("bqkvs", [128, 6], F32))
        st = {k: sb(nc.sbuf_tensor("st_" + k, [128, 1], F32))
              for k in ("ssum", "ssq", "m", "mneg", "msq", "var", "r2",
                        "rstd", "nb")}

        p512 = [sb(nc.psum_tensor("p512a", [128, 512], F32)),
                sb(nc.psum_tensor("p512b", [128, 512], F32))]
        pT = [sb(nc.psum_tensor("pT0", [128, 128], F32)),
              sb(nc.psum_tensor("pT1", [128, 128], F32))]
        XPC = ExitStack()
        pc = [XPC.enter_context(nc.psum_tensor(f"pc{i}", [128, 132], F32))
              for i in range(4)]
        pT6 = [pT[0][:], pT[1][:]] + [pc[i][:, 0:128] for i in range(4)]
        assert nc.psum_banks_remaining >= 0
        assert nc.sbuf_bytes_remaining >= 0

        s_ld, s_x0, s_x1 = sems["s_ld"], sems["s_x0"], sems["s_x1"]
        s_cast, s_te, s_ve, s_sc = (sems["s_cast"], sems["s_te"],
                                    sems["s_ve"], sems["s_sc"])
        s_stash, s_qb, s_kv = sems["s_stash"], sems["s_qb"], sems["s_kv"]
        s_pd, s_cc, s_out, s_bar = (sems["s_pd"], sems["s_cc"],
                                    sems["s_out"], sems["s_bar"])
        s_x = [s_x0, s_x1]

        XA = ExitStack()
        sa = XA.enter_context
        wf32 = sa(nc.sbuf_tensor("wf32", [128, 4096], F32))
        wh = sa(nc.sbuf_tensor("wh", [128, 3 * 4096], BF16))
        wl = sa(nc.sbuf_tensor("wl", [128, 3 * 4096], BF16))
        xb = sa(nc.sbuf_tensor("xb", [128, 2 * D], F32))
        xntm = sa(nc.sbuf_tensor("xntm", [128, D], F32))
        xnh = sa(nc.sbuf_tensor("xnh", [128, 2 * 16 * 512], BF16))
        xnl = sa(nc.sbuf_tensor("xnl", [128, 2 * 16 * 512], BF16))
        qtmp = sa(nc.sbuf_tensor("qtmp", [128, 2 * 512], F32))
        sth = sa(nc.sbuf_tensor("sth", [128, 2 * 512], BF16))
        stl = sa(nc.sbuf_tensor("stl", [128, 2 * 512], BF16))
        vsgh = sa(nc.sbuf_tensor("vsgh", [128, 4 * 132], BF16))
        vsgl = sa(nc.sbuf_tensor("vsgl", [128, 4 * 132], BF16))
        scr = xntm

        b.dma("sync", id32_s[:], id32.ap(), s_ld)
        b.dma("sync", idbf_s[:], idbf.ap(), s_ld)
        b.dma("sync", ones_s[:], onesbf.ap(), s_ld)
        b.dma("sync", bias_s[:], biask.ap(), s_ld)
        b.dma("sync", mstr_s[:], mstrip.ap(), s_ld)
        b.dma("sync", bqkv_s[:], bqkv.ap(), s_ld)
        for i3, w3 in enumerate((wq, wk, wv)):
            sl = slice(4096 * i3, 4096 * (i3 + 1))
            b.dma("sync", wf32[:], w3.ap(), s_ld)
            b.wait("gpsimd", s_ld)
            b.dma("gpsimd", wh[:, sl], wf32[:], s_cast)
            b.wait("vector", s_cast)
            b.tt(wl[:, sl], wf32[:], wh[:, sl], ALU.subtract, sem=s_ve)
            b.wait("sync", s_ve)
        b.memset(vsgl[:], 0.0)
        b.memset(vsgh[:], 0.0)
        for i in range(4):
            b.memset(vsgh[:, 132 * i + 128:132 * i + 129], 1.0)
        b.barrier(s_bar)

        _ntile = {'prep': 0, 'A1t': 1, 'A1': 4, 'A2': 8}.get(stop_after, S // 128)
        for t in range(_ntile):
            blk, stn = t // 4, t % 4
            p = t % 2
            xt = xb[:, D * p:D * (p + 1)]
            b.wait_release("sync", ("xb", p))
            b.dma("sync", xt, x.ap()[128 * t:128 * (t + 1), :], s_x[p])
            b.wait("vector", s_x[p])
            b.wait("scalar", s_x[p])
            emit_ln_tile(b, xt, scr, xntm, st, s_ve, s_sc)
            b.release(("xb", p), s_sc)
            xno = 8192 * (blk % 2)
            if stn == 0:
                b.wait_release("scalar", ("xn", blk % 2))
                b.wait_release("vector", ("xn", blk % 2))
            if DBG.get("dump_ln") and t == DBG.get("dump_ln_tile", 0):
                b.wait("sync", s_sc)
                for ki, kk in enumerate(("ssum", "ssq", "m", "var", "r2",
                                         "rstd", "nb")):
                    b.op("sync", lambda E, o=ln_dump.ap()[:, ki:ki + 1],
                         i=st[kk][:]: E.dma_start(
                             out=o, in_=i, allow_slow_non_contiguous=True),
                         sems["s_out"], 16)
                b.dma("sync", xn_dump.ap(), xntm[:], sems["s_out"])
                b.wait("vector", sems["s_out"])
                b.wait("scalar", sems["s_out"])
                b.wait("tensor", sems["s_out"])
            for j in range(16):
                r = j % 6
                tp = pT6[r]
                b.wait_release("tensor", ("pT", r))
                b.wait_release("tensor", ("xnt",))
                b.tr(tp, xntm[:, 128 * j:128 * (j + 1)], id32_s[:], s_te)
                col = xno + 512 * j + 128 * stn
                b.wait("scalar", s_te)
                b.cp("scalar", xnh[:, col:col + 128], tp, s_sc)
                b.wait("vector", s_sc)
                b.tt(xnl[:, col:col + 128], tp, xnh[:, col:col + 128],
                     ALU.subtract, sem=s_ve)
                b.release(("pT", r), s_ve)
            b.release(("xntm",), s_te)
            b.wait_release("scalar", ("xntm",))
            if DBG.get("dump_ln") and t == 3:
                b.wait("sync", s_ve)
                b.dma("sync", xnh_dump.ap(), xnh[:], sems["s_out"])
                b.dma("sync", xnl_dump.ap(), xnl[:], sems["s_out"])
            if stn != 3:
                continue
            for m3 in range(3):
                for oc in range(2):
                    if m3 * 2 + oc >= DBG["qkv_n"]:
                        continue
                    g = (m3 * 2 + oc) % 2
                    ps = p512[g]
                    b.wait_release("tensor", ("p512", g))
                    b.wait("tensor", s_ve)
                    nmj = DBG.get("qkv_mj", 16)
                    npass = DBG.get("qkv_pass", 3)
                    for mj in range(nmj):
                        wc = slice(4096 * m3 + 256 * mj + 128 * oc,
                                   4096 * m3 + 256 * mj + 128 * (oc + 1))
                        xc = slice(xno + 512 * mj, xno + 512 * (mj + 1))
                        b.mm(ps[:], wh[:, wc], xnh[:, xc], mj == 0,
                             npass == 1 and mj == nmj - 1)
                        if npass >= 2:
                            b.mm(ps[:], wh[:, wc], xnl[:, xc], False,
                                 npass == 2 and mj == nmj - 1)
                        if npass >= 3:
                            b.mm(ps[:], wl[:, wc], xnh[:, xc], False,
                                 mj == nmj - 1)
                    b.inc_last("tensor", s_te)
                    if m3 == 2 and oc == 1:
                        b.release(("xn", blk % 2), s_te)
                    qcol = slice(512 * g, 512 * (g + 1))
                    b.wait_release("vector", ("stage", g))
                    b.wait("vector", s_te)
                    if DBG["qkv_evac"] == "ts":
                        b.ts(qtmp[:, qcol], ps[:],
                             bqkv_s[:, 2 * m3 + oc:2 * m3 + oc + 1], ALU.add,
                             sem=s_ve)
                    else:
                        b.cp("vector", qtmp[:, qcol], ps[:])
                        b.inc_last("vector", s_ve)
                    b.release(("p512", g), s_ve)
                    b.wait("scalar", s_ve)
                    b.cp("scalar", sth[:, qcol], qtmp[:, qcol], s_sc)
                    b.wait("vector", s_sc)
                    b.tt(stl[:, qcol], qtmp[:, qcol], sth[:, qcol],
                         ALU.subtract, sem=s_ve)
                    if not DBG["qkv_stash"]:
                        b.release(("stage", g), s_ve)
                    elif m3 < 2:
                        dd = qd if m3 == 0 else kd
                        tsl = slice(512 * blk, 512 * (blk + 1))
                        b.wait("sync", s_ve)
                        b.dma("sync", dd.ap()[128 * oc:128 * (oc + 1), tsl],
                              sth[:, qcol], s_stash)
                        b.dma("sync",
                              dd.ap()[256 + 128 * oc:256 + 128 * (oc + 1),
                                      tsl],
                              stl[:, qcol], s_stash)
                        b.release(("stage", g), s_stash)
                    else:
                        b.wait_release("scalar", ("vsg",))
                        b.wait_release("vector", ("vsg",))
                        for i in range(4):
                            u = i % 2
                            tp = pT[u]
                            b.wait_release("tensor", ("pT", u))
                            b.wait("tensor", s_ve)
                            b.tr(tp[:],
                                 qtmp[:, 512 * g + 128 * i:
                                      512 * g + 128 * (i + 1)],
                                 id32_s[:], s_te)
                            b.wait("scalar", s_te)
                            b.cp("scalar", vsgh[:, 132 * i:132 * i + 128],
                                 tp[:], s_sc)
                            b.wait("vector", s_sc)
                            b.tt(vsgl[:, 132 * i:132 * i + 128], tp[:],
                                 vsgh[:, 132 * i:132 * i + 128],
                                 ALU.subtract, sem=s_ve)
                            b.release(("pT", u), s_ve)
                        b.wait("sync", s_ve)
                        vsl = slice(132 * 4 * blk, 132 * 4 * (blk + 1))
                        b.dma("sync", vd.ap()[128 * oc:128 * (oc + 1), vsl],
                              vsgh[:], s_stash)
                        b.dma("sync",
                              vd.ap()[256 + 128 * oc:256 + 128 * (oc + 1),
                                      vsl],
                              vsgl[:], s_stash)
                        b.release(("stage", g), s_stash)
                        b.release(("vsg",), s_stash)

        b.barrier(s_bar)
        if DBG.get("dump_mid"):
            b.dma("gpsimd", qd_dump.ap(), qd.ap(), sems["s_out"])
            b.dma("gpsimd", kd_dump.ap(), kd.ap(), sems["s_out"])
            b.dma("gpsimd", vd_dump.ap(), vd.ap(), sems["s_out"])
            b.wait("gpsimd", sems["s_out"])
        XA.close()
        ctxh = sb(nc.sbuf_tensor("ctxh", [128, 2 * S], BF16))
        ctxl = sb(nc.sbuf_tensor("ctxl", [128, 2 * S], BF16))
        XC = ExitStack()
        sc_ = XC.enter_context
        kh = sc_(nc.sbuf_tensor("khb", [128, S], BF16))
        kl = sc_(nc.sbuf_tensor("klb", [128, S], BF16))
        vh = sc_(nc.sbuf_tensor("vhb", [128, 64 * 132], BF16))
        vl = sc_(nc.sbuf_tensor("vlb", [128, 64 * 132], BF16))
        qhb = sc_(nc.sbuf_tensor("qhb", [128, 2 * 512], BF16))
        qlb = sc_(nc.sbuf_tensor("qlb", [128, 2 * 512], BF16))
        ptb = sc_(nc.sbuf_tensor("ptb", [128, 2 * 512], F32))
        phb = sc_(nc.sbuf_tensor("phb", [128, 2 * 512], BF16))
        plb = sc_(nc.sbuf_tensor("plb", [128, 2 * 512], BF16))
        ctmp = sc_(nc.sbuf_tensor("ctmp", [128, 2 * 128], F32))
        linv = sc_(nc.sbuf_tensor("linv", [128, 2], F32))
        mrow_s = sc_(nc.sbuf_tensor("mrows", [1, S], BF16))

        for hh in range(DBG.get('c_heads', 2) if stop_after in (None, 'C', 'Dnc') else 0):
            b.wait_release("sync", ("kvbuf",))
            b.dma("sync", kh[:], kd.ap()[128 * hh:128 * (hh + 1), :], s_kv)
            b.dma("sync", kl[:], kd.ap()[256 + 128 * hh:256 + 128 * (hh + 1),
                                         :], s_kv)
            b.dma("sync", vh[:], vd.ap()[128 * hh:128 * (hh + 1), :], s_kv)
            b.dma("sync", vl[:], vd.ap()[256 + 128 * hh:256 + 128 * (hh + 1),
                                         :], s_kv)
            b.dma("sync", mrow_s[:], mrow.ap()[hh:hh + 1, :], s_kv)
            b.wait("tensor", s_kv)
            for qb in range(DBG.get('c_qb', NBLK)):
                g2 = qb % 2
                qsl = slice(512 * g2, 512 * (g2 + 1))
                b.wait_release("sync", ("qblk", g2))
                b.dma("sync", qhb[:, qsl],
                      qd.ap()[128 * hh:128 * (hh + 1),
                              512 * qb:512 * (qb + 1)], s_qb)
                b.dma("sync", qlb[:, qsl],
                      qd.ap()[256 + 128 * hh:256 + 128 * (hh + 1),
                              512 * qb:512 * (qb + 1)], s_qb)
                b.wait("tensor", s_qb)
                b.wait_release("tensor", ("pcx",))
                nkt = 4 * qb + 4

                def emit_scores(kt, qb=qb, qsl=qsl, hh=hh):
                    u = kt % 2
                    sp = p512[u]
                    b.wait_release("tensor", ("sp", u))
                    ksl = slice(128 * kt, 128 * (kt + 1))
                    b.mm(sp[:], kh[:, ksl], qhb[:, qsl], True, False)
                    b.mm(sp[:], kh[:, ksl], qlb[:, qsl], False, False)
                    b.mm(sp[:], kl[:, ksl], qhb[:, qsl], False, False)
                    b.mm(sp[:], ones_s[:],
                         mrow_s[0:1, 512 * qb:512 * (qb + 1)], False, True,
                         sem=s_te)
                    if kt >= 4 * qb:
                        off = 384 - (128 * kt - 512 * qb)
                        b.wait("vector", s_te)
                        b.tt(sp[:], sp[:], mstr_s[:, off:off + 512], ALU.add,
                             sem=s_ve)
                        b.wait("scalar", s_ve)
                    b.wait_release("scalar", ("pt", u))
                    b.wait("scalar", s_te)
                    ptc = ptb[:, 512 * u:512 * (u + 1)]
                    b.act(ptc, sp[:], AF.Exp,
                          bias=bias_s[:, 2 * kt + hh:2 * kt + hh + 1],
                          scale=SCALE, sem=s_sc)
                    b.release(("sp", u), s_sc)
                    b.wait("scalar", s_sc)
                    phc = phb[:, 512 * u:512 * (u + 1)]
                    plc = plb[:, 512 * u:512 * (u + 1)]
                    b.wait_release("scalar", ("ph", u))
                    b.cp("scalar", phc, ptc, s_sc)
                    b.wait("vector", s_sc)
                    b.tt(plc, ptc, phc, ALU.subtract, sem=s_ve)
                    b.release(("pt", u), s_ve)
                    b.release(("phrdy", u), s_ve)

                def emit_pv(kt, nkt=nkt):
                    u = kt % 2
                    b.wait_release("tensor", ("phrdy", u))
                    vhc = vh[:, 132 * kt:132 * kt + 129]
                    vlc = vl[:, 132 * kt:132 * kt + 129]
                    for sq in range(4):
                        phs = phb[:, 512 * u + 128 * sq:
                                  512 * u + 128 * (sq + 1)]
                        pls = plb[:, 512 * u + 128 * sq:
                                  512 * u + 128 * (sq + 1)]
                        b.mm(pc[sq][:, 0:129], phs, vhc, kt == 0, False)
                        b.mm(pc[sq][:, 0:129], phs, vlc, False, False)
                        b.mm(pc[sq][:, 0:129], pls, vhc, False,
                             kt == nkt - 1)
                    b.inc_last("tensor", s_te)
                    b.release(("ph", u), s_te)

                for kt in range(nkt):
                    emit_scores(kt)
                    if kt >= 1:
                        emit_pv(kt - 1)
                emit_pv(nkt - 1)
                b.wait("vector", s_te)
                for sq in range(4):
                    b.recip(linv[:, sq % 2:sq % 2 + 1], pc[sq][:, 128:129],
                            sem=s_ve)
                    b.wait("vector", s_ve)
                    cts = ctmp[:, 128 * (sq % 2):128 * (sq % 2 + 1)]
                    b.wait_release("vector", ("ctmp", sq % 2))
                    b.ts(cts, pc[sq][:, 0:128], linv[:, sq % 2:sq % 2 + 1],
                         ALU.mult, sem=s_ve)
                    tp = pT[sq % 2]
                    b.wait_release("tensor", ("pT", sq % 2))
                    b.wait("tensor", s_ve)
                    b.tr(tp[:], cts, id32_s[:], s_te)
                    b.release(("ctmp", sq % 2), s_te)
                    b.wait("scalar", s_te)
                    ccol = S * hh + 512 * qb + 128 * sq
                    b.cp("scalar", ctxh[:, ccol:ccol + 128], tp[:], s_sc)
                    b.wait("vector", s_sc)
                    b.tt(ctxl[:, ccol:ccol + 128], tp[:],
                         ctxh[:, ccol:ccol + 128], ALU.subtract, sem=s_ve)
                    b.release(("pT", sq % 2), s_ve)
                b.release(("pcx",), s_ve)
                b.release(("qblk", g2), s_te)
            b.release(("kvbuf",), s_sc)

        b.barrier(s_bar)
        XC.close()
        XPC.close()
        XD = ExitStack()
        sd = XD.enter_context
        wof = sd(nc.sbuf_tensor("wof", [128, 2 * D], F32))
        woh = sd(nc.sbuf_tensor("woh", [128, 2 * D], BF16))
        wol = sd(nc.sbuf_tensor("wol", [128, 2 * D], BF16))
        ptm = sd(nc.sbuf_tensor("ptm", [128, 4 * 512], F32))
        pdx = [p512[0], p512[1],
               sd(nc.psum_tensor("pd2", [128, 512], F32)),
               sd(nc.psum_tensor("pd3", [128, 512], F32))]

        b.dma("sync", wof[:], wo.ap(), s_ld)
        b.wait("gpsimd", s_ld)
        b.dma("gpsimd", woh[:], wof[:], s_cast)
        b.wait("vector", s_cast)
        b.tt(wol[:], wof[:], woh[:], ALU.subtract, sem=s_ve)
        b.wait("tensor", s_ve)
        for blk in range(DBG.get('d_nblk', NBLK)):
            sh, slot = shard_slot(blk)
            for oc in range(16):
                g = (blk * 16 + oc) % 4
                ps = pdx[g]
                b.wait_release("tensor", ("p512d", g))
                for dh in range(2):
                    wcs = slice(2048 * dh + 128 * oc,
                                2048 * dh + 128 * (oc + 1))
                    csl = slice(S * dh + 512 * blk, S * dh + 512 * (blk + 1))
                    b.mm(ps[:], woh[:, wcs], ctxh[:, csl], dh == 0, False)
                    b.mm(ps[:], woh[:, wcs], ctxl[:, csl], False, False)
                    b.mm(ps[:], wol[:, wcs], ctxh[:, csl], False, dh == 1)
                b.inc_last("tensor", s_te)
                b.wait_release("vector", ("ptm", g))
                b.wait("vector", s_te)
                b.cp("vector", ptm[:, 512 * g:512 * (g + 1)], ps[:])
                b.inc_last("vector", s_ve)
                b.release(("p512d", g), s_ve)
                prow = (2 * sh + slot) * D + 128 * oc
                dq, ds_ = (("sync", s_pd) if g % 2 == 0 else
                           ("scalar", s_stash))
                b.wait(dq, s_ve)
                b.dma(dq, part_d.ap()[prow:prow + 128, :],
                      ptm[:, 512 * g:512 * (g + 1)], ds_)
                b.release(("ptm", g), ds_)
        XD.close()
        b.barrier(s_bar)
        b.op("gpsimd", lambda E: E.collective_compute(
            "ReduceScatter", ALU.add,
            replica_groups=[list(range(NCOR))],
            ins=[part_d.ap()], outs=[rs_d.ap()]), sems["s_cc"])
        b.nopinc("gpsimd", sems["s_np"])
    return rs_d



def emit_p2(nc, b, sems, id32, rs_d):
    x_own = nc.dram_tensor("x_own", [TOK, D], F32, kind="ExternalInput")
    bo_r = nc.dram_tensor("bo_r", [128, 16], F32, kind="ExternalInput")
    gwT = nc.dram_tensor("gwT", [IC0 * 128, D], F32, kind="ExternalInput")
    uwT = nc.dram_tensor("uwT", [IC0 * 128, D], F32, kind="ExternalInput")
    dwT = nc.dram_tensor("dwT", [16 * 128, I0], F32, kind="ExternalInput")
    gsc = nc.dram_tensor("gsc", [128, IC0], F32, kind="ExternalInput")
    usc = nc.dram_tensor("usc", [128, IC0], F32, kind="ExternalInput")
    dsc = nc.dram_tensor("dsc", [128, 16], F32, kind="ExternalInput")
    vwT = nc.dram_tensor("vwT", [4 * 128, D], F32, kind="ExternalInput")
    woT1 = nc.dram_tensor("woT1", [16 * 128, D], F32, kind="ExternalInput")
    gwT1 = nc.dram_tensor("gwT1", [128, 128], F32, kind="ExternalInput")
    w1 = nc.dram_tensor("w1", [E * HC * 128, D], F32, kind="ExternalInput")
    w2 = nc.dram_tensor("w2", [E * 16 * 128, HC * 128], F32,
                        kind="ExternalInput")
    w3 = nc.dram_tensor("w3", [E * HC * 128, D], F32, kind="ExternalInput")
    sw1 = nc.dram_tensor("sw1", [HC * 128, D], F32, kind="ExternalInput")
    sw2 = nc.dram_tensor("sw2", [16 * 128, HC * 128], F32,
                         kind="ExternalInput")
    sw3 = nc.dram_tensor("sw3", [HC * 128, D], F32, kind="ExternalInput")
    ones32 = nc.dram_tensor("ones32", [1, 128], F32, kind="ExternalInput")

    y = nc.dram_tensor("y", [TOK, D], F32, kind="ExternalOutput")

    with ExitStack() as X:
        sb = X.enter_context
        id32_s = sb(nc.sbuf_tensor("id32sB", [128, 128], F32))
        ones_s = sb(nc.sbuf_tensor("onessB", [1, 128], F32))
        bo_s = sb(nc.sbuf_tensor("bos", [128, 16], F32))
        gsc_s = sb(nc.sbuf_tensor("gscs", [128, IC0], F32))
        usc_s = sb(nc.sbuf_tensor("uscs", [128, IC0], F32))
        dsc_s = sb(nc.sbuf_tensor("dscs", [128, 16], F32))

        resid = sb(nc.sbuf_tensor("resid", [128, 16 * TOK], F32))
        xnh = sb(nc.sbuf_tensor("xnhB", [128, 16 * TOK], BF16))
        xnl = sb(nc.sbuf_tensor("xnlB", [128, 16 * TOK], BF16))
        st = {k: sb(nc.sbuf_tensor("stB_" + k, [128, 1], F32))
              for k in ("ssum", "ssq", "m", "mneg", "msq", "var", "r2",
                        "rstd", "nb")}

        pA = sb(nc.psum_tensor("pA", [128, 512], F32))
        pB = sb(nc.psum_tensor("pB", [128, 512], F32))
        pC = sb(nc.psum_tensor("pC", [128, 512], F32))
        pD = sb(nc.psum_tensor("pD", [128, 512], F32))
        pE = sb(nc.psum_tensor("pE", [128, 512], F32))
        pF = sb(nc.psum_tensor("pF", [128, 512], F32))
        pT = [sb(nc.psum_tensor("pU0", [128, 128], F32)),
              sb(nc.psum_tensor("pU1", [128, 128], F32))]
        pG = pT[1][:, 0:8]
        pT6 = [pT[0][:], pT[1][:], pA[:, 0:128], pB[:, 0:128],
               pC[:, 0:128], pD[:, 0:128]]
        assert nc.psum_banks_remaining >= 0
        assert nc.sbuf_bytes_remaining >= 0

        s_ld, s_x0, s_x1 = sems["s_ld"], sems["s_x0"], sems["s_x1"]
        s_cast, s_te, s_ve, s_sc = (sems["s_cast"], sems["s_te"],
                                    sems["s_ve"], sems["s_sc"])
        s_w, s_out, s_bar = sems["s_w"], sems["s_out"], sems["s_bar"]
        s_gp = sems["s_gp"]
        s_x = [s_x0, s_x1]

        b.dma("sync", id32_s[:], id32.ap(), s_ld)
        b.dma("sync", ones_s[:], ones32.ap(), s_ld)
        b.dma("sync", bo_s[:], bo_r.ap(), s_ld)
        b.dma("sync", gsc_s[:], gsc.ap(), s_ld)
        b.dma("sync", usc_s[:], usc.ap(), s_ld)
        b.dma("sync", dsc_s[:], dsc.ap(), s_ld)
        for e_ in ENGINES:
            b.wait(e_, s_ld)
        b.barrier(s_bar)

        X1 = ExitStack()
        s1 = X1.enter_context
        xb = s1(nc.sbuf_tensor("xbB", [128, 2 * D], F32))
        xntm = s1(nc.sbuf_tensor("xntmB", [128, D], F32))
        scr = xntm
        for t in range(8):
            p = t % 2
            xt = xb[:, D * p:D * (p + 1)]
            b.wait_release("sync", ("xb", p))
            b.dma("sync", xt, x_own.ap()[128 * t:128 * (t + 1), :], s_x[p])
            b.wait("vector", s_x[p])
            b.wait("scalar", s_x[p])
            emit_ln_tile(b, xt, scr, xntm, st, s_ve, s_sc)
            for j in range(16):
                col = TOK * j + 128 * t
                r0, r1 = (2 * j) % 6, (2 * j + 1) % 6
                tp0 = pT6[r0]
                b.wait_release("tensor", ("pT", r0))
                b.wait("tensor", s_x[p])
                b.tr(tp0, xt[:, 128 * j:128 * (j + 1)], id32_s[:], s_te)
                b.wait("vector", s_te)
                b.cp("vector", resid[:, col:col + 128], tp0)
                b.inc_last("vector", s_ve)
                b.release(("pT", r0), s_ve)
                tp1 = pT6[r1]
                b.wait_release("tensor", ("pT", r1))
                b.wait_release("tensor", ("xnt",))
                b.tr(tp1, xntm[:, 128 * j:128 * (j + 1)], id32_s[:], s_te)
                b.wait("scalar", s_te)
                b.cp("scalar", xnh[:, col:col + 128], tp1, s_sc)
                b.wait("vector", s_sc)
                b.tt(xnl[:, col:col + 128], tp1, xnh[:, col:col + 128],
                     ALU.subtract, sem=s_ve)
                b.release(("pT", r1), s_ve)
            b.release(("xb", p), s_te)
            b.release(("xntm",), s_te)
            b.wait_release("scalar", ("xntm",))
        b.barrier(s_bar)
        X1.close()

        X2 = ExitStack()
        s2_ = X2.enter_context
        huh = s2_(nc.sbuf_tensor("huh", [128, IC0 * 256], BF16))
        hul = s2_(nc.sbuf_tensor("hul", [128, IC0 * 256], BF16))
        gact = s2_(nc.sbuf_tensor("gact", [128, 256], F32))
        uact = s2_(nc.sbuf_tensor("uact", [128, 256], F32))
        humul = s2_(nc.sbuf_tensor("humul", [128, 256], F32))
        wbh = s2_(nc.sbuf_tensor("wbh", [128, 2 * 2048], BF16))
        wbh2 = s2_(nc.sbuf_tensor("wbh2", [128, 2 * 2048], BF16))
        dwh = s2_(nc.sbuf_tensor("dwh", [128, IC0 * 128], BF16))
        for qt in range(4):
            for ic in range(IC0):
                g = ic % 2
                wgh = wbh[:, 2048 * g:2048 * (g + 1)]
                wuh = wbh2[:, 2048 * g:2048 * (g + 1)]
                b.wait_release("gpsimd", ("fw", g))
                b.dma("gpsimd", wgh,
                      gwT.ap()[128 * ic:128 * (ic + 1), :], s_cast)
                b.dma("gpsimd", wuh,
                      uwT.ap()[128 * ic:128 * (ic + 1), :], s_cast)
                pa = (pA, pE)[g]
                pb = (pB, pF)[g]
                b.wait("tensor", s_cast)
                b.wait_release("tensor", ("pAB", g))
                b.wait_release("tensor", ("pAB_sc", g))
                for mj in range(16):
                    xsl = slice(TOK * mj + 256 * qt,
                                TOK * mj + 256 * (qt + 1))
                    wc = slice(128 * mj, 128 * (mj + 1))
                    b.mm(pa[:, 0:256], wgh[:, wc], xnh[:, xsl], mj == 0,
                         False)
                    b.mm(pa[:, 0:256], wgh[:, wc], xnl[:, xsl], False,
                         mj == 15)
                for mj in range(16):
                    xsl = slice(TOK * mj + 256 * qt,
                                TOK * mj + 256 * (qt + 1))
                    wc = slice(128 * mj, 128 * (mj + 1))
                    b.mm(pb[:, 0:256], wuh[:, wc], xnh[:, xsl], mj == 0,
                         False)
                    b.mm(pb[:, 0:256], wuh[:, wc], xnl[:, xsl], False,
                         mj == 15)
                b.inc_last("tensor", s_te)
                b.release(("fw", g), s_te)
                b.wait("scalar", s_te)
                b.wait_release("scalar", ("gact",))
                b.act(gact[:], pa[:, 0:256], AF.Silu,
                      scale=gsc_s[:, ic:ic + 1], sem=s_sc)
                b.release(("pAB_sc", g), s_sc)
                b.wait("vector", s_te)
                b.wait_release("vector", ("uact",))
                b.ts(uact[:], pb[:, 0:256], usc_s[:, ic:ic + 1], ALU.mult,
                     sem=s_ve)
                b.release(("pAB", g), s_ve)
                b.wait("vector", s_ve)
                b.wait("vector", s_sc)
                b.wait_release("vector", ("hm2",))
                b.tt(humul[:], gact[:], uact[:], ALU.mult, sem=s_ve)
                b.release(("gact",), s_ve)
                b.release(("uact",), s_ve)
                hucol = 256 * ic
                b.wait("scalar", s_ve)
                b.cp("scalar", huh[:, hucol:hucol + 256], humul[:], s_sc)
                b.release(("hm2",), s_sc)
                b.wait("vector", s_sc)
                b.tt(hul[:, hucol:hucol + 256], humul[:],
                     huh[:, hucol:hucol + 256], ALU.subtract, sem=s_ve)
            for oc in range(16):
                b.wait_release("gpsimd", ("dw",))
                b.dma("gpsimd", dwh[:],
                      dwT.ap()[128 * oc:128 * (oc + 1), :], s_cast)
                pdn = (pC, pD)[oc % 2]
                b.wait("tensor", s_cast)
                b.wait("tensor", s_ve)
                b.wait("tensor", s_sc)
                b.wait_release("tensor", ("pC", oc % 2))
                for ij in range(IC0):
                    wc = slice(128 * ij, 128 * (ij + 1))
                    b.mm(pdn[:, 0:256], dwh[:, wc],
                         huh[:, 256 * ij:256 * (ij + 1)], ij == 0, False)
                    b.mm(pdn[:, 0:256], dwh[:, wc],
                         hul[:, 256 * ij:256 * (ij + 1)], False,
                         ij == IC0 - 1)
                b.inc_last("tensor", s_te)
                b.release(("dw",), s_te)
                b.wait("vector", s_te)
                rsl = resid[:, TOK * oc + 256 * qt:TOK * oc + 256 * (qt + 1)]
                b.stt(rsl, pdn[:, 0:256], dsc_s[:, oc:oc + 1], rsl, ALU.mult,
                      ALU.add, sem=s_ve)
                b.release(("pC", oc % 2), s_ve)
        b.barrier(s_bar)
        X2.close()

        X3 = ExitStack()
        abuf = X3.enter_context(nc.sbuf_tensor("abuf", [128, TOK], F32))
        b.wait("sync", sems["s_cc"], 1)
        for fc in range(16):
            b.wait_release("sync", ("abuf",))
            b.dma("sync", abuf[:, 0:512],
                  rs_d.ap()[128 * fc:128 * (fc + 1), :], s_x0)
            b.dma("sync", abuf[:, 512:1024],
                  rs_d.ap()[D + 128 * fc:D + 128 * (fc + 1), :], s_x0)
            b.wait("vector", s_x0)
            rsl = resid[:, TOK * fc:TOK * (fc + 1)]
            b.stt(rsl, abuf[:], bo_s[:, fc:fc + 1], rsl, ALU.add, ALU.add,
                  sem=s_ve)
            b.release(("abuf",), s_ve)
        b.barrier(s_bar)
        X3.close()

        X4 = ExitStack()
        s4 = X4.enter_context
        xb = s4(nc.sbuf_tensor("xb4", [128, D], F32))
        xntm = s4(nc.sbuf_tensor("xntm4", [128, D], F32))
        scr = xntm
        for t in range(8):
            xt = xb[:]
            b.wait_release("vector", ("xbD4",))
            for j in range(16):
                r = j % 6
                tp = pT6[r]
                b.wait_release("tensor", ("pT", r))
                b.tr(tp,
                     resid[:, TOK * j + 128 * t:TOK * j + 128 * (t + 1)],
                     id32_s[:], s_te)
                b.wait("vector", s_te)
                b.cp("vector", xt[:, 128 * j:128 * (j + 1)], tp)
                b.inc_last("vector", s_ve)
                b.release(("pT", r), s_ve)
            b.wait("vector", s_ve)
            b.wait("scalar", s_ve)
            emit_ln_tile(b, xt, scr, xntm, st, s_ve, s_sc)
            for j in range(16):
                col = TOK * j + 128 * t
                r = j % 6
                tp = pT6[r]
                b.wait_release("tensor", ("pT", r))
                b.wait_release("tensor", ("xnt",))
                b.tr(tp, xntm[:, 128 * j:128 * (j + 1)], id32_s[:], s_te)
                b.wait("scalar", s_te)
                b.cp("scalar", xnh[:, col:col + 128], tp, s_sc)
                b.wait("vector", s_sc)
                b.tt(xnl[:, col:col + 128], tp, xnh[:, col:col + 128],
                     ALU.subtract, sem=s_ve)
                b.release(("pT", r), s_ve)
            b.release(("xbD4",), s_sc)
            b.release(("xntm",), s_te)
            b.wait_release("scalar", ("xntm",))
        b.barrier(s_bar)
        X4.close()

        X5 = ExitStack()
        s5 = X5.enter_context
        wbf = s5(nc.sbuf_tensor("wbf5", [128, 2 * 2048], F32))
        wpr = [(s5(nc.sbuf_tensor("wbh5", [128, 2048], BF16)),
                s5(nc.sbuf_tensor("wbl5", [128, 2048], BF16))),
               (s5(nc.sbuf_tensor("wbh25", [128, 2048], BF16)),
                s5(nc.sbuf_tensor("wbl25", [128, 2048], BF16)))]
        PWH = [(pA, pB), (pE, pF)]

        def wchunk2(col_ap, ncols, pair, hb=None, lb=None):
            hbuf, lbuf = wpr[pair] if hb is None else (hb, lb)
            wsl = wbf[:, 2048 * pair:2048 * pair + ncols]
            b.wait_release("sync", ("wbf", pair))
            b.dma("sync", wsl, col_ap, s_w)
            if hb is None:
                b.wait_release("gpsimd", ("wpair", pair))
            b.dma("gpsimd", hbuf[:, 0:ncols], col_ap, s_cast)
            b.wait("gpsimd", s_w)
            b.wait("gpsimd", s_cast)
            b.tt(lbuf[:, 0:ncols], wsl, hbuf[:, 0:ncols], ALU.subtract,
                 sem=s_gp, eng="gpsimd")
            b.release(("wbf", pair), s_gp)

        def mm3(ps, whb, wlb, xh_of, xl_of, nmj):
            for mj in range(nmj):
                wc = slice(128 * mj, 128 * (mj + 1))
                xh, xl = xh_of(mj), xl_of(mj)
                b.mm(ps, whb[:, wc], xh, mj == 0, False)
                b.mm(ps, whb[:, wc], xl, False, False)
                b.mm(ps, wlb[:, wc], xh, False, mj == nmj - 1)

        def xn_of(half):
            return (lambda mj, h=half: xnh[:, TOK * mj + 512 * h:
                                           TOK * mj + 512 * (h + 1)],
                    lambda mj, h=half: xnl[:, TOK * mj + 512 * h:
                                           TOK * mj + 512 * (h + 1)])

        X5b = ExitStack()
        s5b = X5b.enter_context
        v1h = s5b(nc.sbuf_tensor("v1h", [128, 4 * TOK], BF16))
        v1l = s5b(nc.sbuf_tensor("v1l", [128, 4 * TOK], BF16))
        for vc in range(4):
            pr = vc % 2
            wchunk2(vwT.ap()[128 * vc:128 * (vc + 1), :], 2048, pr)
            whb, wlb = wpr[pr]
            b.wait("tensor", s_gp)
            for half in range(2):
                ps = PWH[pr][half]
                b.wait_release("tensor", ("pw", pr, half))
                fh, fl = xn_of(half)
                mm3(ps[:], whb, wlb, fh, fl, 16)
                b.inc_last("tensor", s_te)
                if half == 1:
                    b.release(("wpair", pr), s_te)
                col = TOK * vc + 512 * half
                b.wait("scalar", s_te)
                b.cp("scalar", v1h[:, col:col + 512], ps[:], s_sc)
                b.wait("vector", s_sc)
                b.tt(v1l[:, col:col + 512], ps[:], v1h[:, col:col + 512],
                     ALU.subtract, sem=s_ve)
                b.release(("pw", pr, half), s_ve)
        for oc in range(16):
            pr = oc % 2
            wchunk2(woT1.ap()[128 * oc:128 * (oc + 1), :], 2048, pr)
            whb, wlb = wpr[pr]
            b.wait("tensor", s_gp)
            for half in range(2):
                ps = PWH[pr][half]
                b.wait_release("tensor", ("pw", pr, half))
                mm3(ps[:], whb, wlb,
                    lambda mj, h=half: v1h[:, TOK * (mj // 4) + 512 * h:
                                           TOK * (mj // 4) + 512 * (h + 1)],
                    lambda mj, h=half: v1l[:, TOK * (mj // 4) + 512 * h:
                                           TOK * (mj // 4) + 512 * (h + 1)],
                    16)
                b.inc_last("tensor", s_te)
                if half == 1:
                    b.release(("wpair", pr), s_te)
                b.wait("vector", s_te)
                rsl = resid[:, TOK * oc + 512 * half:
                            TOK * oc + 512 * (half + 1)]
                b.tt(rsl, rsl, ps[:], ALU.add, sem=s_ve)
                b.release(("pw", pr, half), s_ve)
        b.barrier(s_bar)
        X5b.close()

        X6 = ExitStack()
        s6 = X6.enter_context
        comb = s6(nc.sbuf_tensor("comb", [8, TOK], F32))
        g1h = s6(nc.sbuf_tensor("g1h", [128, 16 * 8], BF16))
        g1l = s6(nc.sbuf_tensor("g1l", [128, 16 * 8], BF16))
        sg8 = s6(nc.sbuf_tensor("sg8", [128, 8], F32))
        t8a = s6(nc.sbuf_tensor("t8a", [128, 8], F32))
        t8b = s6(nc.sbuf_tensor("t8b", [128, 8], F32))
        m81 = s6(nc.sbuf_tensor("m81", [128, 1], F32))
        m82 = s6(nc.sbuf_tensor("m82", [128, 1], F32))
        wchunk2(gwT1.ap(), 128, 0, hb=g1h, lb=g1l)
        b.wait("tensor", s_gp)
        for t in range(8):
            b.wait("tensor", s_ve)
            b.wait_release("tensor", ("pg8",))
            for mj in range(16):
                xh = xnh[:, TOK * mj + 128 * t:TOK * mj + 128 * (t + 1)]
                xl = xnl[:, TOK * mj + 128 * t:TOK * mj + 128 * (t + 1)]
                gc = slice(8 * mj, 8 * (mj + 1))
                b.mm(pG, xh, g1h[:, gc], mj == 0, False)
                b.mm(pG, xl, g1h[:, gc], False, False)
                b.mm(pG, xh, g1l[:, gc], False, mj == 15)
            b.inc_last("tensor", s_te)
            b.wait("scalar", s_te)
            b.act(sg8[:], pG, AF.Sigmoid, sem=s_sc)
            b.wait("vector", s_sc)
            def _vs(fn):
                b.op("vector", fn, s_ve)
                b.wait("vector", s_ve)
            b.red(m81[:], sg8[:], ALU.max, sem=s_ve)
            b.wait("vector", s_ve)
            _vs(lambda E, o=t8a[:], i=sg8[:], s=m81[:]:
                E.tensor_scalar(out=o, in0=i, scalar1=s, scalar2=None,
                                op0=ALU.subtract))
            _vs(lambda E, o=t8a[:], i=t8a[:]:
                E.tensor_scalar(out=o, in0=i, scalar1=0.0, scalar2=None,
                                op0=ALU.is_ge))
            _vs(lambda E, o=t8b[:], i=t8a[:], j=sg8[:]:
                E.scalar_tensor_tensor(out=o, in0=i, scalar=-1e30, in1=j,
                                       op0=ALU.mult, op1=ALU.add))
            b.red(m82[:], t8b[:], ALU.max, sem=s_ve)
            b.wait("vector", s_ve)
            _vs(lambda E, o=t8a[:], i=sg8[:], s=m82[:]:
                E.tensor_scalar(out=o, in0=i, scalar1=s, scalar2=None,
                                op0=ALU.subtract))
            _vs(lambda E, o=t8a[:], i=t8a[:]:
                E.tensor_scalar(out=o, in0=i, scalar1=0.0, scalar2=None,
                                op0=ALU.is_ge))
            _vs(lambda E, o=t8a[:], i=t8a[:], j=sg8[:]:
                E.tensor_tensor(o, i, j, ALU.mult))
            _vs(lambda E, o=m82[:], i=m82[:], j=m81[:]:
                E.tensor_tensor(o, i, j, ALU.add))
            _vs(lambda E, o=m82[:], i=m82[:]:
                E.tensor_scalar(out=o, in0=i, scalar1=1e-20, scalar2=None,
                                op0=ALU.add))
            _vs(lambda E, o=m81[:], i=m82[:]: E.reciprocal(o, i))
            b.ts(t8b[:], t8a[:], m81[:], ALU.mult, s2=2.5, op1=ALU.mult,
                 sem=s_ve)
            b.release(("pg8",), s_ve)
            tp = pT[0]
            b.wait_release("tensor", ("pT", 0))
            b.wait("tensor", s_ve)
            b.tr(tp[0:8, :], t8b[:], id32_s[:], s_te)
            b.wait("vector", s_te)
            b.cp("vector", comb[0:8, 128 * t:128 * (t + 1)], tp[0:8, 0:128])
            b.inc_last("vector", s_ve)
            b.release(("pT", 0), s_ve)
        b.barrier(s_bar)

        X7 = ExitStack()
        s7 = X7.enter_context
        GRPS = [list(range(0, 6)), list(range(6, HC))]
        GMAX = max(len(g) for g in GRPS)
        huh = s7(nc.sbuf_tensor("huh7", [128, GMAX * TOK], BF16))
        hul = s7(nc.sbuf_tensor("hul7", [128, GMAX * TOK], BF16))
        gact = s7(nc.sbuf_tensor("gact7", [128, 512], F32))
        uact = s7(nc.sbuf_tensor("uact7", [128, 512], F32))
        humul = s7(nc.sbuf_tensor("humul7", [128, 512], F32))
        cbc = s7(nc.sbuf_tensor("cbc", [128, TOK], F32))
        crow = s7(nc.sbuf_tensor("crow", [1, TOK], F32))
        evb = s7(nc.sbuf_tensor("evb", [128, 512], F32))

        def silu_mul_split2(pa, pb, hucol, kpa, kpb):
            b.wait("scalar", s_te)
            b.wait_release("scalar", ("gact",))
            b.act(gact[:], pa[:], AF.Silu, sem=s_sc)
            b.release(kpa, s_sc)
            b.wait("vector", s_te)
            b.wait_release("vector", ("uact",))
            b.cp("vector", uact[:], pb[:], sem=s_ve)
            b.release(kpb, s_ve)
            b.wait("vector", s_ve)
            b.wait("vector", s_sc)
            b.wait_release("vector", ("humul",))
            b.tt(humul[:], gact[:], uact[:], ALU.mult, sem=s_ve)
            b.release(("gact",), s_ve)
            b.release(("uact",), s_ve)
            b.wait("scalar", s_ve)
            b.wait_release("scalar", ("hu",))
            b.cp("scalar", huh[:, hucol:hucol + 512], humul[:], s_sc)
            b.release(("humul",), s_sc)
            b.wait("vector", s_sc)
            b.wait_release("vector", ("hu",))
            b.tt(hul[:, hucol:hucol + 512], humul[:],
                 huh[:, hucol:hucol + 512], ALU.subtract, sem=s_ve)

        def expert_hu(w1ap, w3ap, grp):
            for i_hc, hc in enumerate(grp):
                par = i_hc % 2
                wchunk2(w1ap(hc), 2048, 0)
                whb, wlb = wpr[0]
                b.wait("tensor", s_gp)
                for half in range(2):
                    b.wait_release("tensor", ("pw", par, half))
                    fh, fl = xn_of(half)
                    mm3(PWH[par][half][:], whb, wlb, fh, fl, 16)
                    b.inc_last("tensor", s_te)
                b.release(("wpair", 0), s_te)
                wchunk2(w3ap(hc), 2048, 1)
                whb2, wlb2 = wpr[1]
                b.wait("tensor", s_gp)
                for half in range(2):
                    b.wait_release("tensor", ("pw3", half))
                    fh, fl = xn_of(half)
                    mm3((pC, pD)[half][:], whb2, wlb2, fh, fl, 16)
                    b.inc_last("tensor", s_te)
                b.release(("wpair", 1), s_te)
                for half in range(2):
                    silu_mul_split2(PWH[par][half], (pC, pD)[half],
                                    TOK * i_hc + 512 * half,
                                    ("pw", par, half), ("pw3", half))

        def expert_down(w2colap, grp, use_cbc):
            nij = len(grp)
            for oc in range(16):
                par = oc % 2
                wchunk2(w2colap(oc), nij * 128, par)
                whb, wlb = wpr[par]
                b.wait("tensor", s_gp)
                b.wait("tensor", s_ve)
                b.wait("tensor", s_sc)
                for half in range(2):
                    ps = PWH[par][half]
                    b.wait_release("tensor", ("pw", par, half))
                    mm3(ps[:], whb, wlb,
                        lambda ij, h=half: huh[:, TOK * ij + 512 * h:
                                               TOK * ij + 512 * (h + 1)],
                        lambda ij, h=half: hul[:, TOK * ij + 512 * h:
                                               TOK * ij + 512 * (h + 1)],
                        nij)
                    b.inc_last("tensor", s_te)
                    if half == 1:
                        b.release(("wpair", par), s_te)
                    b.wait("vector", s_te)
                    rsl = resid[:, TOK * oc + 512 * half:
                                TOK * oc + 512 * (half + 1)]
                    if use_cbc:
                        b.wait_release("vector", ("evb",))
                        b.tt(evb[:], ps[:],
                             cbc[:, 512 * half:512 * (half + 1)],
                             ALU.mult, sem=s_ve)
                        b.release(("pw", par, half), s_ve)
                        b.wait("vector", s_ve)
                        b.tt(rsl, rsl, evb[:], ALU.add, sem=s_ve)
                        b.release(("evb",), s_ve)
                    else:
                        b.tt(rsl, rsl, ps[:], ALU.add, sem=s_ve)
                        b.release(("pw", par, half), s_ve)
            b.release(("hu",), s_te)

        for e in range(E):
            b.wait_release("sync", ("crow",))
            b.dma("sync", crow[:], comb[e:e + 1, :], s_w)
            b.wait("tensor", s_w)
            for half in range(2):
                ps = PWH[0][half]
                b.wait_release("tensor", ("pw", 0, half))
                b.mm(ps[:], ones_s[:],
                     crow[0:1, 512 * half:512 * (half + 1)], True, True)
                b.inc_last("tensor", s_te)
            b.release(("crow",), s_te)
            b.wait("vector", s_te)
            for half in range(2):
                ps = PWH[0][half]
                b.cp("vector", cbc[:, 512 * half:512 * (half + 1)], ps[:])
                b.inc_last("vector", s_ve)
                b.release(("pw", 0, half), s_ve)
            for grp in GRPS:
                g0, ng = grp[0], len(grp)
                expert_hu(
                    lambda hc, e=e: w1.ap()[(e * HC + hc) * 128:
                                            (e * HC + hc + 1) * 128, :],
                    lambda hc, e=e: w3.ap()[(e * HC + hc) * 128:
                                            (e * HC + hc + 1) * 128, :],
                    grp)
                expert_down(
                    lambda oc, e=e, g0=g0, n=ng:
                    w2.ap()[(e * 16 + oc) * 128:(e * 16 + oc + 1) * 128,
                            128 * g0:128 * (g0 + n)],
                    grp, True)

        for grp in GRPS:
            g0, ng = grp[0], len(grp)
            expert_hu(
                lambda hc: sw1.ap()[128 * hc:128 * (hc + 1), :],
                lambda hc: sw3.ap()[128 * hc:128 * (hc + 1), :],
                grp)
            expert_down(
                lambda oc, g0=g0, n=ng:
                sw2.ap()[128 * oc:128 * (oc + 1), 128 * g0:128 * (g0 + n)],
                grp, False)
        b.barrier(s_bar)
        X7.close()
        X6.close()
        X5.close()

        X9 = ExitStack()
        xb = X9.enter_context(nc.sbuf_tensor("xb9", [128, D], F32))
        for t in range(8):
            b.wait_release("vector", ("yst",))
            for j in range(16):
                r = j % 6
                tp = pT6[r]
                b.wait_release("tensor", ("pT", r))
                b.tr(tp,
                     resid[:, TOK * j + 128 * t:TOK * j + 128 * (t + 1)],
                     id32_s[:], s_te)
                b.wait("vector", s_te)
                b.cp("vector", xb[:, 128 * j:128 * (j + 1)], tp)
                b.inc_last("vector", s_ve)
                b.release(("pT", r), s_ve)
            b.wait("sync", s_ve)
            b.dma("sync", y.ap()[128 * t:128 * (t + 1), :], xb[:], s_out)
            b.release(("yst",), s_out)
        b.wait("sync", s_out)
        X9.close()


def build_fused():
    nc = bass.Bass("TRN2", target_bir_lowering=False,
                   detect_race_conditions=False)
    id32 = nc.dram_tensor("id32", [128, 128], F32, kind="ExternalInput")
    with ExitStack() as XS:
        sems = {k: Sem(XS.enter_context(nc.semaphore(k))) for k in
                ("s_ld", "s_x0", "s_x1", "s_cast", "s_te", "s_ve", "s_sc",
                 "s_stash", "s_qb", "s_kv", "s_pd", "s_cc", "s_out", "s_bar",
                 "s_w", "s_np", "s_gp")}
        b = Bld(nc)
        rs_d = emit_p1(nc, b, sems, id32)
        emit_p2(nc, b, sems, id32, rs_d)
        b.emit()
    return nc



def _t128(a):
    a = np.ascontiguousarray(np.asarray(a, np.float32))
    k, n = a.shape
    return np.ascontiguousarray(
        a.reshape(k // 128, 128, n).swapaxes(0, 1).reshape(128, -1))


def _tcol(a, cw=128):
    a = np.ascontiguousarray(np.asarray(a, np.float32))
    k, n = a.shape
    return np.ascontiguousarray(
        a.reshape(k // 128, 128, n // cw, cw).transpose(2, 1, 0, 3)
        .reshape((n // cw) * 128, (k // 128) * cw))


def _alibi_slopes():
    ratio = 2.0 ** (-8.0 / 16.0)
    return np.array([ratio ** (h + 1) for h in range(16)], dtype=np.float64)


def _p1_inputs(x, Wq, Wk, Wv, Wo, bq, bk, bv):
    f32 = np.float32
    slopes = _alibi_slopes()
    pcol = np.arange(128, dtype=np.float64)[:, None]
    ucol = np.arange(896, dtype=np.float64)[None, :]
    mstrip = np.where(ucol >= pcol + 384, 0.0, -1e9).astype(f32)
    id32 = np.eye(128, dtype=f32)
    idbf = np.eye(128, dtype=BF16NP)
    onesbf = np.ones((1, 128), dtype=BF16NP)
    qi = np.arange(S, dtype=np.float64)
    kt = np.arange(64, dtype=np.float64)[None, :]
    base = 128.0 * kt + pcol - 8191.0

    ins = []
    for c in range(NCOR):
        hsl = slice(256 * c, 256 * (c + 1))
        bqkv = np.zeros((128, 6), f32)
        for i3, bb in enumerate((bq, bk, bv)):
            for oc in range(2):
                bqkv[:, 2 * i3 + oc] = bb[hsl].reshape(2, 128)[oc]
        biask = np.zeros((128, 128), np.float64)
        for oc in range(2):
            biask[:, 2 * np.arange(64) + oc] = slopes[2 * c + oc] * base
        mr = np.stack([-SQ128 * slopes[2 * c + oc] * (qi - 8191.0)
                       for oc in range(2)])
        ins.append({
            "x": x,
            "wq": _t128(Wq[:, hsl]),
            "wk": _t128(Wk[:, hsl]),
            "wv": _t128(Wv[:, hsl]),
            "bqkv": bqkv,
            "wo": _t128(Wo[hsl, :]),
            "biask": biask.astype(f32),
            "mrow": mr.astype(BF16NP),
            "mstrip": mstrip,
            "id32": id32, "idbf": idbf, "onesbf": onesbf,
        })
    return ins


def _p2_shared(L0_attn_bo, L0_ffn_gate_w, L0_ffn_up_w, L0_ffn_down_w,
               L0_ffn_gate_s, L0_ffn_up_s, L0_ffn_down_s,
               L1_attn_wqkv, L1_attn_wo, L1_ffn_gate_w,
               L1_ffn_w1, L1_ffn_w2, L1_ffn_w3,
               L1_ffn_sw1, L1_ffn_sw2, L1_ffn_sw3):
    f32 = np.float32
    wqkv = np.asarray(L1_attn_wqkv, f32)
    v_rows = np.concatenate([np.arange(2176 + 256 * g, 2304 + 256 * g)
                             for g in range(4)])
    w1a = np.asarray(L1_ffn_w1, f32)
    w2a = np.asarray(L1_ffn_w2, f32)
    w3a = np.asarray(L1_ffn_w3, f32)
    return {
        "bo_r": np.ascontiguousarray(
            np.asarray(L0_attn_bo, f32).reshape(16, 128).T),
        "gwT": _tcol(np.asarray(L0_ffn_gate_w, f32).T),
        "uwT": _tcol(np.asarray(L0_ffn_up_w, f32).T),
        "dwT": _tcol(np.asarray(L0_ffn_down_w, f32).T),
        "gsc": np.ascontiguousarray(
            np.asarray(L0_ffn_gate_s, f32).reshape(IC0, 128).T),
        "usc": np.ascontiguousarray(
            np.asarray(L0_ffn_up_s, f32).reshape(IC0, 128).T),
        "dsc": np.ascontiguousarray(
            np.asarray(L0_ffn_down_s, f32).reshape(16, 128).T),
        "vwT": _tcol(wqkv[v_rows, :].T),
        "woT1": _tcol(np.asarray(L1_attn_wo, f32).T),
        "gwT1": _t128(np.asarray(L1_ffn_gate_w, f32).T),
        "w1": np.ascontiguousarray(
            w1a.reshape(E, 16, 128, HC, 128).transpose(0, 3, 2, 1, 4)
            .reshape(E * HC * 128, D)),
        "w2": np.ascontiguousarray(
            w2a.reshape(E, HC, 128, 16, 128).transpose(0, 3, 2, 1, 4)
            .reshape(E * 16 * 128, HC * 128)),
        "w3": np.ascontiguousarray(
            w3a.reshape(E, 16, 128, HC, 128).transpose(0, 3, 2, 1, 4)
            .reshape(E * HC * 128, D)),
        "sw1": _tcol(np.asarray(L1_ffn_sw1, f32)),
        "sw2": _tcol(np.asarray(L1_ffn_sw2, f32)),
        "sw3": _tcol(np.asarray(L1_ffn_sw3, f32)),
        "ones32": np.ones((1, 128), dtype=f32),
    }


def kernel(L0_attn_x, L0_attn_Wq, L0_attn_bq, L0_attn_Wk, L0_attn_bk,
           L0_attn_Wv, L0_attn_bv, L0_attn_Wo, L0_attn_bo,
           L0_ffn_gate_w, L0_ffn_up_w, L0_ffn_down_w,
           L0_ffn_gate_s, L0_ffn_up_s, L0_ffn_down_s,
           L1_attn_wqkv, L1_attn_wo,
           L1_ffn_gate_w, L1_ffn_w1, L1_ffn_w2, L1_ffn_w3,
           L1_ffn_sw1, L1_ffn_sw2, L1_ffn_sw3):
    f32 = np.float32
    x = np.ascontiguousarray(np.asarray(L0_attn_x, dtype=f32).reshape(S, D))

    in1 = _p1_inputs(x, np.asarray(L0_attn_Wq, f32), np.asarray(L0_attn_Wk, f32),
                     np.asarray(L0_attn_Wv, f32), np.asarray(L0_attn_Wo, f32),
                     np.asarray(L0_attn_bq, f32), np.asarray(L0_attn_bk, f32),
                     np.asarray(L0_attn_bv, f32))
    shared = _p2_shared(L0_attn_bo, L0_ffn_gate_w, L0_ffn_up_w,
                        L0_ffn_down_w, L0_ffn_gate_s, L0_ffn_up_s,
                        L0_ffn_down_s, L1_attn_wqkv, L1_attn_wo,
                        L1_ffn_gate_w, L1_ffn_w1, L1_ffn_w2, L1_ffn_w3,
                        L1_ffn_sw1, L1_ffn_sw2, L1_ffn_sw3)
    ins = []
    for c in range(NCOR):
        d = dict(in1[c])
        d.update(shared)
        d["x_own"] = np.ascontiguousarray(x[tok_index(c)])
        ins.append(d)
    nc = build_fused()
    res = run_bass_kernel_spmd(nc, ins, core_ids=list(range(NCOR)))

    out = np.zeros((S, D), dtype=f32)
    for c in range(NCOR):
        out[tok_index(c)] = res.results[c]["y"]
    return out.reshape(1, S, D)


# version 11
# speedup vs baseline: 1.1783x; 1.0242x over previous
import numpy as np
import ml_dtypes
from contextlib import ExitStack

import concourse.bass as bass
import concourse.mybir as mybir
from concourse.bass_utils import run_bass_kernel_spmd

F32 = mybir.dt.float32
BF16 = mybir.dt.bfloat16
AF = mybir.ActivationFunctionType
ALU = mybir.AluOpType
AX = mybir.AxisListType.X

NCOR = 8
S = 8192
D = 2048
NBLK = 16
TOK = 1024
I0 = 5632
IC0 = I0 // 128
E = 8
HE = 1408
HC = HE // 128
SCALE = float(1.0 / np.sqrt(np.float32(128.0)))
EPS = 1e-5
SQ128 = float(np.sqrt(np.float64(128.0)))

ENGINES = ("sync", "tensor", "vector", "scalar", "gpsimd")
DBG = {"qkv_n": 6, "qkv_evac": "ts", "qkv_stash": True}
BF16NP = ml_dtypes.bfloat16


def blocks_of(core):
    return (core, 15 - core)


def tok_index(core):
    b0, b1 = blocks_of(core)
    return np.concatenate([np.arange(512 * b0, 512 * b0 + 512),
                           np.arange(512 * b1, 512 * b1 + 512)])


def shard_slot(b):
    return (b, 0) if b < 8 else (15 - b, 1)


class Sem:
    def __init__(self, h):
        self.h = h
        self.n = 0


class Bld:
    def __init__(self, nc):
        self.nc = nc
        self.q = {e: [] for e in ENGINES}
        self.rel = {}
        self.dma_sems = set()

    def emit(self):
        with self.nc.Block() as block:
            for e in ENGINES:
                ops = self.q[e]
                getattr(block, e)(lambda eng, ops=ops: [op(eng) for op in ops])

    def wait(self, eng, sem, val=None):
        v = sem.n if val is None else val
        if v <= 0:
            return
        g = lambda E, s=sem.h, v=v: E.wait_ge(s, v)
        g.is_op = False
        g.sem = None
        self.q[eng].append(g)

    def release(self, key, sem):
        self.rel[key] = (sem, sem.n)

    def wait_release(self, eng, key):
        if key in self.rel:
            sem, v = self.rel[key]
            self.wait(eng, sem, v)

    def op(self, eng, fn, sem=None, inc=1):
        def g(E, fn=fn, sem=sem, inc=inc):
            i = fn(E)
            if sem is not None:
                i.then_inc(sem.h, inc)
            return i
        g.is_op = True
        g.sem = sem
        self.q[eng].append(g)
        if sem is not None:
            sem.n += inc

    def inc_last(self, eng, sem, inc=1):
        prev = self.q[eng].pop()
        assert getattr(prev, "is_op", False), "inc_last needs a real op"
        assert getattr(prev, "sem", None) is None, \
            "inc_last on an op that already has a sem update"
        def g(E, prev=prev, sem=sem, inc=inc):
            i = prev(E)
            i.then_inc(sem.h, inc)
            return i
        g.is_op = True
        g.sem = sem
        self.q[eng].append(g)
        sem.n += inc

    def dma(self, eng, out, in_, sem, inc=16):
        self.dma_sems.add(sem)
        self.wait(eng, sem)
        self.op(eng, lambda E, o=out, i=in_: E.dma_start(out=o, in_=i),
                sem, inc)

    def mm(self, out, lh, rh, start, stop, sem=None, inc=1):
        self.op("tensor",
                lambda E, o=out, l=lh, r=rh, a=start, b=stop:
                E.matmul(o, l, r, start=a, stop=b), sem, inc)

    def tr(self, out, in_, ident, sem=None):
        self.op("tensor",
                lambda E, o=out, i=in_, d=ident: E.transpose(o, i, d), sem)

    def cp(self, eng, out, in_, sem=None):
        if eng == "scalar":
            self.op(eng, lambda E, o=out, i=in_: E.copy(out=o, in_=i), sem)
        else:
            self.op(eng, lambda E, o=out, i=in_: E.tensor_copy(out=o, in_=i),
                    sem)

    def tt(self, out, a, c, alu, sem=None, eng="vector"):
        self.op(eng, lambda E, o=out, x=a, y=c, u=alu:
                E.tensor_tensor(o, x, y, u), sem)

    def ts(self, out, in0, s1, op0, s2=None, op1=None, sem=None, eng="vector"):
        if op1 is None:
            self.op(eng, lambda E, o=out, i=in0, a=s1, u=op0:
                    E.tensor_scalar(out=o, in0=i, scalar1=a, scalar2=None,
                                    op0=u), sem)
        else:
            self.op(eng, lambda E, o=out, i=in0, a=s1, c=s2, u=op0, v=op1:
                    E.tensor_scalar(out=o, in0=i, scalar1=a, scalar2=c,
                                    op0=u, op1=v), sem)

    def stt(self, out, in0, sc, in1, op0, op1, sem=None, eng="vector"):
        self.op(eng, lambda E, o=out, i=in0, s=sc, j=in1, u=op0, v=op1:
                E.scalar_tensor_tensor(out=o, in0=i, scalar=s, in1=j,
                                       op0=u, op1=v), sem)

    def act(self, out, in_, func, bias=0.0, scale=1.0, accum=None, sem=None):
        self.op("scalar", lambda E, o=out, i=in_, f=func, c=bias, s=scale,
                a=accum:
                E.activation(out=o, in_=i, func=f, bias=c, scale=s,
                             accum_out=a), sem)

    def red(self, out, in_, alu, sem=None):
        self.op("vector", lambda E, o=out, i=in_, u=alu:
                E.tensor_reduce(o, i, AX, u), sem)

    def recip(self, out, in_, sem=None):
        self.op("vector", lambda E, o=out, i=in_: E.reciprocal(o, i), sem)

    def nopinc(self, eng, sem, inc=1):
        self.op(eng, lambda E: E.nop(), sem, inc)

    def memset(self, ap, v, sem=None):
        self.op("gpsimd", lambda E, o=ap, w=v: E.memset(o, w), sem)

    def barrier(self, sem):
        for e in ENGINES:
            for ds in self.dma_sems:
                self.wait(e, ds)
        for e in ENGINES:
            idx = None
            for j in range(len(self.q[e]) - 1, -1, -1):
                if getattr(self.q[e][j], "is_op", False):
                    idx = j
                    break
            if idx is None:
                self.nopinc(e, sem)
                continue
            last_op = self.q[e][idx]
            if last_op.sem is not None:
                self.wait(e, last_op.sem)
                self.nopinc(e, sem)
            else:
                def g(E, prev=last_op, sem=sem):
                    i = prev(E)
                    i.then_inc(sem.h, 1)
                    return i
                g.is_op = True
                g.sem = sem
                self.q[e][idx] = g
                sem.n += 1
        tgt = sem.n
        for e in ENGINES:
            self.wait(e, sem, tgt)



def csplit(ap2d, p=128):
    return ap2d.rearrange("(mj p) o -> p mj o", p=p)


def dsplit(sbuf_ap, owid):
    return sbuf_ap.rearrange("p (mj o) -> p mj o", o=owid)

def emit_ln_tile(b, xb, scr, xntm, st, s_ve, s_sc):
    def v(out, fn):
        b.op("vector", fn, s_ve)
        b.wait("vector", s_ve)
    b.red(st["ssum"][:], xb[:], ALU.add, sem=s_ve)
    b.act(scr[:], xb[:], AF.Square, accum=st["ssq"][:], sem=s_sc)
    b.wait("vector", s_ve)
    v(None, lambda E, o=st["m"][:], i=st["ssum"][:]:
      E.tensor_scalar(out=o, in0=i, scalar1=1.0 / D, scalar2=None,
                      op0=ALU.mult))
    v(None, lambda E, o=st["mneg"][:], i=st["m"][:]:
      E.tensor_scalar(out=o, in0=i, scalar1=-1.0, scalar2=None, op0=ALU.mult))
    v(None, lambda E, o=st["msq"][:], i=st["m"][:]:
      E.tensor_tensor(o, i, i, ALU.mult))
    b.wait("vector", s_sc)
    v(None, lambda E, o=st["var"][:], i=st["ssq"][:]:
      E.tensor_scalar(out=o, in0=i, scalar1=1.0 / D, scalar2=None,
                      op0=ALU.mult))
    v(None, lambda E, o=st["var"][:], i=st["var"][:], j=st["msq"][:]:
      E.tensor_tensor(o, i, j, ALU.subtract))
    v(None, lambda E, o=st["var"][:], i=st["var"][:]:
      E.tensor_scalar(out=o, in0=i, scalar1=EPS, scalar2=None, op0=ALU.add))
    b.wait("scalar", s_ve)
    b.act(st["r2"][:], st["var"][:], AF.Ln, sem=s_sc)
    b.wait("scalar", s_sc)
    b.act(st["rstd"][:], st["r2"][:], AF.Exp, scale=-0.5, sem=s_sc)
    b.wait("scalar", s_sc)
    b.wait("vector", s_sc)
    v(None, lambda E, o=st["nb"][:], i=st["mneg"][:], j=st["rstd"][:]:
      E.tensor_tensor(o, i, j, ALU.mult))
    b.wait("scalar", s_ve)
    b.act(xntm[:], xb[:], AF.Identity, bias=st["nb"][:], scale=st["rstd"][:],
          sem=s_sc)
    b.release(("xnt",), s_sc)



def emit_p1(nc, b, sems, id32, stop_after=None):
    x = nc.dram_tensor("x", [S, D], F32, kind="ExternalInput")
    wq = nc.dram_tensor("wq", [128, 4096], F32, kind="ExternalInput")
    wk = nc.dram_tensor("wk", [128, 4096], F32, kind="ExternalInput")
    wv = nc.dram_tensor("wv", [128, 4096], F32, kind="ExternalInput")
    bqkv = nc.dram_tensor("bqkv", [128, 6], F32, kind="ExternalInput")
    wo = nc.dram_tensor("wo", [128, 2 * D], F32, kind="ExternalInput")
    biask = nc.dram_tensor("biask", [128, 128], F32, kind="ExternalInput")
    mrow = nc.dram_tensor("mrow", [2, S], BF16, kind="ExternalInput")
    mstrip = nc.dram_tensor("mstrip", [128, 896], F32, kind="ExternalInput")
    idbf = nc.dram_tensor("idbf", [128, 128], BF16, kind="ExternalInput")
    onesbf = nc.dram_tensor("onesbf", [1, 128], BF16, kind="ExternalInput")

    part_dump = (nc.dram_tensor("part_dump", [8 * D, TOK], F32,
                                kind="ExternalOutput")
                 if DBG.get("dump_all") else None)
    if DBG.get("dump_ln"):
        ln_dump = nc.dram_tensor("ln_dump", [128, 16], F32, kind="ExternalOutput")
        xn_dump = nc.dram_tensor("xn_dump", [128, D], F32, kind="ExternalOutput")
        xnh_dump = nc.dram_tensor("xnh_dump", [128, 16 * 512], BF16, kind="ExternalOutput")
        xnl_dump = nc.dram_tensor("xnl_dump", [128, 16 * 512], BF16, kind="ExternalOutput")
    if DBG.get("dump_mid"):
        qd_dump = nc.dram_tensor("qd_dump", [512, S], BF16, kind="ExternalOutput")
        kd_dump = nc.dram_tensor("kd_dump", [512, S], BF16, kind="ExternalOutput")
        vd_dump = nc.dram_tensor("vd_dump", [512, 64 * 132], BF16, kind="ExternalOutput")

    qd = nc.dram_tensor("qd", [512, S], BF16)
    kd = nc.dram_tensor("kd", [512, S], BF16)
    vd = nc.dram_tensor("vd", [512, 64 * 132], BF16)
    part_d = nc.dram_tensor("part_d", [16 * D, 512], F32)
    rs_d = nc.dram_tensor("rs_d", [2 * D, 512], F32)

    with ExitStack() as X:
        sb = X.enter_context
        id32_s = sb(nc.sbuf_tensor("id32s", [128, 128], F32))
        idbf_s = sb(nc.sbuf_tensor("idbfs", [128, 128], BF16))
        ones_s = sb(nc.sbuf_tensor("oness", [1, 128], BF16))
        bias_s = sb(nc.sbuf_tensor("biass", [128, 128], F32))
        mstr_s = sb(nc.sbuf_tensor("mstrs", [128, 896], F32))
        bqkv_s = sb(nc.sbuf_tensor("bqkvs", [128, 6], F32))
        st = {k: sb(nc.sbuf_tensor("st_" + k, [128, 1], F32))
              for k in ("ssum", "ssq", "m", "mneg", "msq", "var", "r2",
                        "rstd", "nb")}

        p512 = [sb(nc.psum_tensor("p512a", [128, 512], F32)),
                sb(nc.psum_tensor("p512b", [128, 512], F32))]
        pT = [sb(nc.psum_tensor("pT0", [128, 128], F32)),
              sb(nc.psum_tensor("pT1", [128, 128], F32))]
        XPC = ExitStack()
        pc = [XPC.enter_context(nc.psum_tensor(f"pc{i}", [128, 132], F32))
              for i in range(4)]
        pT6 = [pT[0][:], pT[1][:]] + [pc[i][:, 0:128] for i in range(4)]
        assert nc.psum_banks_remaining >= 0
        assert nc.sbuf_bytes_remaining >= 0

        s_ld, s_x0, s_x1 = sems["s_ld"], sems["s_x0"], sems["s_x1"]
        s_cast, s_te, s_ve, s_sc = (sems["s_cast"], sems["s_te"],
                                    sems["s_ve"], sems["s_sc"])
        s_stash, s_qb, s_kv = sems["s_stash"], sems["s_qb"], sems["s_kv"]
        s_pd, s_cc, s_out, s_bar = (sems["s_pd"], sems["s_cc"],
                                    sems["s_out"], sems["s_bar"])
        s_gp = sems["s_gp"]
        s_x = [s_x0, s_x1]

        XA = ExitStack()
        sa = XA.enter_context
        wf32 = sa(nc.sbuf_tensor("wf32", [128, 4096], F32))
        wh = sa(nc.sbuf_tensor("wh", [128, 3 * 4096], BF16))
        wl = sa(nc.sbuf_tensor("wl", [128, 3 * 4096], BF16))
        xb = sa(nc.sbuf_tensor("xb", [128, 2 * D], F32))
        xntm = sa(nc.sbuf_tensor("xntm", [128, D], F32))
        xnh = sa(nc.sbuf_tensor("xnh", [128, 2 * 16 * 512], BF16))
        xnl = sa(nc.sbuf_tensor("xnl", [128, 2 * 16 * 512], BF16))
        qtmp = sa(nc.sbuf_tensor("qtmp", [128, 2 * 512], F32))
        sth = sa(nc.sbuf_tensor("sth", [128, 2 * 512], BF16))
        stl = sa(nc.sbuf_tensor("stl", [128, 2 * 512], BF16))
        vsgh = sa(nc.sbuf_tensor("vsgh", [128, 4 * 132], BF16))
        vsgl = sa(nc.sbuf_tensor("vsgl", [128, 4 * 132], BF16))
        scr = xntm

        b.dma("sync", id32_s[:], id32.ap(), s_ld)
        b.dma("sync", idbf_s[:], idbf.ap(), s_ld)
        b.dma("sync", ones_s[:], onesbf.ap(), s_ld)
        b.dma("sync", bias_s[:], biask.ap(), s_ld)
        b.dma("sync", mstr_s[:], mstrip.ap(), s_ld)
        b.dma("sync", bqkv_s[:], bqkv.ap(), s_ld)
        for i3, w3 in enumerate((wq, wk, wv)):
            sl = slice(4096 * i3, 4096 * (i3 + 1))
            b.dma("sync", wf32[:], w3.ap(), s_ld)
            b.wait("gpsimd", s_ld)
            b.dma("gpsimd", wh[:, sl], wf32[:], s_cast)
            b.wait("vector", s_cast)
            b.tt(wl[:, sl], wf32[:], wh[:, sl], ALU.subtract, sem=s_ve)
            b.wait("sync", s_ve)
        b.memset(vsgl[:], 0.0)
        b.memset(vsgh[:], 0.0)
        for i in range(4):
            b.memset(vsgh[:, 132 * i + 128:132 * i + 129], 1.0)
        b.barrier(s_bar)

        _ntile = {'prep': 0, 'A1t': 1, 'A1': 4, 'A2': 8}.get(stop_after, S // 128)
        for t in range(_ntile):
            blk, stn = t // 4, t % 4
            p = t % 2
            xt = xb[:, D * p:D * (p + 1)]
            b.wait_release("sync", ("xb", p))
            b.dma("sync", xt, x.ap()[128 * t:128 * (t + 1), :], s_x[p])
            b.wait("vector", s_x[p])
            b.wait("scalar", s_x[p])
            emit_ln_tile(b, xt, scr, xntm, st, s_ve, s_sc)
            b.release(("xb", p), s_sc)
            xno = 8192 * (blk % 2)
            if stn == 0:
                b.wait_release("scalar", ("xn", blk % 2))
                b.wait_release("vector", ("xn", blk % 2))
            if DBG.get("dump_ln") and t == DBG.get("dump_ln_tile", 0):
                b.wait("sync", s_sc)
                for ki, kk in enumerate(("ssum", "ssq", "m", "var", "r2",
                                         "rstd", "nb")):
                    b.op("sync", lambda E, o=ln_dump.ap()[:, ki:ki + 1],
                         i=st[kk][:]: E.dma_start(
                             out=o, in_=i, allow_slow_non_contiguous=True),
                         sems["s_out"], 16)
                b.dma("sync", xn_dump.ap(), xntm[:], sems["s_out"])
                b.wait("vector", sems["s_out"])
                b.wait("scalar", sems["s_out"])
                b.wait("tensor", sems["s_out"])
            for j in range(16):
                r = j % 6
                tp = pT6[r]
                b.wait_release("tensor", ("pT", r))
                b.wait_release("tensor", ("xnt",))
                b.tr(tp, xntm[:, 128 * j:128 * (j + 1)], id32_s[:], s_te)
                col = xno + 512 * j + 128 * stn
                b.wait("scalar", s_te)
                b.cp("scalar", xnh[:, col:col + 128], tp, s_sc)
                b.wait("vector", s_sc)
                b.tt(xnl[:, col:col + 128], tp, xnh[:, col:col + 128],
                     ALU.subtract, sem=s_ve)
                b.release(("pT", r), s_ve)
            b.release(("xntm",), s_te)
            b.wait_release("scalar", ("xntm",))
            if DBG.get("dump_ln") and t == 3:
                b.wait("sync", s_ve)
                b.dma("sync", xnh_dump.ap(), xnh[:], sems["s_out"])
                b.dma("sync", xnl_dump.ap(), xnl[:], sems["s_out"])
            if stn != 3:
                continue
            for m3 in range(3):
                for oc in range(2):
                    if m3 * 2 + oc >= DBG["qkv_n"]:
                        continue
                    g = (m3 * 2 + oc) % 2
                    ps = p512[g]
                    b.wait_release("tensor", ("p512", g))
                    b.wait("tensor", s_ve)
                    nmj = DBG.get("qkv_mj", 16)
                    npass = DBG.get("qkv_pass", 3)
                    for mj in range(nmj):
                        wc = slice(4096 * m3 + 256 * mj + 128 * oc,
                                   4096 * m3 + 256 * mj + 128 * (oc + 1))
                        xc = slice(xno + 512 * mj, xno + 512 * (mj + 1))
                        b.mm(ps[:], wh[:, wc], xnh[:, xc], mj == 0,
                             npass == 1 and mj == nmj - 1)
                        if npass >= 2:
                            b.mm(ps[:], wh[:, wc], xnl[:, xc], False,
                                 npass == 2 and mj == nmj - 1)
                        if npass >= 3:
                            b.mm(ps[:], wl[:, wc], xnh[:, xc], False,
                                 mj == nmj - 1)
                    b.inc_last("tensor", s_te)
                    if m3 == 2 and oc == 1:
                        b.release(("xn", blk % 2), s_te)
                    qcol = slice(512 * g, 512 * (g + 1))
                    b.wait_release("scalar", ("stage", g))
                    b.wait("scalar", s_te)
                    b.act(qtmp[:, qcol], ps[:], AF.Identity,
                          bias=bqkv_s[:, 2 * m3 + oc:2 * m3 + oc + 1],
                          sem=s_sc)
                    b.release(("p512", g), s_sc)
                    b.wait("scalar", s_sc)
                    b.cp("scalar", sth[:, qcol], qtmp[:, qcol], s_sc)
                    b.wait("gpsimd", s_sc)
                    b.tt(stl[:, qcol], qtmp[:, qcol], sth[:, qcol],
                         ALU.subtract, sem=s_gp, eng="gpsimd")
                    if m3 < 2:
                        dd = qd if m3 == 0 else kd
                        tsl = slice(512 * blk, 512 * (blk + 1))
                        b.wait("sync", s_gp)
                        b.dma("sync", dd.ap()[128 * oc:128 * (oc + 1), tsl],
                              sth[:, qcol], s_stash)
                        b.dma("sync",
                              dd.ap()[256 + 128 * oc:256 + 128 * (oc + 1),
                                      tsl],
                              stl[:, qcol], s_stash)
                        b.release(("stage", g), s_stash)
                    else:
                        b.wait_release("scalar", ("vsg",))
                        b.wait_release("vector", ("vsg",))
                        for i in range(4):
                            u = i % 2
                            tp = pT[u]
                            b.wait_release("tensor", ("pT", u))
                            b.wait("tensor", s_sc)
                            b.tr(tp[:],
                                 qtmp[:, 512 * g + 128 * i:
                                      512 * g + 128 * (i + 1)],
                                 id32_s[:], s_te)
                            b.wait("scalar", s_te)
                            b.cp("scalar", vsgh[:, 132 * i:132 * i + 128],
                                 tp[:], s_sc)
                            b.wait("vector", s_sc)
                            b.tt(vsgl[:, 132 * i:132 * i + 128], tp[:],
                                 vsgh[:, 132 * i:132 * i + 128],
                                 ALU.subtract, sem=s_ve)
                            b.release(("pT", u), s_ve)
                        b.wait("sync", s_ve)
                        b.wait("sync", s_sc)
                        vsl = slice(132 * 4 * blk, 132 * 4 * (blk + 1))
                        b.dma("sync", vd.ap()[128 * oc:128 * (oc + 1), vsl],
                              vsgh[:], s_stash)
                        b.dma("sync",
                              vd.ap()[256 + 128 * oc:256 + 128 * (oc + 1),
                                      vsl],
                              vsgl[:], s_stash)
                        b.release(("stage", g), s_stash)
                        b.release(("vsg",), s_stash)

        b.barrier(s_bar)
        if DBG.get("dump_mid"):
            b.dma("gpsimd", qd_dump.ap(), qd.ap(), sems["s_out"])
            b.dma("gpsimd", kd_dump.ap(), kd.ap(), sems["s_out"])
            b.dma("gpsimd", vd_dump.ap(), vd.ap(), sems["s_out"])
            b.wait("gpsimd", sems["s_out"])
        XA.close()
        ctxh = sb(nc.sbuf_tensor("ctxh", [128, 2 * S], BF16))
        ctxl = sb(nc.sbuf_tensor("ctxl", [128, 2 * S], BF16))
        XC = ExitStack()
        sc_ = XC.enter_context
        kh = sc_(nc.sbuf_tensor("khb", [128, S], BF16))
        kl = sc_(nc.sbuf_tensor("klb", [128, S], BF16))
        vh = sc_(nc.sbuf_tensor("vhb", [128, 64 * 132], BF16))
        vl = sc_(nc.sbuf_tensor("vlb", [128, 64 * 132], BF16))
        qhb = sc_(nc.sbuf_tensor("qhb", [128, 2 * 512], BF16))
        qlb = sc_(nc.sbuf_tensor("qlb", [128, 2 * 512], BF16))
        ptb = sc_(nc.sbuf_tensor("ptb", [128, 2 * 512], F32))
        phb = sc_(nc.sbuf_tensor("phb", [128, 2 * 512], BF16))
        plb = sc_(nc.sbuf_tensor("plb", [128, 2 * 512], BF16))
        ctmp = sc_(nc.sbuf_tensor("ctmp", [128, 2 * 128], F32))
        linv = sc_(nc.sbuf_tensor("linv", [128, 2], F32))
        mrow_s = sc_(nc.sbuf_tensor("mrows", [1, S], BF16))

        for hh in range(DBG.get('c_heads', 2) if stop_after in (None, 'C', 'Dnc') else 0):
            b.wait_release("sync", ("kvbuf",))
            b.dma("sync", kh[:], kd.ap()[128 * hh:128 * (hh + 1), :], s_kv)
            b.dma("sync", kl[:], kd.ap()[256 + 128 * hh:256 + 128 * (hh + 1),
                                         :], s_kv)
            b.dma("sync", vh[:], vd.ap()[128 * hh:128 * (hh + 1), :], s_kv)
            b.dma("sync", vl[:], vd.ap()[256 + 128 * hh:256 + 128 * (hh + 1),
                                         :], s_kv)
            b.dma("sync", mrow_s[:], mrow.ap()[hh:hh + 1, :], s_kv)
            b.wait("tensor", s_kv)
            for qb in range(DBG.get('c_qb', NBLK)):
                g2 = qb % 2
                qsl = slice(512 * g2, 512 * (g2 + 1))
                b.wait_release("sync", ("qblk", g2))
                b.dma("sync", qhb[:, qsl],
                      qd.ap()[128 * hh:128 * (hh + 1),
                              512 * qb:512 * (qb + 1)], s_qb)
                b.dma("sync", qlb[:, qsl],
                      qd.ap()[256 + 128 * hh:256 + 128 * (hh + 1),
                              512 * qb:512 * (qb + 1)], s_qb)
                b.wait("tensor", s_qb)
                b.wait_release("tensor", ("pcx",))
                nkt = 4 * qb + 4

                def emit_scores(kt, qb=qb, qsl=qsl, hh=hh):
                    u = kt % 2
                    sp = p512[u]
                    b.wait_release("tensor", ("sp", u))
                    ksl = slice(128 * kt, 128 * (kt + 1))
                    b.mm(sp[:], kh[:, ksl], qhb[:, qsl], True, False)
                    b.mm(sp[:], kh[:, ksl], qlb[:, qsl], False, False)
                    b.mm(sp[:], kl[:, ksl], qhb[:, qsl], False, False)
                    b.mm(sp[:], ones_s[:],
                         mrow_s[0:1, 512 * qb:512 * (qb + 1)], False, True,
                         sem=s_te)
                    if kt >= 4 * qb:
                        off = 384 - (128 * kt - 512 * qb)
                        b.wait("vector", s_te)
                        b.tt(sp[:], sp[:], mstr_s[:, off:off + 512], ALU.add,
                             sem=s_ve)
                        b.wait("scalar", s_ve)
                    b.wait_release("scalar", ("pt", u))
                    b.wait("scalar", s_te)
                    ptc = ptb[:, 512 * u:512 * (u + 1)]
                    b.act(ptc, sp[:], AF.Exp,
                          bias=bias_s[:, 2 * kt + hh:2 * kt + hh + 1],
                          scale=SCALE, sem=s_sc)
                    b.release(("sp", u), s_sc)
                    b.wait("scalar", s_sc)
                    phc = phb[:, 512 * u:512 * (u + 1)]
                    plc = plb[:, 512 * u:512 * (u + 1)]
                    b.wait_release("scalar", ("ph", u))
                    b.cp("scalar", phc, ptc, s_sc)
                    b.wait("vector", s_sc)
                    b.tt(plc, ptc, phc, ALU.subtract, sem=s_ve)
                    b.release(("pt", u), s_ve)
                    b.release(("phrdy", u), s_ve)

                def emit_pv(kt, nkt=nkt):
                    u = kt % 2
                    b.wait_release("tensor", ("phrdy", u))
                    vhc = vh[:, 132 * kt:132 * kt + 129]
                    vlc = vl[:, 132 * kt:132 * kt + 129]
                    for sq in range(4):
                        phs = phb[:, 512 * u + 128 * sq:
                                  512 * u + 128 * (sq + 1)]
                        pls = plb[:, 512 * u + 128 * sq:
                                  512 * u + 128 * (sq + 1)]
                        b.mm(pc[sq][:, 0:129], phs, vhc, kt == 0, False)
                        b.mm(pc[sq][:, 0:129], phs, vlc, False, False)
                        b.mm(pc[sq][:, 0:129], pls, vhc, False,
                             kt == nkt - 1)
                    b.inc_last("tensor", s_te)
                    b.release(("ph", u), s_te)

                for kt in range(nkt):
                    emit_scores(kt)
                    if kt >= 1:
                        emit_pv(kt - 1)
                emit_pv(nkt - 1)
                b.wait("vector", s_te)
                for sq in range(4):
                    b.recip(linv[:, sq % 2:sq % 2 + 1], pc[sq][:, 128:129],
                            sem=s_ve)
                    b.wait("vector", s_ve)
                    cts = ctmp[:, 128 * (sq % 2):128 * (sq % 2 + 1)]
                    b.wait_release("vector", ("ctmp", sq % 2))
                    b.ts(cts, pc[sq][:, 0:128], linv[:, sq % 2:sq % 2 + 1],
                         ALU.mult, sem=s_ve)
                    tp = pT[sq % 2]
                    b.wait_release("tensor", ("pT", sq % 2))
                    b.wait("tensor", s_ve)
                    b.tr(tp[:], cts, id32_s[:], s_te)
                    b.release(("ctmp", sq % 2), s_te)
                    b.wait("scalar", s_te)
                    ccol = S * hh + 512 * qb + 128 * sq
                    b.cp("scalar", ctxh[:, ccol:ccol + 128], tp[:], s_sc)
                    b.wait("vector", s_sc)
                    b.tt(ctxl[:, ccol:ccol + 128], tp[:],
                         ctxh[:, ccol:ccol + 128], ALU.subtract, sem=s_ve)
                    b.release(("pT", sq % 2), s_ve)
                b.release(("pcx",), s_ve)
                b.release(("qblk", g2), s_te)
            b.release(("kvbuf",), s_sc)

        b.barrier(s_bar)
        XC.close()
        XPC.close()
        XD = ExitStack()
        sd = XD.enter_context
        wof = sd(nc.sbuf_tensor("wof", [128, 2 * D], F32))
        woh = sd(nc.sbuf_tensor("woh", [128, 2 * D], BF16))
        wol = sd(nc.sbuf_tensor("wol", [128, 2 * D], BF16))
        ptm = sd(nc.sbuf_tensor("ptm", [128, 4 * 512], F32))
        pdx = [p512[0], p512[1],
               sd(nc.psum_tensor("pd2", [128, 512], F32)),
               sd(nc.psum_tensor("pd3", [128, 512], F32))]

        b.dma("sync", wof[:], wo.ap(), s_ld)
        b.wait("gpsimd", s_ld)
        b.dma("gpsimd", woh[:], wof[:], s_cast)
        b.wait("vector", s_cast)
        b.tt(wol[:], wof[:], woh[:], ALU.subtract, sem=s_ve)
        b.wait("tensor", s_ve)
        for blk in range(DBG.get('d_nblk', NBLK)):
            sh, slot = shard_slot(blk)
            for oc in range(16):
                g = (blk * 16 + oc) % 4
                ps = pdx[g]
                b.wait_release("tensor", ("p512d", g))
                for dh in range(2):
                    wcs = slice(2048 * dh + 128 * oc,
                                2048 * dh + 128 * (oc + 1))
                    csl = slice(S * dh + 512 * blk, S * dh + 512 * (blk + 1))
                    b.mm(ps[:], woh[:, wcs], ctxh[:, csl], dh == 0, False)
                    b.mm(ps[:], woh[:, wcs], ctxl[:, csl], False, False)
                    b.mm(ps[:], wol[:, wcs], ctxh[:, csl], False, dh == 1)
                b.inc_last("tensor", s_te)
                b.wait_release("vector", ("ptm", g))
                b.wait("vector", s_te)
                b.cp("vector", ptm[:, 512 * g:512 * (g + 1)], ps[:])
                b.inc_last("vector", s_ve)
                b.release(("p512d", g), s_ve)
                prow = (2 * sh + slot) * D + 128 * oc
                dq, ds_ = (("sync", s_pd) if g % 2 == 0 else
                           ("scalar", s_stash))
                b.wait(dq, s_ve)
                b.dma(dq, part_d.ap()[prow:prow + 128, :],
                      ptm[:, 512 * g:512 * (g + 1)], ds_)
                b.release(("ptm", g), ds_)
        XD.close()
        b.barrier(s_bar)
        b.op("gpsimd", lambda E: E.collective_compute(
            "ReduceScatter", ALU.add,
            replica_groups=[list(range(NCOR))],
            ins=[part_d.ap()], outs=[rs_d.ap()]), sems["s_cc"])
        b.nopinc("gpsimd", sems["s_np"])
    return rs_d



def emit_p2(nc, b, sems, id32, rs_d):
    x_own = nc.dram_tensor("x_own", [TOK, D], F32, kind="ExternalInput")
    bo_r = nc.dram_tensor("bo_r", [128, 16], F32, kind="ExternalInput")
    gwT = nc.dram_tensor("gwT", [IC0 * 128, D], F32, kind="ExternalInput")
    uwT = nc.dram_tensor("uwT", [IC0 * 128, D], F32, kind="ExternalInput")
    dwT = nc.dram_tensor("dwT", [16 * 128, I0], F32, kind="ExternalInput")
    gsc = nc.dram_tensor("gsc", [128, IC0], F32, kind="ExternalInput")
    usc = nc.dram_tensor("usc", [128, IC0], F32, kind="ExternalInput")
    dsc = nc.dram_tensor("dsc", [128, 16], F32, kind="ExternalInput")
    vwT = nc.dram_tensor("vwT", [4 * 128, D], F32, kind="ExternalInput")
    woT1 = nc.dram_tensor("woT1", [16 * 128, D], F32, kind="ExternalInput")
    gwT1 = nc.dram_tensor("gwT1", [128, 128], F32, kind="ExternalInput")
    w1 = nc.dram_tensor("w1", [E * HC * 128, D], F32, kind="ExternalInput")
    w2 = nc.dram_tensor("w2", [E * 16 * 128, HC * 128], F32,
                        kind="ExternalInput")
    w3 = nc.dram_tensor("w3", [E * HC * 128, D], F32, kind="ExternalInput")
    sw1 = nc.dram_tensor("sw1", [HC * 128, D], F32, kind="ExternalInput")
    sw2 = nc.dram_tensor("sw2", [16 * 128, HC * 128], F32,
                         kind="ExternalInput")
    sw3 = nc.dram_tensor("sw3", [HC * 128, D], F32, kind="ExternalInput")
    ones32 = nc.dram_tensor("ones32", [1, 128], F32, kind="ExternalInput")

    y = nc.dram_tensor("y", [TOK, D], F32, kind="ExternalOutput")

    with ExitStack() as X:
        sb = X.enter_context
        id32_s = sb(nc.sbuf_tensor("id32sB", [128, 128], F32))
        ones_s = sb(nc.sbuf_tensor("onessB", [1, 128], F32))
        bo_s = sb(nc.sbuf_tensor("bos", [128, 16], F32))
        gsc_s = sb(nc.sbuf_tensor("gscs", [128, IC0], F32))
        usc_s = sb(nc.sbuf_tensor("uscs", [128, IC0], F32))
        dsc_s = sb(nc.sbuf_tensor("dscs", [128, 16], F32))

        resid = sb(nc.sbuf_tensor("resid", [128, 16 * TOK], F32))
        xnh = sb(nc.sbuf_tensor("xnhB", [128, 16 * TOK], BF16))
        xnl = sb(nc.sbuf_tensor("xnlB", [128, 16 * TOK], BF16))
        st = {k: sb(nc.sbuf_tensor("stB_" + k, [128, 1], F32))
              for k in ("ssum", "ssq", "m", "mneg", "msq", "var", "r2",
                        "rstd", "nb")}

        pA = sb(nc.psum_tensor("pA", [128, 512], F32))
        pB = sb(nc.psum_tensor("pB", [128, 512], F32))
        pC = sb(nc.psum_tensor("pC", [128, 512], F32))
        pD = sb(nc.psum_tensor("pD", [128, 512], F32))
        pE = sb(nc.psum_tensor("pE", [128, 512], F32))
        pF = sb(nc.psum_tensor("pF", [128, 512], F32))
        pT = [sb(nc.psum_tensor("pU0", [128, 128], F32)),
              sb(nc.psum_tensor("pU1", [128, 128], F32))]
        pG = pT[1][:, 0:8]
        pT6 = [pT[0][:], pT[1][:], pA[:, 0:128], pB[:, 0:128],
               pC[:, 0:128], pD[:, 0:128]]
        assert nc.psum_banks_remaining >= 0
        assert nc.sbuf_bytes_remaining >= 0

        s_ld, s_x0, s_x1 = sems["s_ld"], sems["s_x0"], sems["s_x1"]
        s_cast, s_te, s_ve, s_sc = (sems["s_cast"], sems["s_te"],
                                    sems["s_ve"], sems["s_sc"])
        s_w, s_out, s_bar = sems["s_w"], sems["s_out"], sems["s_bar"]
        s_gp = sems["s_gp"]
        s_x = [s_x0, s_x1]

        b.dma("sync", id32_s[:], id32.ap(), s_ld)
        b.dma("sync", ones_s[:], ones32.ap(), s_ld)
        b.dma("sync", bo_s[:], bo_r.ap(), s_ld)
        b.dma("sync", gsc_s[:], gsc.ap(), s_ld)
        b.dma("sync", usc_s[:], usc.ap(), s_ld)
        b.dma("sync", dsc_s[:], dsc.ap(), s_ld)
        for e_ in ENGINES:
            b.wait(e_, s_ld)
        b.barrier(s_bar)

        X1 = ExitStack()
        s1 = X1.enter_context
        xb = s1(nc.sbuf_tensor("xbB", [128, 2 * D], F32))
        xntm = s1(nc.sbuf_tensor("xntmB", [128, D], F32))
        scr = xntm
        for t in range(8):
            p = t % 2
            xt = xb[:, D * p:D * (p + 1)]
            b.wait_release("sync", ("xb", p))
            b.dma("sync", xt, x_own.ap()[128 * t:128 * (t + 1), :], s_x[p])
            b.wait("vector", s_x[p])
            b.wait("scalar", s_x[p])
            emit_ln_tile(b, xt, scr, xntm, st, s_ve, s_sc)
            for j in range(16):
                col = TOK * j + 128 * t
                r0, r1 = (2 * j) % 6, (2 * j + 1) % 6
                tp0 = pT6[r0]
                b.wait_release("tensor", ("pT", r0))
                b.wait("tensor", s_x[p])
                b.tr(tp0, xt[:, 128 * j:128 * (j + 1)], id32_s[:], s_te)
                b.wait("vector", s_te)
                b.cp("vector", resid[:, col:col + 128], tp0)
                b.inc_last("vector", s_ve)
                b.release(("pT", r0), s_ve)
                tp1 = pT6[r1]
                b.wait_release("tensor", ("pT", r1))
                b.wait_release("tensor", ("xnt",))
                b.tr(tp1, xntm[:, 128 * j:128 * (j + 1)], id32_s[:], s_te)
                b.wait("scalar", s_te)
                b.cp("scalar", xnh[:, col:col + 128], tp1, s_sc)
                b.wait("vector", s_sc)
                b.tt(xnl[:, col:col + 128], tp1, xnh[:, col:col + 128],
                     ALU.subtract, sem=s_ve)
                b.release(("pT", r1), s_ve)
            b.release(("xb", p), s_te)
            b.release(("xntm",), s_te)
            b.wait_release("scalar", ("xntm",))
        b.barrier(s_bar)
        X1.close()

        X2 = ExitStack()
        s2_ = X2.enter_context
        huh = s2_(nc.sbuf_tensor("huh", [128, IC0 * 256], BF16))
        hul = s2_(nc.sbuf_tensor("hul", [128, IC0 * 256], BF16))
        gact = s2_(nc.sbuf_tensor("gact", [128, 256], F32))
        uact = s2_(nc.sbuf_tensor("uact", [128, 256], F32))
        humul = s2_(nc.sbuf_tensor("humul", [128, 256], F32))
        wbh = s2_(nc.sbuf_tensor("wbh", [128, 2 * 2048], BF16))
        wbh2 = s2_(nc.sbuf_tensor("wbh2", [128, 2 * 2048], BF16))
        dwhH = [s2_(nc.sbuf_tensor("dwhA", [128, 22 * 128], BF16)),
                s2_(nc.sbuf_tensor("dwhB", [128, 22 * 128], BF16))]
        for qt in range(4):
            for ic in range(IC0):
                g = ic % 2
                wgh = wbh[:, 2048 * g:2048 * (g + 1)]
                wuh = wbh2[:, 2048 * g:2048 * (g + 1)]
                b.wait_release("gpsimd", ("fw", g))
                b.dma("gpsimd", wgh,
                      gwT.ap()[128 * ic:128 * (ic + 1), :], s_cast)
                b.dma("gpsimd", wuh,
                      uwT.ap()[128 * ic:128 * (ic + 1), :], s_cast)
                pa = (pA, pE)[g]
                pb = (pB, pF)[g]
                b.wait("tensor", s_cast)
                b.wait_release("tensor", ("pAB", g))
                b.wait_release("tensor", ("pAB_sc", g))
                for mj in range(16):
                    xsl = slice(TOK * mj + 256 * qt,
                                TOK * mj + 256 * (qt + 1))
                    wc = slice(128 * mj, 128 * (mj + 1))
                    b.mm(pa[:, 0:256], wgh[:, wc], xnh[:, xsl], mj == 0,
                         False)
                    b.mm(pa[:, 0:256], wgh[:, wc], xnl[:, xsl], False,
                         mj == 15)
                for mj in range(16):
                    xsl = slice(TOK * mj + 256 * qt,
                                TOK * mj + 256 * (qt + 1))
                    wc = slice(128 * mj, 128 * (mj + 1))
                    b.mm(pb[:, 0:256], wuh[:, wc], xnh[:, xsl], mj == 0,
                         False)
                    b.mm(pb[:, 0:256], wuh[:, wc], xnl[:, xsl], False,
                         mj == 15)
                b.inc_last("tensor", s_te)
                b.release(("fw", g), s_te)
                b.wait("scalar", s_te)
                b.wait_release("scalar", ("gact",))
                b.act(gact[:], pa[:, 0:256], AF.Silu,
                      scale=gsc_s[:, ic:ic + 1], sem=s_sc)
                b.release(("pAB_sc", g), s_sc)
                b.wait("vector", s_te)
                b.wait_release("vector", ("uact",))
                b.ts(uact[:], pb[:, 0:256], usc_s[:, ic:ic + 1], ALU.mult,
                     sem=s_ve)
                b.release(("pAB", g), s_ve)
                b.wait("vector", s_ve)
                b.wait("vector", s_sc)
                b.wait_release("vector", ("hm2",))
                b.tt(humul[:], gact[:], uact[:], ALU.mult, sem=s_ve)
                b.release(("gact",), s_ve)
                b.release(("uact",), s_ve)
                hucol = 256 * ic
                b.wait("scalar", s_ve)
                b.cp("scalar", huh[:, hucol:hucol + 256], humul[:], s_sc)
                b.release(("hm2",), s_sc)
                b.wait("vector", s_sc)
                b.tt(hul[:, hucol:hucol + 256], humul[:],
                     huh[:, hucol:hucol + 256], ALU.subtract, sem=s_ve)
            for oc in range(16):
                for h in (0, 1):
                    b.wait_release("gpsimd", ("dw", h))
                    b.dma("gpsimd", dwhH[h][:],
                          dwT.ap()[128 * oc:128 * (oc + 1),
                                   2816 * h:2816 * (h + 1)], s_cast)
                pdn = (pC, pD)[oc % 2]
                b.wait("tensor", s_cast)
                b.wait("tensor", s_ve)
                b.wait("tensor", s_sc)
                b.wait_release("tensor", ("pC", oc % 2))
                for ij in range(IC0):
                    h, ijr = divmod(ij, 22)
                    wc = slice(128 * ijr, 128 * (ijr + 1))
                    b.mm(pdn[:, 0:256], dwhH[h][:, wc],
                         huh[:, 256 * ij:256 * (ij + 1)], ij == 0, False)
                    b.mm(pdn[:, 0:256], dwhH[h][:, wc],
                         hul[:, 256 * ij:256 * (ij + 1)], False,
                         ij == IC0 - 1)
                    if ij == 21:
                        b.inc_last("tensor", s_te)
                        b.release(("dw", 0), s_te)
                b.inc_last("tensor", s_te)
                b.release(("dw", 1), s_te)
                b.wait("vector", s_te)
                rsl = resid[:, TOK * oc + 256 * qt:TOK * oc + 256 * (qt + 1)]
                b.stt(rsl, pdn[:, 0:256], dsc_s[:, oc:oc + 1], rsl, ALU.mult,
                      ALU.add, sem=s_ve)
                b.release(("pC", oc % 2), s_ve)
        b.barrier(s_bar)
        X2.close()

        X3 = ExitStack()
        abuf = X3.enter_context(nc.sbuf_tensor("abuf", [128, TOK], F32))
        b.wait("sync", sems["s_cc"], 1)
        for fc in range(16):
            b.wait_release("sync", ("abuf",))
            b.dma("sync", abuf[:, 0:512],
                  rs_d.ap()[128 * fc:128 * (fc + 1), :], s_x0)
            b.dma("sync", abuf[:, 512:1024],
                  rs_d.ap()[D + 128 * fc:D + 128 * (fc + 1), :], s_x0)
            b.wait("vector", s_x0)
            rsl = resid[:, TOK * fc:TOK * (fc + 1)]
            b.stt(rsl, abuf[:], bo_s[:, fc:fc + 1], rsl, ALU.add, ALU.add,
                  sem=s_ve)
            b.release(("abuf",), s_ve)
        b.barrier(s_bar)
        X3.close()

        X4 = ExitStack()
        s4 = X4.enter_context
        xb = s4(nc.sbuf_tensor("xb4", [128, D], F32))
        xntm = s4(nc.sbuf_tensor("xntm4", [128, D], F32))
        scr = xntm
        for t in range(8):
            xt = xb[:]
            b.wait_release("vector", ("xbD4",))
            for j in range(16):
                r = j % 6
                tp = pT6[r]
                b.wait_release("tensor", ("pT", r))
                b.tr(tp,
                     resid[:, TOK * j + 128 * t:TOK * j + 128 * (t + 1)],
                     id32_s[:], s_te)
                b.wait("vector", s_te)
                b.cp("vector", xt[:, 128 * j:128 * (j + 1)], tp)
                b.inc_last("vector", s_ve)
                b.release(("pT", r), s_ve)
            b.wait("vector", s_ve)
            b.wait("scalar", s_ve)
            emit_ln_tile(b, xt, scr, xntm, st, s_ve, s_sc)
            for j in range(16):
                col = TOK * j + 128 * t
                r = j % 6
                tp = pT6[r]
                b.wait_release("tensor", ("pT", r))
                b.wait_release("tensor", ("xnt",))
                b.tr(tp, xntm[:, 128 * j:128 * (j + 1)], id32_s[:], s_te)
                b.wait("scalar", s_te)
                b.cp("scalar", xnh[:, col:col + 128], tp, s_sc)
                b.wait("vector", s_sc)
                b.tt(xnl[:, col:col + 128], tp, xnh[:, col:col + 128],
                     ALU.subtract, sem=s_ve)
                b.release(("pT", r), s_ve)
            b.release(("xbD4",), s_sc)
            b.release(("xntm",), s_te)
            b.wait_release("scalar", ("xntm",))
        b.barrier(s_bar)
        X4.close()

        X5 = ExitStack()
        s5 = X5.enter_context
        wbf = s5(nc.sbuf_tensor("wbf5", [128, 2 * 2048], F32))
        wpr = [(s5(nc.sbuf_tensor("wbh5", [128, 2048], BF16)),
                s5(nc.sbuf_tensor("wbl5", [128, 2048], BF16))),
               (s5(nc.sbuf_tensor("wbh25", [128, 2048], BF16)),
                s5(nc.sbuf_tensor("wbl25", [128, 2048], BF16)))]
        PWH = [(pA, pB), (pE, pF)]

        def wchunk2(col_ap, ncols, pair, hb=None, lb=None):
            hbuf, lbuf = wpr[pair] if hb is None else (hb, lb)
            wsl = wbf[:, 2048 * pair:2048 * pair + ncols]
            b.wait_release("sync", ("wbf", pair))
            b.dma("sync", wsl, col_ap, s_w)
            if hb is None:
                b.wait_release("gpsimd", ("wpair", pair))
            b.dma("gpsimd", hbuf[:, 0:ncols], col_ap, s_cast)
            b.wait("gpsimd", s_w)
            b.wait("gpsimd", s_cast)
            b.tt(lbuf[:, 0:ncols], wsl, hbuf[:, 0:ncols], ALU.subtract,
                 sem=s_gp, eng="gpsimd")
            b.release(("wbf", pair), s_gp)

        def mm3(ps, whb, wlb, xh_of, xl_of, nmj):
            for mj in range(nmj):
                wc = slice(128 * mj, 128 * (mj + 1))
                xh, xl = xh_of(mj), xl_of(mj)
                b.mm(ps, whb[:, wc], xh, mj == 0, False)
                b.mm(ps, whb[:, wc], xl, False, False)
                b.mm(ps, wlb[:, wc], xh, False, mj == nmj - 1)

        def xn_of(half):
            return (lambda mj, h=half: xnh[:, TOK * mj + 512 * h:
                                           TOK * mj + 512 * (h + 1)],
                    lambda mj, h=half: xnl[:, TOK * mj + 512 * h:
                                           TOK * mj + 512 * (h + 1)])

        X5b = ExitStack()
        s5b = X5b.enter_context
        v1h = s5b(nc.sbuf_tensor("v1h", [128, 4 * TOK], BF16))
        v1l = s5b(nc.sbuf_tensor("v1l", [128, 4 * TOK], BF16))
        for vc in range(4):
            pr = vc % 2
            wchunk2(vwT.ap()[128 * vc:128 * (vc + 1), :], 2048, pr)
            whb, wlb = wpr[pr]
            b.wait("tensor", s_gp)
            for half in range(2):
                ps = PWH[pr][half]
                b.wait_release("tensor", ("pw", pr, half))
                fh, fl = xn_of(half)
                mm3(ps[:], whb, wlb, fh, fl, 16)
                b.inc_last("tensor", s_te)
                if half == 1:
                    b.release(("wpair", pr), s_te)
                col = TOK * vc + 512 * half
                b.wait("scalar", s_te)
                b.cp("scalar", v1h[:, col:col + 512], ps[:], s_sc)
                b.wait("vector", s_sc)
                b.tt(v1l[:, col:col + 512], ps[:], v1h[:, col:col + 512],
                     ALU.subtract, sem=s_ve)
                b.release(("pw", pr, half), s_ve)
        for oc in range(16):
            pr = oc % 2
            wchunk2(woT1.ap()[128 * oc:128 * (oc + 1), :], 2048, pr)
            whb, wlb = wpr[pr]
            b.wait("tensor", s_gp)
            for half in range(2):
                ps = PWH[pr][half]
                b.wait_release("tensor", ("pw", pr, half))
                mm3(ps[:], whb, wlb,
                    lambda mj, h=half: v1h[:, TOK * (mj // 4) + 512 * h:
                                           TOK * (mj // 4) + 512 * (h + 1)],
                    lambda mj, h=half: v1l[:, TOK * (mj // 4) + 512 * h:
                                           TOK * (mj // 4) + 512 * (h + 1)],
                    16)
                b.inc_last("tensor", s_te)
                if half == 1:
                    b.release(("wpair", pr), s_te)
                b.wait("vector", s_te)
                rsl = resid[:, TOK * oc + 512 * half:
                            TOK * oc + 512 * (half + 1)]
                b.tt(rsl, rsl, ps[:], ALU.add, sem=s_ve)
                b.release(("pw", pr, half), s_ve)
        b.barrier(s_bar)
        X5b.close()

        X6 = ExitStack()
        s6 = X6.enter_context
        comb = s6(nc.sbuf_tensor("comb", [8, TOK], F32))
        g1h = s6(nc.sbuf_tensor("g1h", [128, 16 * 8], BF16))
        g1l = s6(nc.sbuf_tensor("g1l", [128, 16 * 8], BF16))
        sg8 = s6(nc.sbuf_tensor("sg8", [128, 8], F32))
        t8a = s6(nc.sbuf_tensor("t8a", [128, 8], F32))
        t8b = s6(nc.sbuf_tensor("t8b", [128, 8], F32))
        m81 = s6(nc.sbuf_tensor("m81", [128, 1], F32))
        m82 = s6(nc.sbuf_tensor("m82", [128, 1], F32))
        wchunk2(gwT1.ap(), 128, 0, hb=g1h, lb=g1l)
        b.wait("tensor", s_gp)
        for t in range(8):
            b.wait("tensor", s_ve)
            b.wait_release("tensor", ("pg8",))
            for mj in range(16):
                xh = xnh[:, TOK * mj + 128 * t:TOK * mj + 128 * (t + 1)]
                xl = xnl[:, TOK * mj + 128 * t:TOK * mj + 128 * (t + 1)]
                gc = slice(8 * mj, 8 * (mj + 1))
                b.mm(pG, xh, g1h[:, gc], mj == 0, False)
                b.mm(pG, xl, g1h[:, gc], False, False)
                b.mm(pG, xh, g1l[:, gc], False, mj == 15)
            b.inc_last("tensor", s_te)
            b.wait("scalar", s_te)
            b.act(sg8[:], pG, AF.Sigmoid, sem=s_sc)
            b.wait("vector", s_sc)
            def _vs(fn):
                b.op("vector", fn, s_ve)
                b.wait("vector", s_ve)
            b.red(m81[:], sg8[:], ALU.max, sem=s_ve)
            b.wait("vector", s_ve)
            _vs(lambda E, o=t8a[:], i=sg8[:], s=m81[:]:
                E.tensor_scalar(out=o, in0=i, scalar1=s, scalar2=None,
                                op0=ALU.subtract))
            _vs(lambda E, o=t8a[:], i=t8a[:]:
                E.tensor_scalar(out=o, in0=i, scalar1=0.0, scalar2=None,
                                op0=ALU.is_ge))
            _vs(lambda E, o=t8b[:], i=t8a[:], j=sg8[:]:
                E.scalar_tensor_tensor(out=o, in0=i, scalar=-1e30, in1=j,
                                       op0=ALU.mult, op1=ALU.add))
            b.red(m82[:], t8b[:], ALU.max, sem=s_ve)
            b.wait("vector", s_ve)
            _vs(lambda E, o=t8a[:], i=sg8[:], s=m82[:]:
                E.tensor_scalar(out=o, in0=i, scalar1=s, scalar2=None,
                                op0=ALU.subtract))
            _vs(lambda E, o=t8a[:], i=t8a[:]:
                E.tensor_scalar(out=o, in0=i, scalar1=0.0, scalar2=None,
                                op0=ALU.is_ge))
            _vs(lambda E, o=t8a[:], i=t8a[:], j=sg8[:]:
                E.tensor_tensor(o, i, j, ALU.mult))
            _vs(lambda E, o=m82[:], i=m82[:], j=m81[:]:
                E.tensor_tensor(o, i, j, ALU.add))
            _vs(lambda E, o=m82[:], i=m82[:]:
                E.tensor_scalar(out=o, in0=i, scalar1=1e-20, scalar2=None,
                                op0=ALU.add))
            _vs(lambda E, o=m81[:], i=m82[:]: E.reciprocal(o, i))
            b.ts(t8b[:], t8a[:], m81[:], ALU.mult, s2=2.5, op1=ALU.mult,
                 sem=s_ve)
            b.release(("pg8",), s_ve)
            tp = pT[0]
            b.wait_release("tensor", ("pT", 0))
            b.wait("tensor", s_ve)
            b.tr(tp[0:8, :], t8b[:], id32_s[:], s_te)
            b.wait("vector", s_te)
            b.cp("vector", comb[0:8, 128 * t:128 * (t + 1)], tp[0:8, 0:128])
            b.inc_last("vector", s_ve)
            b.release(("pT", 0), s_ve)
        b.barrier(s_bar)

        X7 = ExitStack()
        s7 = X7.enter_context
        GRPS = [list(range(0, 6)), list(range(6, HC))]
        GMAX = max(len(g) for g in GRPS)
        huh = s7(nc.sbuf_tensor("huh7", [128, GMAX * TOK], BF16))
        hul = s7(nc.sbuf_tensor("hul7", [128, GMAX * TOK], BF16))
        gact = s7(nc.sbuf_tensor("gact7", [128, 512], F32))
        uact = s7(nc.sbuf_tensor("uact7", [128, 512], F32))
        humul = s7(nc.sbuf_tensor("humul7", [128, 512], F32))
        cbc = s7(nc.sbuf_tensor("cbc", [128, TOK], F32))
        crow = s7(nc.sbuf_tensor("crow", [1, TOK], F32))
        evb = s7(nc.sbuf_tensor("evb", [128, 512], F32))

        def silu_mul_split2(pa, pb, hucol, kpa, kpb):
            b.wait("scalar", s_te)
            b.wait_release("scalar", ("gact",))
            b.act(gact[:], pa[:], AF.Silu, sem=s_sc)
            b.release(kpa, s_sc)
            b.wait("vector", s_te)
            b.wait_release("vector", ("uact",))
            b.cp("vector", uact[:], pb[:], sem=s_ve)
            b.release(kpb, s_ve)
            b.wait("vector", s_ve)
            b.wait("vector", s_sc)
            b.wait_release("vector", ("humul",))
            b.tt(humul[:], gact[:], uact[:], ALU.mult, sem=s_ve)
            b.release(("gact",), s_ve)
            b.release(("uact",), s_ve)
            b.wait("scalar", s_ve)
            b.wait_release("scalar", ("hu",))
            b.cp("scalar", huh[:, hucol:hucol + 512], humul[:], s_sc)
            b.release(("humul",), s_sc)
            b.wait("vector", s_sc)
            b.wait_release("vector", ("hu",))
            b.tt(hul[:, hucol:hucol + 512], humul[:],
                 huh[:, hucol:hucol + 512], ALU.subtract, sem=s_ve)

        def expert_hu(w1ap, w3ap, grp):
            for i_hc, hc in enumerate(grp):
                par = i_hc % 2
                wchunk2(w1ap(hc), 2048, 0)
                whb, wlb = wpr[0]
                b.wait("tensor", s_gp)
                for half in range(2):
                    b.wait_release("tensor", ("pw", par, half))
                    fh, fl = xn_of(half)
                    mm3(PWH[par][half][:], whb, wlb, fh, fl, 16)
                    b.inc_last("tensor", s_te)
                b.release(("wpair", 0), s_te)
                wchunk2(w3ap(hc), 2048, 1)
                whb2, wlb2 = wpr[1]
                b.wait("tensor", s_gp)
                for half in range(2):
                    b.wait_release("tensor", ("pw3", half))
                    fh, fl = xn_of(half)
                    mm3((pC, pD)[half][:], whb2, wlb2, fh, fl, 16)
                    b.inc_last("tensor", s_te)
                b.release(("wpair", 1), s_te)
                for half in range(2):
                    silu_mul_split2(PWH[par][half], (pC, pD)[half],
                                    TOK * i_hc + 512 * half,
                                    ("pw", par, half), ("pw3", half))

        def expert_down(w2colap, grp, use_cbc):
            nij = len(grp)
            for oc in range(16):
                par = oc % 2
                wchunk2(w2colap(oc), nij * 128, par)
                whb, wlb = wpr[par]
                b.wait("tensor", s_gp)
                b.wait("tensor", s_ve)
                b.wait("tensor", s_sc)
                for half in range(2):
                    ps = PWH[par][half]
                    b.wait_release("tensor", ("pw", par, half))
                    mm3(ps[:], whb, wlb,
                        lambda ij, h=half: huh[:, TOK * ij + 512 * h:
                                               TOK * ij + 512 * (h + 1)],
                        lambda ij, h=half: hul[:, TOK * ij + 512 * h:
                                               TOK * ij + 512 * (h + 1)],
                        nij)
                    b.inc_last("tensor", s_te)
                    if half == 1:
                        b.release(("wpair", par), s_te)
                    b.wait("vector", s_te)
                    rsl = resid[:, TOK * oc + 512 * half:
                                TOK * oc + 512 * (half + 1)]
                    if use_cbc:
                        b.wait_release("vector", ("evb",))
                        b.tt(evb[:], ps[:],
                             cbc[:, 512 * half:512 * (half + 1)],
                             ALU.mult, sem=s_ve)
                        b.release(("pw", par, half), s_ve)
                        b.wait("vector", s_ve)
                        b.tt(rsl, rsl, evb[:], ALU.add, sem=s_ve)
                        b.release(("evb",), s_ve)
                    else:
                        b.tt(rsl, rsl, ps[:], ALU.add, sem=s_ve)
                        b.release(("pw", par, half), s_ve)
            b.release(("hu",), s_te)

        for e in range(E):
            b.wait_release("sync", ("crow",))
            b.dma("sync", crow[:], comb[e:e + 1, :], s_w)
            b.wait("tensor", s_w)
            for half in range(2):
                ps = PWH[0][half]
                b.wait_release("tensor", ("pw", 0, half))
                b.mm(ps[:], ones_s[:],
                     crow[0:1, 512 * half:512 * (half + 1)], True, True)
                b.inc_last("tensor", s_te)
            b.release(("crow",), s_te)
            b.wait("vector", s_te)
            for half in range(2):
                ps = PWH[0][half]
                b.cp("vector", cbc[:, 512 * half:512 * (half + 1)], ps[:])
                b.inc_last("vector", s_ve)
                b.release(("pw", 0, half), s_ve)
            for grp in GRPS:
                g0, ng = grp[0], len(grp)
                expert_hu(
                    lambda hc, e=e: w1.ap()[(e * HC + hc) * 128:
                                            (e * HC + hc + 1) * 128, :],
                    lambda hc, e=e: w3.ap()[(e * HC + hc) * 128:
                                            (e * HC + hc + 1) * 128, :],
                    grp)
                expert_down(
                    lambda oc, e=e, g0=g0, n=ng:
                    w2.ap()[(e * 16 + oc) * 128:(e * 16 + oc + 1) * 128,
                            128 * g0:128 * (g0 + n)],
                    grp, True)

        for grp in GRPS:
            g0, ng = grp[0], len(grp)
            expert_hu(
                lambda hc: sw1.ap()[128 * hc:128 * (hc + 1), :],
                lambda hc: sw3.ap()[128 * hc:128 * (hc + 1), :],
                grp)
            expert_down(
                lambda oc, g0=g0, n=ng:
                sw2.ap()[128 * oc:128 * (oc + 1), 128 * g0:128 * (g0 + n)],
                grp, False)
        b.barrier(s_bar)
        X7.close()
        X6.close()
        X5.close()

        X9 = ExitStack()
        xb = X9.enter_context(nc.sbuf_tensor("xb9", [128, D], F32))
        for t in range(8):
            b.wait_release("vector", ("yst",))
            for j in range(16):
                r = j % 6
                tp = pT6[r]
                b.wait_release("tensor", ("pT", r))
                b.tr(tp,
                     resid[:, TOK * j + 128 * t:TOK * j + 128 * (t + 1)],
                     id32_s[:], s_te)
                b.wait("vector", s_te)
                b.cp("vector", xb[:, 128 * j:128 * (j + 1)], tp)
                b.inc_last("vector", s_ve)
                b.release(("pT", r), s_ve)
            b.wait("sync", s_ve)
            b.dma("sync", y.ap()[128 * t:128 * (t + 1), :], xb[:], s_out)
            b.release(("yst",), s_out)
        b.wait("sync", s_out)
        X9.close()


def build_fused():
    nc = bass.Bass("TRN2", target_bir_lowering=False,
                   detect_race_conditions=False)
    id32 = nc.dram_tensor("id32", [128, 128], F32, kind="ExternalInput")
    with ExitStack() as XS:
        sems = {k: Sem(XS.enter_context(nc.semaphore(k))) for k in
                ("s_ld", "s_x0", "s_x1", "s_cast", "s_te", "s_ve", "s_sc",
                 "s_stash", "s_qb", "s_kv", "s_pd", "s_cc", "s_out", "s_bar",
                 "s_w", "s_np", "s_gp")}
        b = Bld(nc)
        rs_d = emit_p1(nc, b, sems, id32)
        emit_p2(nc, b, sems, id32, rs_d)
        b.emit()
    return nc



def _t128(a):
    a = np.ascontiguousarray(np.asarray(a, np.float32))
    k, n = a.shape
    return np.ascontiguousarray(
        a.reshape(k // 128, 128, n).swapaxes(0, 1).reshape(128, -1))


def _tcol(a, cw=128):
    a = np.ascontiguousarray(np.asarray(a, np.float32))
    k, n = a.shape
    return np.ascontiguousarray(
        a.reshape(k // 128, 128, n // cw, cw).transpose(2, 1, 0, 3)
        .reshape((n // cw) * 128, (k // 128) * cw))


def _alibi_slopes():
    ratio = 2.0 ** (-8.0 / 16.0)
    return np.array([ratio ** (h + 1) for h in range(16)], dtype=np.float64)


def _p1_inputs(x, Wq, Wk, Wv, Wo, bq, bk, bv):
    f32 = np.float32
    slopes = _alibi_slopes()
    pcol = np.arange(128, dtype=np.float64)[:, None]
    ucol = np.arange(896, dtype=np.float64)[None, :]
    mstrip = np.where(ucol >= pcol + 384, 0.0, -1e9).astype(f32)
    id32 = np.eye(128, dtype=f32)
    idbf = np.eye(128, dtype=BF16NP)
    onesbf = np.ones((1, 128), dtype=BF16NP)
    qi = np.arange(S, dtype=np.float64)
    kt = np.arange(64, dtype=np.float64)[None, :]
    base = 128.0 * kt + pcol - 8191.0

    ins = []
    for c in range(NCOR):
        hsl = slice(256 * c, 256 * (c + 1))
        bqkv = np.zeros((128, 6), f32)
        for i3, bb in enumerate((bq, bk, bv)):
            for oc in range(2):
                bqkv[:, 2 * i3 + oc] = bb[hsl].reshape(2, 128)[oc]
        biask = np.zeros((128, 128), np.float64)
        for oc in range(2):
            biask[:, 2 * np.arange(64) + oc] = slopes[2 * c + oc] * base
        mr = np.stack([-SQ128 * slopes[2 * c + oc] * (qi - 8191.0)
                       for oc in range(2)])
        ins.append({
            "x": x,
            "wq": _t128(Wq[:, hsl]),
            "wk": _t128(Wk[:, hsl]),
            "wv": _t128(Wv[:, hsl]),
            "bqkv": bqkv,
            "wo": _t128(Wo[hsl, :]),
            "biask": biask.astype(f32),
            "mrow": mr.astype(BF16NP),
            "mstrip": mstrip,
            "id32": id32, "idbf": idbf, "onesbf": onesbf,
        })
    return ins


def _p2_shared(L0_attn_bo, L0_ffn_gate_w, L0_ffn_up_w, L0_ffn_down_w,
               L0_ffn_gate_s, L0_ffn_up_s, L0_ffn_down_s,
               L1_attn_wqkv, L1_attn_wo, L1_ffn_gate_w,
               L1_ffn_w1, L1_ffn_w2, L1_ffn_w3,
               L1_ffn_sw1, L1_ffn_sw2, L1_ffn_sw3):
    f32 = np.float32
    wqkv = np.asarray(L1_attn_wqkv, f32)
    v_rows = np.concatenate([np.arange(2176 + 256 * g, 2304 + 256 * g)
                             for g in range(4)])
    w1a = np.asarray(L1_ffn_w1, f32)
    w2a = np.asarray(L1_ffn_w2, f32)
    w3a = np.asarray(L1_ffn_w3, f32)
    return {
        "bo_r": np.ascontiguousarray(
            np.asarray(L0_attn_bo, f32).reshape(16, 128).T),
        "gwT": _tcol(np.asarray(L0_ffn_gate_w, f32).T),
        "uwT": _tcol(np.asarray(L0_ffn_up_w, f32).T),
        "dwT": _tcol(np.asarray(L0_ffn_down_w, f32).T),
        "gsc": np.ascontiguousarray(
            np.asarray(L0_ffn_gate_s, f32).reshape(IC0, 128).T),
        "usc": np.ascontiguousarray(
            np.asarray(L0_ffn_up_s, f32).reshape(IC0, 128).T),
        "dsc": np.ascontiguousarray(
            np.asarray(L0_ffn_down_s, f32).reshape(16, 128).T),
        "vwT": _tcol(wqkv[v_rows, :].T),
        "woT1": _tcol(np.asarray(L1_attn_wo, f32).T),
        "gwT1": _t128(np.asarray(L1_ffn_gate_w, f32).T),
        "w1": np.ascontiguousarray(
            w1a.reshape(E, 16, 128, HC, 128).transpose(0, 3, 2, 1, 4)
            .reshape(E * HC * 128, D)),
        "w2": np.ascontiguousarray(
            w2a.reshape(E, HC, 128, 16, 128).transpose(0, 3, 2, 1, 4)
            .reshape(E * 16 * 128, HC * 128)),
        "w3": np.ascontiguousarray(
            w3a.reshape(E, 16, 128, HC, 128).transpose(0, 3, 2, 1, 4)
            .reshape(E * HC * 128, D)),
        "sw1": _tcol(np.asarray(L1_ffn_sw1, f32)),
        "sw2": _tcol(np.asarray(L1_ffn_sw2, f32)),
        "sw3": _tcol(np.asarray(L1_ffn_sw3, f32)),
        "ones32": np.ones((1, 128), dtype=f32),
    }


def kernel(L0_attn_x, L0_attn_Wq, L0_attn_bq, L0_attn_Wk, L0_attn_bk,
           L0_attn_Wv, L0_attn_bv, L0_attn_Wo, L0_attn_bo,
           L0_ffn_gate_w, L0_ffn_up_w, L0_ffn_down_w,
           L0_ffn_gate_s, L0_ffn_up_s, L0_ffn_down_s,
           L1_attn_wqkv, L1_attn_wo,
           L1_ffn_gate_w, L1_ffn_w1, L1_ffn_w2, L1_ffn_w3,
           L1_ffn_sw1, L1_ffn_sw2, L1_ffn_sw3):
    f32 = np.float32
    x = np.ascontiguousarray(np.asarray(L0_attn_x, dtype=f32).reshape(S, D))

    in1 = _p1_inputs(x, np.asarray(L0_attn_Wq, f32), np.asarray(L0_attn_Wk, f32),
                     np.asarray(L0_attn_Wv, f32), np.asarray(L0_attn_Wo, f32),
                     np.asarray(L0_attn_bq, f32), np.asarray(L0_attn_bk, f32),
                     np.asarray(L0_attn_bv, f32))
    shared = _p2_shared(L0_attn_bo, L0_ffn_gate_w, L0_ffn_up_w,
                        L0_ffn_down_w, L0_ffn_gate_s, L0_ffn_up_s,
                        L0_ffn_down_s, L1_attn_wqkv, L1_attn_wo,
                        L1_ffn_gate_w, L1_ffn_w1, L1_ffn_w2, L1_ffn_w3,
                        L1_ffn_sw1, L1_ffn_sw2, L1_ffn_sw3)
    ins = []
    for c in range(NCOR):
        d = dict(in1[c])
        d.update(shared)
        d["x_own"] = np.ascontiguousarray(x[tok_index(c)])
        ins.append(d)
    nc = build_fused()
    res = run_bass_kernel_spmd(nc, ins, core_ids=list(range(NCOR)))

    out = np.zeros((S, D), dtype=f32)
    for c in range(NCOR):
        out[tok_index(c)] = res.results[c]["y"]
    return out.reshape(1, S, D)


# version 12
# speedup vs baseline: 1.2090x; 1.0260x over previous
import numpy as np
import ml_dtypes
from contextlib import ExitStack

import concourse.bass as bass
import concourse.mybir as mybir
from concourse.bass_utils import run_bass_kernel_spmd

F32 = mybir.dt.float32
BF16 = mybir.dt.bfloat16
AF = mybir.ActivationFunctionType
ALU = mybir.AluOpType
AX = mybir.AxisListType.X

NCOR = 8
S = 8192
D = 2048
NBLK = 16
TOK = 1024
I0 = 5632
IC0 = I0 // 128
E = 8
HE = 1408
HC = HE // 128
SCALE = float(1.0 / np.sqrt(np.float32(128.0)))
EPS = 1e-5
SQ128 = float(np.sqrt(np.float64(128.0)))

ENGINES = ("sync", "tensor", "vector", "scalar", "gpsimd")
DBG = {"qkv_n": 6, "qkv_evac": "ts", "qkv_stash": True}
BF16NP = ml_dtypes.bfloat16


def blocks_of(core):
    return (core, 15 - core)


def tok_index(core):
    b0, b1 = blocks_of(core)
    return np.concatenate([np.arange(512 * b0, 512 * b0 + 512),
                           np.arange(512 * b1, 512 * b1 + 512)])


def shard_slot(b):
    return (b, 0) if b < 8 else (15 - b, 1)


class Sem:
    def __init__(self, h):
        self.h = h
        self.n = 0


class Bld:
    def __init__(self, nc):
        self.nc = nc
        self.q = {e: [] for e in ENGINES}
        self.rel = {}
        self.dma_sems = set()

    def emit(self):
        with self.nc.Block() as block:
            for e in ENGINES:
                ops = self.q[e]
                getattr(block, e)(lambda eng, ops=ops: [op(eng) for op in ops])

    def wait(self, eng, sem, val=None):
        v = sem.n if val is None else val
        if v <= 0:
            return
        g = lambda E, s=sem.h, v=v: E.wait_ge(s, v)
        g.is_op = False
        g.sem = None
        self.q[eng].append(g)

    def release(self, key, sem):
        self.rel[key] = (sem, sem.n)

    def wait_release(self, eng, key):
        if key in self.rel:
            sem, v = self.rel[key]
            self.wait(eng, sem, v)

    def op(self, eng, fn, sem=None, inc=1):
        def g(E, fn=fn, sem=sem, inc=inc):
            i = fn(E)
            if sem is not None:
                i.then_inc(sem.h, inc)
            return i
        g.is_op = True
        g.sem = sem
        self.q[eng].append(g)
        if sem is not None:
            sem.n += inc

    def inc_last(self, eng, sem, inc=1):
        prev = self.q[eng].pop()
        assert getattr(prev, "is_op", False), "inc_last needs a real op"
        assert getattr(prev, "sem", None) is None, \
            "inc_last on an op that already has a sem update"
        def g(E, prev=prev, sem=sem, inc=inc):
            i = prev(E)
            i.then_inc(sem.h, inc)
            return i
        g.is_op = True
        g.sem = sem
        self.q[eng].append(g)
        sem.n += inc

    def dma(self, eng, out, in_, sem, inc=16):
        self.dma_sems.add(sem)
        self.wait(eng, sem)
        self.op(eng, lambda E, o=out, i=in_: E.dma_start(out=o, in_=i),
                sem, inc)

    def mm(self, out, lh, rh, start, stop, sem=None, inc=1):
        self.op("tensor",
                lambda E, o=out, l=lh, r=rh, a=start, b=stop:
                E.matmul(o, l, r, start=a, stop=b), sem, inc)

    def tr(self, out, in_, ident, sem=None):
        self.op("tensor",
                lambda E, o=out, i=in_, d=ident: E.transpose(o, i, d), sem)

    def cp(self, eng, out, in_, sem=None):
        if eng == "scalar":
            self.op(eng, lambda E, o=out, i=in_: E.copy(out=o, in_=i), sem)
        else:
            self.op(eng, lambda E, o=out, i=in_: E.tensor_copy(out=o, in_=i),
                    sem)

    def tt(self, out, a, c, alu, sem=None, eng="vector"):
        self.op(eng, lambda E, o=out, x=a, y=c, u=alu:
                E.tensor_tensor(o, x, y, u), sem)

    def ts(self, out, in0, s1, op0, s2=None, op1=None, sem=None, eng="vector"):
        if op1 is None:
            self.op(eng, lambda E, o=out, i=in0, a=s1, u=op0:
                    E.tensor_scalar(out=o, in0=i, scalar1=a, scalar2=None,
                                    op0=u), sem)
        else:
            self.op(eng, lambda E, o=out, i=in0, a=s1, c=s2, u=op0, v=op1:
                    E.tensor_scalar(out=o, in0=i, scalar1=a, scalar2=c,
                                    op0=u, op1=v), sem)

    def stt(self, out, in0, sc, in1, op0, op1, sem=None, eng="vector"):
        self.op(eng, lambda E, o=out, i=in0, s=sc, j=in1, u=op0, v=op1:
                E.scalar_tensor_tensor(out=o, in0=i, scalar=s, in1=j,
                                       op0=u, op1=v), sem)

    def act(self, out, in_, func, bias=0.0, scale=1.0, accum=None, sem=None):
        self.op("scalar", lambda E, o=out, i=in_, f=func, c=bias, s=scale,
                a=accum:
                E.activation(out=o, in_=i, func=f, bias=c, scale=s,
                             accum_out=a), sem)

    def red(self, out, in_, alu, sem=None):
        self.op("vector", lambda E, o=out, i=in_, u=alu:
                E.tensor_reduce(o, i, AX, u), sem)

    def recip(self, out, in_, sem=None):
        self.op("vector", lambda E, o=out, i=in_: E.reciprocal(o, i), sem)

    def nopinc(self, eng, sem, inc=1):
        self.op(eng, lambda E: E.nop(), sem, inc)

    def memset(self, ap, v, sem=None):
        self.op("gpsimd", lambda E, o=ap, w=v: E.memset(o, w), sem)

    def barrier(self, sem):
        for e in ENGINES:
            for ds in self.dma_sems:
                self.wait(e, ds)
        for e in ENGINES:
            idx = None
            for j in range(len(self.q[e]) - 1, -1, -1):
                if getattr(self.q[e][j], "is_op", False):
                    idx = j
                    break
            if idx is None:
                self.nopinc(e, sem)
                continue
            last_op = self.q[e][idx]
            if last_op.sem is not None:
                self.wait(e, last_op.sem)
                self.nopinc(e, sem)
            else:
                def g(E, prev=last_op, sem=sem):
                    i = prev(E)
                    i.then_inc(sem.h, 1)
                    return i
                g.is_op = True
                g.sem = sem
                self.q[e][idx] = g
                sem.n += 1
        tgt = sem.n
        for e in ENGINES:
            self.wait(e, sem, tgt)



def csplit(ap2d, p=128):
    return ap2d.rearrange("(mj p) o -> p mj o", p=p)


def dsplit(sbuf_ap, owid):
    return sbuf_ap.rearrange("p (mj o) -> p mj o", o=owid)

def emit_ln_tile(b, xb, scr, xntm, st, s_ve, s_sc):
    def v(out, fn):
        b.op("vector", fn, s_ve)
        b.wait("vector", s_ve)
    b.red(st["ssum"][:], xb[:], ALU.add, sem=s_ve)
    b.act(scr[:], xb[:], AF.Square, accum=st["ssq"][:], sem=s_sc)
    b.wait("vector", s_ve)
    v(None, lambda E, o=st["m"][:], i=st["ssum"][:]:
      E.tensor_scalar(out=o, in0=i, scalar1=1.0 / D, scalar2=None,
                      op0=ALU.mult))
    v(None, lambda E, o=st["mneg"][:], i=st["m"][:]:
      E.tensor_scalar(out=o, in0=i, scalar1=-1.0, scalar2=None, op0=ALU.mult))
    v(None, lambda E, o=st["msq"][:], i=st["m"][:]:
      E.tensor_tensor(o, i, i, ALU.mult))
    b.wait("vector", s_sc)
    v(None, lambda E, o=st["var"][:], i=st["ssq"][:]:
      E.tensor_scalar(out=o, in0=i, scalar1=1.0 / D, scalar2=None,
                      op0=ALU.mult))
    v(None, lambda E, o=st["var"][:], i=st["var"][:], j=st["msq"][:]:
      E.tensor_tensor(o, i, j, ALU.subtract))
    v(None, lambda E, o=st["var"][:], i=st["var"][:]:
      E.tensor_scalar(out=o, in0=i, scalar1=EPS, scalar2=None, op0=ALU.add))
    b.wait("scalar", s_ve)
    b.act(st["r2"][:], st["var"][:], AF.Ln, sem=s_sc)
    b.wait("scalar", s_sc)
    b.act(st["rstd"][:], st["r2"][:], AF.Exp, scale=-0.5, sem=s_sc)
    b.wait("scalar", s_sc)
    b.wait("vector", s_sc)
    v(None, lambda E, o=st["nb"][:], i=st["mneg"][:], j=st["rstd"][:]:
      E.tensor_tensor(o, i, j, ALU.mult))
    b.wait("scalar", s_ve)
    b.act(xntm[:], xb[:], AF.Identity, bias=st["nb"][:], scale=st["rstd"][:],
          sem=s_sc)
    b.release(("xnt",), s_sc)



def emit_p1(nc, b, sems, id32, stop_after=None):
    x = nc.dram_tensor("x", [S, D], F32, kind="ExternalInput")
    wq = nc.dram_tensor("wq", [128, 4096], F32, kind="ExternalInput")
    wk = nc.dram_tensor("wk", [128, 4096], F32, kind="ExternalInput")
    wv = nc.dram_tensor("wv", [128, 4096], F32, kind="ExternalInput")
    bqkv = nc.dram_tensor("bqkv", [128, 6], F32, kind="ExternalInput")
    wo = nc.dram_tensor("wo", [128, 2 * D], F32, kind="ExternalInput")
    biask = nc.dram_tensor("biask", [128, 128], F32, kind="ExternalInput")
    mrow = nc.dram_tensor("mrow", [2, S], BF16, kind="ExternalInput")
    mstrip = nc.dram_tensor("mstrip", [128, 896], F32, kind="ExternalInput")
    idbf = nc.dram_tensor("idbf", [128, 128], BF16, kind="ExternalInput")
    onesbf = nc.dram_tensor("onesbf", [1, 128], BF16, kind="ExternalInput")

    part_dump = (nc.dram_tensor("part_dump", [8 * D, TOK], F32,
                                kind="ExternalOutput")
                 if DBG.get("dump_all") else None)
    if DBG.get("dump_ln"):
        ln_dump = nc.dram_tensor("ln_dump", [128, 16], F32, kind="ExternalOutput")
        xn_dump = nc.dram_tensor("xn_dump", [128, D], F32, kind="ExternalOutput")
        xnh_dump = nc.dram_tensor("xnh_dump", [128, 16 * 512], BF16, kind="ExternalOutput")
        xnl_dump = nc.dram_tensor("xnl_dump", [128, 16 * 512], BF16, kind="ExternalOutput")
    if DBG.get("dump_mid"):
        qd_dump = nc.dram_tensor("qd_dump", [512, S], BF16, kind="ExternalOutput")
        kd_dump = nc.dram_tensor("kd_dump", [512, S], BF16, kind="ExternalOutput")
        vd_dump = nc.dram_tensor("vd_dump", [512, 64 * 132], BF16, kind="ExternalOutput")

    qd = nc.dram_tensor("qd", [512, S], BF16)
    kd = nc.dram_tensor("kd", [512, S], BF16)
    vd = nc.dram_tensor("vd", [512, 64 * 132], BF16)
    part_d = nc.dram_tensor("part_d", [16 * D, 512], F32)
    rs_d = nc.dram_tensor("rs_d", [2 * D, 512], F32)

    with ExitStack() as X:
        sb = X.enter_context
        id32_s = sb(nc.sbuf_tensor("id32s", [128, 128], F32))
        idbf_s = sb(nc.sbuf_tensor("idbfs", [128, 128], BF16))
        ones_s = sb(nc.sbuf_tensor("oness", [1, 128], BF16))
        bias_s = sb(nc.sbuf_tensor("biass", [128, 128], F32))
        mstr_s = sb(nc.sbuf_tensor("mstrs", [128, 896], F32))
        bqkv_s = sb(nc.sbuf_tensor("bqkvs", [128, 6], F32))
        st = {k: sb(nc.sbuf_tensor("st_" + k, [128, 1], F32))
              for k in ("ssum", "ssq", "m", "mneg", "msq", "var", "r2",
                        "rstd", "nb")}

        p512 = [sb(nc.psum_tensor("p512a", [128, 512], F32)),
                sb(nc.psum_tensor("p512b", [128, 512], F32))]
        pT = [sb(nc.psum_tensor("pT0", [128, 128], F32)),
              sb(nc.psum_tensor("pT1", [128, 128], F32))]
        XPC = ExitStack()
        pc = [XPC.enter_context(nc.psum_tensor(f"pc{i}", [128, 132], F32))
              for i in range(4)]
        pT6 = [pT[0][:], pT[1][:]] + [pc[i][:, 0:128] for i in range(4)]
        assert nc.psum_banks_remaining >= 0
        assert nc.sbuf_bytes_remaining >= 0

        s_ld, s_x0, s_x1 = sems["s_ld"], sems["s_x0"], sems["s_x1"]
        s_cast, s_te, s_ve, s_sc = (sems["s_cast"], sems["s_te"],
                                    sems["s_ve"], sems["s_sc"])
        s_stash, s_qb, s_kv = sems["s_stash"], sems["s_qb"], sems["s_kv"]
        s_pd, s_cc, s_out, s_bar = (sems["s_pd"], sems["s_cc"],
                                    sems["s_out"], sems["s_bar"])
        s_gp = sems["s_gp"]
        s_x = [s_x0, s_x1]

        XA = ExitStack()
        sa = XA.enter_context
        wf32 = sa(nc.sbuf_tensor("wf32", [128, 4096], F32))
        wh = sa(nc.sbuf_tensor("wh", [128, 3 * 4096], BF16))
        wl = sa(nc.sbuf_tensor("wl", [128, 3 * 4096], BF16))
        xb = sa(nc.sbuf_tensor("xb", [128, 2 * D], F32))
        xntm = sa(nc.sbuf_tensor("xntm", [128, D], F32))
        xnh = sa(nc.sbuf_tensor("xnh", [128, 2 * 16 * 512], BF16))
        xnl = sa(nc.sbuf_tensor("xnl", [128, 2 * 16 * 512], BF16))
        qtmp = sa(nc.sbuf_tensor("qtmp", [128, 2 * 512], F32))
        sth = sa(nc.sbuf_tensor("sth", [128, 2 * 512], BF16))
        stl = sa(nc.sbuf_tensor("stl", [128, 2 * 512], BF16))
        vsgh = sa(nc.sbuf_tensor("vsgh", [128, 4 * 132], BF16))
        vsgl = sa(nc.sbuf_tensor("vsgl", [128, 4 * 132], BF16))
        scr = xntm

        b.dma("sync", id32_s[:], id32.ap(), s_ld)
        b.dma("sync", idbf_s[:], idbf.ap(), s_ld)
        b.dma("sync", ones_s[:], onesbf.ap(), s_ld)
        b.dma("sync", bias_s[:], biask.ap(), s_ld)
        b.dma("sync", mstr_s[:], mstrip.ap(), s_ld)
        b.dma("sync", bqkv_s[:], bqkv.ap(), s_ld)
        for i3, w3 in enumerate((wq, wk, wv)):
            sl = slice(4096 * i3, 4096 * (i3 + 1))
            b.dma("sync", wf32[:], w3.ap(), s_ld)
            b.wait("gpsimd", s_ld)
            b.dma("gpsimd", wh[:, sl], wf32[:], s_cast)
            b.wait("vector", s_cast)
            b.tt(wl[:, sl], wf32[:], wh[:, sl], ALU.subtract, sem=s_ve)
            b.wait("sync", s_ve)
        b.memset(vsgl[:], 0.0)
        b.memset(vsgh[:], 0.0)
        for i in range(4):
            b.memset(vsgh[:, 132 * i + 128:132 * i + 129], 1.0)
        b.barrier(s_bar)

        _ntile = {'prep': 0, 'A1t': 1, 'A1': 4, 'A2': 8}.get(stop_after, S // 128)
        for t in range(_ntile):
            blk, stn = t // 4, t % 4
            p = t % 2
            xt = xb[:, D * p:D * (p + 1)]
            b.wait_release("sync", ("xb", p))
            b.dma("sync", xt, x.ap()[128 * t:128 * (t + 1), :], s_x[p])
            b.wait("vector", s_x[p])
            b.wait("scalar", s_x[p])
            emit_ln_tile(b, xt, scr, xntm, st, s_ve, s_sc)
            b.release(("xb", p), s_sc)
            xno = 8192 * (blk % 2)
            if stn == 0:
                b.wait_release("scalar", ("xn", blk % 2))
                b.wait_release("vector", ("xn", blk % 2))
            if DBG.get("dump_ln") and t == DBG.get("dump_ln_tile", 0):
                b.wait("sync", s_sc)
                for ki, kk in enumerate(("ssum", "ssq", "m", "var", "r2",
                                         "rstd", "nb")):
                    b.op("sync", lambda E, o=ln_dump.ap()[:, ki:ki + 1],
                         i=st[kk][:]: E.dma_start(
                             out=o, in_=i, allow_slow_non_contiguous=True),
                         sems["s_out"], 16)
                b.dma("sync", xn_dump.ap(), xntm[:], sems["s_out"])
                b.wait("vector", sems["s_out"])
                b.wait("scalar", sems["s_out"])
                b.wait("tensor", sems["s_out"])
            for j in range(16):
                r = j % 6
                tp = pT6[r]
                b.wait_release("tensor", ("pT", r))
                b.wait_release("tensor", ("xnt",))
                b.tr(tp, xntm[:, 128 * j:128 * (j + 1)], id32_s[:], s_te)
                col = xno + 512 * j + 128 * stn
                b.wait("scalar", s_te)
                b.cp("scalar", xnh[:, col:col + 128], tp, s_sc)
                b.wait("vector", s_sc)
                b.tt(xnl[:, col:col + 128], tp, xnh[:, col:col + 128],
                     ALU.subtract, sem=s_ve)
                b.release(("pT", r), s_ve)
            b.release(("xntm",), s_te)
            b.wait_release("scalar", ("xntm",))
            if DBG.get("dump_ln") and t == 3:
                b.wait("sync", s_ve)
                b.dma("sync", xnh_dump.ap(), xnh[:], sems["s_out"])
                b.dma("sync", xnl_dump.ap(), xnl[:], sems["s_out"])
            if stn != 3:
                continue
            for m3 in range(3):
                for oc in range(2):
                    if m3 * 2 + oc >= DBG["qkv_n"]:
                        continue
                    g = (m3 * 2 + oc) % 2
                    ps = p512[g]
                    b.wait_release("tensor", ("p512", g))
                    b.wait("tensor", s_ve)
                    nmj = DBG.get("qkv_mj", 16)
                    npass = DBG.get("qkv_pass", 3)
                    for mj in range(nmj):
                        wc = slice(4096 * m3 + 256 * mj + 128 * oc,
                                   4096 * m3 + 256 * mj + 128 * (oc + 1))
                        xc = slice(xno + 512 * mj, xno + 512 * (mj + 1))
                        b.mm(ps[:], wh[:, wc], xnh[:, xc], mj == 0,
                             npass == 1 and mj == nmj - 1)
                        if npass >= 2:
                            b.mm(ps[:], wh[:, wc], xnl[:, xc], False,
                                 npass == 2 and mj == nmj - 1)
                        if npass >= 3:
                            b.mm(ps[:], wl[:, wc], xnh[:, xc], False,
                                 mj == nmj - 1)
                    b.inc_last("tensor", s_te)
                    if m3 == 2 and oc == 1:
                        b.release(("xn", blk % 2), s_te)
                    qcol = slice(512 * g, 512 * (g + 1))
                    b.wait_release("scalar", ("stage", g))
                    b.wait("scalar", s_te)
                    b.act(qtmp[:, qcol], ps[:], AF.Identity,
                          bias=bqkv_s[:, 2 * m3 + oc:2 * m3 + oc + 1],
                          sem=s_sc)
                    b.release(("p512", g), s_sc)
                    b.wait("scalar", s_sc)
                    b.cp("scalar", sth[:, qcol], qtmp[:, qcol], s_sc)
                    b.wait("gpsimd", s_sc)
                    b.tt(stl[:, qcol], qtmp[:, qcol], sth[:, qcol],
                         ALU.subtract, sem=s_gp, eng="gpsimd")
                    if m3 < 2:
                        dd = qd if m3 == 0 else kd
                        tsl = slice(512 * blk, 512 * (blk + 1))
                        b.wait("sync", s_gp)
                        b.dma("sync", dd.ap()[128 * oc:128 * (oc + 1), tsl],
                              sth[:, qcol], s_stash)
                        b.dma("sync",
                              dd.ap()[256 + 128 * oc:256 + 128 * (oc + 1),
                                      tsl],
                              stl[:, qcol], s_stash)
                        b.release(("stage", g), s_stash)
                    else:
                        b.wait_release("scalar", ("vsg",))
                        b.wait_release("vector", ("vsg",))
                        b.wait("tensor", s_sc)
                        for i in range(4):
                            u = i % 2
                            tp = pT[u]
                            b.wait_release("tensor", ("pT", u))
                            b.tr(tp[:],
                                 qtmp[:, 512 * g + 128 * i:
                                      512 * g + 128 * (i + 1)],
                                 id32_s[:], s_te)
                            b.wait("scalar", s_te)
                            b.cp("scalar", vsgh[:, 132 * i:132 * i + 128],
                                 tp[:], s_sc)
                            b.wait("vector", s_sc)
                            b.tt(vsgl[:, 132 * i:132 * i + 128], tp[:],
                                 vsgh[:, 132 * i:132 * i + 128],
                                 ALU.subtract, sem=s_ve)
                            b.release(("pT", u), s_ve)
                        b.wait("sync", s_ve)
                        b.wait("sync", s_sc)
                        vsl = slice(132 * 4 * blk, 132 * 4 * (blk + 1))
                        b.dma("sync", vd.ap()[128 * oc:128 * (oc + 1), vsl],
                              vsgh[:], s_stash)
                        b.dma("sync",
                              vd.ap()[256 + 128 * oc:256 + 128 * (oc + 1),
                                      vsl],
                              vsgl[:], s_stash)
                        b.release(("stage", g), s_stash)
                        b.release(("vsg",), s_stash)

        b.barrier(s_bar)
        if DBG.get("dump_mid"):
            b.dma("gpsimd", qd_dump.ap(), qd.ap(), sems["s_out"])
            b.dma("gpsimd", kd_dump.ap(), kd.ap(), sems["s_out"])
            b.dma("gpsimd", vd_dump.ap(), vd.ap(), sems["s_out"])
            b.wait("gpsimd", sems["s_out"])
        XA.close()
        ctxh = sb(nc.sbuf_tensor("ctxh", [128, 2 * S], BF16))
        ctxl = sb(nc.sbuf_tensor("ctxl", [128, 2 * S], BF16))
        XC = ExitStack()
        sc_ = XC.enter_context
        kh = sc_(nc.sbuf_tensor("khb", [128, S], BF16))
        kl = sc_(nc.sbuf_tensor("klb", [128, S], BF16))
        vh = sc_(nc.sbuf_tensor("vhb", [128, 64 * 132], BF16))
        vl = sc_(nc.sbuf_tensor("vlb", [128, 64 * 132], BF16))
        qhb = sc_(nc.sbuf_tensor("qhb", [128, 2 * 512], BF16))
        qlb = sc_(nc.sbuf_tensor("qlb", [128, 2 * 512], BF16))
        ptb = sc_(nc.sbuf_tensor("ptb", [128, 2 * 512], F32))
        phb = sc_(nc.sbuf_tensor("phb", [128, 2 * 512], BF16))
        plb = sc_(nc.sbuf_tensor("plb", [128, 2 * 512], BF16))
        ctmp = sc_(nc.sbuf_tensor("ctmp", [128, 2 * 128], F32))
        linv = sc_(nc.sbuf_tensor("linv", [128, 2], F32))
        mrow_s = sc_(nc.sbuf_tensor("mrows", [1, S], BF16))

        for hh in range(DBG.get('c_heads', 2) if stop_after in (None, 'C', 'Dnc') else 0):
            b.wait_release("sync", ("kvbuf",))
            b.dma("sync", kh[:], kd.ap()[128 * hh:128 * (hh + 1), :], s_kv)
            b.dma("sync", kl[:], kd.ap()[256 + 128 * hh:256 + 128 * (hh + 1),
                                         :], s_kv)
            b.dma("sync", vh[:], vd.ap()[128 * hh:128 * (hh + 1), :], s_kv)
            b.dma("sync", vl[:], vd.ap()[256 + 128 * hh:256 + 128 * (hh + 1),
                                         :], s_kv)
            b.dma("sync", mrow_s[:], mrow.ap()[hh:hh + 1, :], s_kv)
            b.wait("tensor", s_kv)
            for qb in range(DBG.get('c_qb', NBLK)):
                g2 = qb % 2
                qsl = slice(512 * g2, 512 * (g2 + 1))
                b.wait_release("sync", ("qblk", g2))
                b.dma("sync", qhb[:, qsl],
                      qd.ap()[128 * hh:128 * (hh + 1),
                              512 * qb:512 * (qb + 1)], s_qb)
                b.dma("sync", qlb[:, qsl],
                      qd.ap()[256 + 128 * hh:256 + 128 * (hh + 1),
                              512 * qb:512 * (qb + 1)], s_qb)
                b.wait("tensor", s_qb)
                b.wait_release("tensor", ("pcx",))
                nkt = 4 * qb + 4

                def emit_scores(kt, qb=qb, qsl=qsl, hh=hh):
                    u = kt % 2
                    sp = p512[u]
                    b.wait_release("tensor", ("sp", u))
                    ksl = slice(128 * kt, 128 * (kt + 1))
                    b.mm(sp[:], kh[:, ksl], qhb[:, qsl], True, False)
                    b.mm(sp[:], kh[:, ksl], qlb[:, qsl], False, False)
                    b.mm(sp[:], kl[:, ksl], qhb[:, qsl], False, False)
                    b.mm(sp[:], ones_s[:],
                         mrow_s[0:1, 512 * qb:512 * (qb + 1)], False, True,
                         sem=s_te)
                    if kt >= 4 * qb:
                        off = 384 - (128 * kt - 512 * qb)
                        b.wait("vector", s_te)
                        b.tt(sp[:], sp[:], mstr_s[:, off:off + 512], ALU.add,
                             sem=s_ve)
                        b.wait("scalar", s_ve)
                    b.wait_release("scalar", ("pt", u))
                    b.wait("scalar", s_te)
                    ptc = ptb[:, 512 * u:512 * (u + 1)]
                    b.act(ptc, sp[:], AF.Exp,
                          bias=bias_s[:, 2 * kt + hh:2 * kt + hh + 1],
                          scale=SCALE, sem=s_sc)
                    b.release(("sp", u), s_sc)
                    b.wait("scalar", s_sc)
                    phc = phb[:, 512 * u:512 * (u + 1)]
                    plc = plb[:, 512 * u:512 * (u + 1)]
                    b.wait_release("scalar", ("ph", u))
                    b.cp("scalar", phc, ptc, s_sc)
                    b.wait("vector", s_sc)
                    b.tt(plc, ptc, phc, ALU.subtract, sem=s_ve)
                    b.release(("pt", u), s_ve)
                    b.release(("phrdy", u), s_ve)

                def emit_pv(kt, nkt=nkt):
                    u = kt % 2
                    b.wait_release("tensor", ("phrdy", u))
                    vhc = vh[:, 132 * kt:132 * kt + 129]
                    vlc = vl[:, 132 * kt:132 * kt + 129]
                    for sq in range(4):
                        phs = phb[:, 512 * u + 128 * sq:
                                  512 * u + 128 * (sq + 1)]
                        pls = plb[:, 512 * u + 128 * sq:
                                  512 * u + 128 * (sq + 1)]
                        b.mm(pc[sq][:, 0:129], phs, vhc, kt == 0, False)
                        b.mm(pc[sq][:, 0:129], phs, vlc, False, False)
                        b.mm(pc[sq][:, 0:129], pls, vhc, False,
                             kt == nkt - 1)
                    b.inc_last("tensor", s_te)
                    b.release(("ph", u), s_te)

                for kt in range(nkt):
                    emit_scores(kt)
                    if kt >= 1:
                        emit_pv(kt - 1)
                emit_pv(nkt - 1)
                b.wait("vector", s_te)
                for sq in range(4):
                    b.recip(linv[:, sq % 2:sq % 2 + 1], pc[sq][:, 128:129],
                            sem=s_ve)
                    b.wait("vector", s_ve)
                    cts = ctmp[:, 128 * (sq % 2):128 * (sq % 2 + 1)]
                    b.wait_release("vector", ("ctmp", sq % 2))
                    b.ts(cts, pc[sq][:, 0:128], linv[:, sq % 2:sq % 2 + 1],
                         ALU.mult, sem=s_ve)
                    tp = pT[sq % 2]
                    b.wait_release("tensor", ("pT", sq % 2))
                    b.wait("tensor", s_ve)
                    b.tr(tp[:], cts, id32_s[:], s_te)
                    b.release(("ctmp", sq % 2), s_te)
                    b.wait("scalar", s_te)
                    ccol = S * hh + 512 * qb + 128 * sq
                    b.cp("scalar", ctxh[:, ccol:ccol + 128], tp[:], s_sc)
                    b.wait("vector", s_sc)
                    b.tt(ctxl[:, ccol:ccol + 128], tp[:],
                         ctxh[:, ccol:ccol + 128], ALU.subtract, sem=s_ve)
                    b.release(("pT", sq % 2), s_ve)
                b.release(("pcx",), s_ve)
                b.release(("qblk", g2), s_te)
            b.release(("kvbuf",), s_sc)

        b.barrier(s_bar)
        XC.close()
        XPC.close()
        XD = ExitStack()
        sd = XD.enter_context
        wof = sd(nc.sbuf_tensor("wof", [128, 2 * D], F32))
        woh = sd(nc.sbuf_tensor("woh", [128, 2 * D], BF16))
        wol = sd(nc.sbuf_tensor("wol", [128, 2 * D], BF16))
        ptm = sd(nc.sbuf_tensor("ptm", [128, 4 * 512], F32))
        pdx = [p512[0], p512[1],
               sd(nc.psum_tensor("pd2", [128, 512], F32)),
               sd(nc.psum_tensor("pd3", [128, 512], F32))]

        b.dma("sync", wof[:], wo.ap(), s_ld)
        b.wait("gpsimd", s_ld)
        b.dma("gpsimd", woh[:], wof[:], s_cast)
        b.wait("vector", s_cast)
        b.tt(wol[:], wof[:], woh[:], ALU.subtract, sem=s_ve)
        b.wait("tensor", s_ve)
        for blk in range(DBG.get('d_nblk', NBLK)):
            sh, slot = shard_slot(blk)
            for oc in range(16):
                g = (blk * 16 + oc) % 4
                ps = pdx[g]
                b.wait_release("tensor", ("p512d", g))
                for dh in range(2):
                    wcs = slice(2048 * dh + 128 * oc,
                                2048 * dh + 128 * (oc + 1))
                    csl = slice(S * dh + 512 * blk, S * dh + 512 * (blk + 1))
                    b.mm(ps[:], woh[:, wcs], ctxh[:, csl], dh == 0, False)
                    b.mm(ps[:], woh[:, wcs], ctxl[:, csl], False, False)
                    b.mm(ps[:], wol[:, wcs], ctxh[:, csl], False, dh == 1)
                b.inc_last("tensor", s_te)
                b.wait_release("vector", ("ptm", g))
                b.wait("vector", s_te)
                b.cp("vector", ptm[:, 512 * g:512 * (g + 1)], ps[:])
                b.inc_last("vector", s_ve)
                b.release(("p512d", g), s_ve)
                prow = (2 * sh + slot) * D + 128 * oc
                dq, ds_ = (("sync", s_pd) if g % 2 == 0 else
                           ("scalar", s_stash))
                b.wait(dq, s_ve)
                b.dma(dq, part_d.ap()[prow:prow + 128, :],
                      ptm[:, 512 * g:512 * (g + 1)], ds_)
                b.release(("ptm", g), ds_)
        XD.close()
        b.barrier(s_bar)
        b.op("gpsimd", lambda E: E.collective_compute(
            "ReduceScatter", ALU.add,
            replica_groups=[list(range(NCOR))],
            ins=[part_d.ap()], outs=[rs_d.ap()]), sems["s_cc"])
        b.nopinc("gpsimd", sems["s_np"])
    return rs_d



def emit_p2(nc, b, sems, id32, rs_d):
    x_own = nc.dram_tensor("x_own", [TOK, D], F32, kind="ExternalInput")
    bo_r = nc.dram_tensor("bo_r", [128, 16], F32, kind="ExternalInput")
    guwT = nc.dram_tensor("guwT", [IC0 * 128, 2 * D], F32,
                          kind="ExternalInput")
    dwT = nc.dram_tensor("dwT", [16 * 128, I0], F32, kind="ExternalInput")
    gsc = nc.dram_tensor("gsc", [128, IC0], F32, kind="ExternalInput")
    usc = nc.dram_tensor("usc", [128, IC0], F32, kind="ExternalInput")
    dsc = nc.dram_tensor("dsc", [128, 16], F32, kind="ExternalInput")
    vwT = nc.dram_tensor("vwT", [4 * 128, D], F32, kind="ExternalInput")
    woT1 = nc.dram_tensor("woT1", [16 * 128, D], F32, kind="ExternalInput")
    gwT1 = nc.dram_tensor("gwT1", [128, 128], F32, kind="ExternalInput")
    w1 = nc.dram_tensor("w1", [E * HC * 128, D], F32, kind="ExternalInput")
    w2 = nc.dram_tensor("w2", [E * 16 * 128, HC * 128], F32,
                        kind="ExternalInput")
    w3 = nc.dram_tensor("w3", [E * HC * 128, D], F32, kind="ExternalInput")
    sw1 = nc.dram_tensor("sw1", [HC * 128, D], F32, kind="ExternalInput")
    sw2 = nc.dram_tensor("sw2", [16 * 128, HC * 128], F32,
                         kind="ExternalInput")
    sw3 = nc.dram_tensor("sw3", [HC * 128, D], F32, kind="ExternalInput")
    ones32 = nc.dram_tensor("ones32", [1, 128], F32, kind="ExternalInput")

    y = nc.dram_tensor("y", [TOK, D], F32, kind="ExternalOutput")

    with ExitStack() as X:
        sb = X.enter_context
        id32_s = sb(nc.sbuf_tensor("id32sB", [128, 128], F32))
        ones_s = sb(nc.sbuf_tensor("onessB", [1, 128], F32))
        bo_s = sb(nc.sbuf_tensor("bos", [128, 16], F32))
        gsc_s = sb(nc.sbuf_tensor("gscs", [128, IC0], F32))
        usc_s = sb(nc.sbuf_tensor("uscs", [128, IC0], F32))
        dsc_s = sb(nc.sbuf_tensor("dscs", [128, 16], F32))

        resid = sb(nc.sbuf_tensor("resid", [128, 16 * TOK], F32))
        xnh = sb(nc.sbuf_tensor("xnhB", [128, 16 * TOK], BF16))
        xnl = sb(nc.sbuf_tensor("xnlB", [128, 16 * TOK], BF16))
        st = {k: sb(nc.sbuf_tensor("stB_" + k, [128, 1], F32))
              for k in ("ssum", "ssq", "m", "mneg", "msq", "var", "r2",
                        "rstd", "nb")}

        pA = sb(nc.psum_tensor("pA", [128, 512], F32))
        pB = sb(nc.psum_tensor("pB", [128, 512], F32))
        pC = sb(nc.psum_tensor("pC", [128, 512], F32))
        pD = sb(nc.psum_tensor("pD", [128, 512], F32))
        pE = sb(nc.psum_tensor("pE", [128, 512], F32))
        pF = sb(nc.psum_tensor("pF", [128, 512], F32))
        pT = [sb(nc.psum_tensor("pU0", [128, 128], F32)),
              sb(nc.psum_tensor("pU1", [128, 128], F32))]
        pG = pT[1][:, 0:8]
        pT6 = [pT[0][:], pT[1][:], pA[:, 0:128], pB[:, 0:128],
               pC[:, 0:128], pD[:, 0:128]]
        assert nc.psum_banks_remaining >= 0
        assert nc.sbuf_bytes_remaining >= 0

        s_ld, s_x0, s_x1 = sems["s_ld"], sems["s_x0"], sems["s_x1"]
        s_cast, s_te, s_ve, s_sc = (sems["s_cast"], sems["s_te"],
                                    sems["s_ve"], sems["s_sc"])
        s_w, s_out, s_bar = sems["s_w"], sems["s_out"], sems["s_bar"]
        s_gp = sems["s_gp"]
        s_x = [s_x0, s_x1]

        b.dma("sync", id32_s[:], id32.ap(), s_ld)
        b.dma("sync", ones_s[:], ones32.ap(), s_ld)
        b.dma("sync", bo_s[:], bo_r.ap(), s_ld)
        b.dma("sync", gsc_s[:], gsc.ap(), s_ld)
        b.dma("sync", usc_s[:], usc.ap(), s_ld)
        b.dma("sync", dsc_s[:], dsc.ap(), s_ld)
        for e_ in ENGINES:
            b.wait(e_, s_ld)
        b.barrier(s_bar)

        X1 = ExitStack()
        s1 = X1.enter_context
        xb = s1(nc.sbuf_tensor("xbB", [128, 2 * D], F32))
        xntm = s1(nc.sbuf_tensor("xntmB", [128, D], F32))
        scr = xntm
        for t in range(8):
            p = t % 2
            xt = xb[:, D * p:D * (p + 1)]
            b.wait_release("sync", ("xb", p))
            b.dma("sync", xt, x_own.ap()[128 * t:128 * (t + 1), :], s_x[p])
            b.wait("vector", s_x[p])
            b.wait("scalar", s_x[p])
            emit_ln_tile(b, xt, scr, xntm, st, s_ve, s_sc)
            for j in range(16):
                col = TOK * j + 128 * t
                r0, r1 = (2 * j) % 6, (2 * j + 1) % 6
                tp0 = pT6[r0]
                b.wait_release("tensor", ("pT", r0))
                b.wait("tensor", s_x[p])
                b.tr(tp0, xt[:, 128 * j:128 * (j + 1)], id32_s[:], s_te)
                b.wait("vector", s_te)
                b.cp("vector", resid[:, col:col + 128], tp0)
                b.inc_last("vector", s_ve)
                b.release(("pT", r0), s_ve)
                tp1 = pT6[r1]
                b.wait_release("tensor", ("pT", r1))
                b.wait_release("tensor", ("xnt",))
                b.tr(tp1, xntm[:, 128 * j:128 * (j + 1)], id32_s[:], s_te)
                b.wait("scalar", s_te)
                b.cp("scalar", xnh[:, col:col + 128], tp1, s_sc)
                b.wait("vector", s_sc)
                b.tt(xnl[:, col:col + 128], tp1, xnh[:, col:col + 128],
                     ALU.subtract, sem=s_ve)
                b.release(("pT", r1), s_ve)
            b.release(("xb", p), s_te)
            b.release(("xntm",), s_te)
            b.wait_release("scalar", ("xntm",))
        b.barrier(s_bar)
        X1.close()

        X2 = ExitStack()
        s2_ = X2.enter_context
        huh = s2_(nc.sbuf_tensor("huh", [128, IC0 * 256], BF16))
        hul = s2_(nc.sbuf_tensor("hul", [128, IC0 * 256], BF16))
        gact = s2_(nc.sbuf_tensor("gact", [128, 256], F32))
        uact = s2_(nc.sbuf_tensor("uact", [128, 256], F32))
        humul = s2_(nc.sbuf_tensor("humul", [128, 256], F32))
        wbh = s2_(nc.sbuf_tensor("wbh", [128, 2 * 2048], BF16))
        wbh2 = s2_(nc.sbuf_tensor("wbh2", [128, 2 * 2048], BF16))
        dwhH = [s2_(nc.sbuf_tensor("dwhA", [128, 22 * 128], BF16)),
                s2_(nc.sbuf_tensor("dwhB", [128, 22 * 128], BF16))]
        for qt in range(4):
            for ic in range(IC0):
                g = ic % 2
                wgu = (wbh, wbh2)[g]
                wgh = wgu[:, 0:2048]
                wuh = wgu[:, 2048:4096]
                b.wait_release("gpsimd", ("fw", g))
                b.dma("gpsimd", wgu[:],
                      guwT.ap()[128 * ic:128 * (ic + 1), :], s_cast)
                pa = (pA, pE)[g]
                pb = (pB, pF)[g]
                b.wait("tensor", s_cast)
                b.wait_release("tensor", ("pAB", g))
                b.wait_release("tensor", ("pAB_sc", g))
                for mj in range(16):
                    xsl = slice(TOK * mj + 256 * qt,
                                TOK * mj + 256 * (qt + 1))
                    wc = slice(128 * mj, 128 * (mj + 1))
                    b.mm(pa[:, 0:256], wgh[:, wc], xnh[:, xsl], mj == 0,
                         False)
                    b.mm(pa[:, 0:256], wgh[:, wc], xnl[:, xsl], False,
                         mj == 15)
                for mj in range(16):
                    xsl = slice(TOK * mj + 256 * qt,
                                TOK * mj + 256 * (qt + 1))
                    wc = slice(128 * mj, 128 * (mj + 1))
                    b.mm(pb[:, 0:256], wuh[:, wc], xnh[:, xsl], mj == 0,
                         False)
                    b.mm(pb[:, 0:256], wuh[:, wc], xnl[:, xsl], False,
                         mj == 15)
                b.inc_last("tensor", s_te)
                b.release(("fw", g), s_te)
                b.wait("scalar", s_te)
                b.wait_release("scalar", ("gact",))
                b.act(gact[:], pa[:, 0:256], AF.Silu,
                      scale=gsc_s[:, ic:ic + 1], sem=s_sc)
                b.release(("pAB_sc", g), s_sc)
                b.wait("vector", s_te)
                b.wait_release("vector", ("uact",))
                b.ts(uact[:], pb[:, 0:256], usc_s[:, ic:ic + 1], ALU.mult,
                     sem=s_ve)
                b.release(("pAB", g), s_ve)
                b.wait("vector", s_ve)
                b.wait("vector", s_sc)
                b.wait_release("vector", ("hm2",))
                b.tt(humul[:], gact[:], uact[:], ALU.mult, sem=s_ve)
                b.release(("gact",), s_ve)
                b.release(("uact",), s_ve)
                hucol = 256 * ic
                b.wait("scalar", s_ve)
                b.cp("scalar", huh[:, hucol:hucol + 256], humul[:], s_sc)
                b.release(("hm2",), s_sc)
                b.wait("vector", s_sc)
                b.tt(hul[:, hucol:hucol + 256], humul[:],
                     huh[:, hucol:hucol + 256], ALU.subtract, sem=s_ve)
            for oc in range(16):
                for h in (0, 1):
                    b.wait_release("gpsimd", ("dw", h))
                    b.dma("gpsimd", dwhH[h][:],
                          dwT.ap()[128 * oc:128 * (oc + 1),
                                   2816 * h:2816 * (h + 1)], s_cast)
                pdn = (pC, pD)[oc % 2]
                b.wait("tensor", s_cast)
                b.wait("tensor", s_ve)
                b.wait("tensor", s_sc)
                b.wait_release("tensor", ("pC", oc % 2))
                for ij in range(IC0):
                    h, ijr = divmod(ij, 22)
                    wc = slice(128 * ijr, 128 * (ijr + 1))
                    b.mm(pdn[:, 0:256], dwhH[h][:, wc],
                         huh[:, 256 * ij:256 * (ij + 1)], ij == 0, False)
                    b.mm(pdn[:, 0:256], dwhH[h][:, wc],
                         hul[:, 256 * ij:256 * (ij + 1)], False,
                         ij == IC0 - 1)
                    if ij == 21:
                        b.inc_last("tensor", s_te)
                        b.release(("dw", 0), s_te)
                b.inc_last("tensor", s_te)
                b.release(("dw", 1), s_te)
                b.wait("vector", s_te)
                rsl = resid[:, TOK * oc + 256 * qt:TOK * oc + 256 * (qt + 1)]
                b.stt(rsl, pdn[:, 0:256], dsc_s[:, oc:oc + 1], rsl, ALU.mult,
                      ALU.add, sem=s_ve)
                b.release(("pC", oc % 2), s_ve)
        b.barrier(s_bar)
        X2.close()

        X3 = ExitStack()
        abuf = X3.enter_context(nc.sbuf_tensor("abuf", [128, TOK], F32))
        b.wait("sync", sems["s_cc"], 1)
        for fc in range(16):
            b.wait_release("sync", ("abuf",))
            b.dma("sync", abuf[:, 0:512],
                  rs_d.ap()[128 * fc:128 * (fc + 1), :], s_x0)
            b.dma("sync", abuf[:, 512:1024],
                  rs_d.ap()[D + 128 * fc:D + 128 * (fc + 1), :], s_x0)
            b.wait("vector", s_x0)
            rsl = resid[:, TOK * fc:TOK * (fc + 1)]
            b.stt(rsl, abuf[:], bo_s[:, fc:fc + 1], rsl, ALU.add, ALU.add,
                  sem=s_ve)
            b.release(("abuf",), s_ve)
        b.barrier(s_bar)
        X3.close()

        X4 = ExitStack()
        s4 = X4.enter_context
        xb = s4(nc.sbuf_tensor("xb4", [128, D], F32))
        xntm = s4(nc.sbuf_tensor("xntm4", [128, D], F32))
        scr = xntm
        for t in range(8):
            xt = xb[:]
            b.wait_release("vector", ("xbD4",))
            for j in range(16):
                r = j % 6
                tp = pT6[r]
                b.wait_release("tensor", ("pT", r))
                b.tr(tp,
                     resid[:, TOK * j + 128 * t:TOK * j + 128 * (t + 1)],
                     id32_s[:], s_te)
                b.wait("vector", s_te)
                b.cp("vector", xt[:, 128 * j:128 * (j + 1)], tp)
                b.inc_last("vector", s_ve)
                b.release(("pT", r), s_ve)
            b.wait("vector", s_ve)
            b.wait("scalar", s_ve)
            emit_ln_tile(b, xt, scr, xntm, st, s_ve, s_sc)
            for j in range(16):
                col = TOK * j + 128 * t
                r = j % 6
                tp = pT6[r]
                b.wait_release("tensor", ("pT", r))
                b.wait_release("tensor", ("xnt",))
                b.tr(tp, xntm[:, 128 * j:128 * (j + 1)], id32_s[:], s_te)
                b.wait("scalar", s_te)
                b.cp("scalar", xnh[:, col:col + 128], tp, s_sc)
                b.wait("vector", s_sc)
                b.tt(xnl[:, col:col + 128], tp, xnh[:, col:col + 128],
                     ALU.subtract, sem=s_ve)
                b.release(("pT", r), s_ve)
            b.release(("xbD4",), s_sc)
            b.release(("xntm",), s_te)
            b.wait_release("scalar", ("xntm",))
        b.barrier(s_bar)
        X4.close()

        X5 = ExitStack()
        s5 = X5.enter_context
        wbf = s5(nc.sbuf_tensor("wbf5", [128, 2 * 2048], F32))
        wpr = [(s5(nc.sbuf_tensor("wbh5", [128, 2048], BF16)),
                s5(nc.sbuf_tensor("wbl5", [128, 2048], BF16))),
               (s5(nc.sbuf_tensor("wbh25", [128, 2048], BF16)),
                s5(nc.sbuf_tensor("wbl25", [128, 2048], BF16)))]
        PWH = [(pA, pB), (pE, pF)]

        def wchunk2(col_ap, ncols, pair, hb=None, lb=None):
            hbuf, lbuf = wpr[pair] if hb is None else (hb, lb)
            wsl = wbf[:, 2048 * pair:2048 * pair + ncols]
            b.wait_release("sync", ("wbf", pair))
            b.dma("sync", wsl, col_ap, s_w)
            if hb is None:
                b.wait_release("gpsimd", ("wpair", pair))
            b.dma("gpsimd", hbuf[:, 0:ncols], col_ap, s_cast)
            b.wait("gpsimd", s_w)
            b.wait("gpsimd", s_cast)
            b.tt(lbuf[:, 0:ncols], wsl, hbuf[:, 0:ncols], ALU.subtract,
                 sem=s_gp, eng="gpsimd")
            b.release(("wbf", pair), s_gp)

        def mm3(ps, whb, wlb, xh_of, xl_of, nmj):
            for mj in range(nmj):
                wc = slice(128 * mj, 128 * (mj + 1))
                xh, xl = xh_of(mj), xl_of(mj)
                b.mm(ps, whb[:, wc], xh, mj == 0, False)
                b.mm(ps, whb[:, wc], xl, False, False)
                b.mm(ps, wlb[:, wc], xh, False, mj == nmj - 1)

        def xn_of(half):
            return (lambda mj, h=half: xnh[:, TOK * mj + 512 * h:
                                           TOK * mj + 512 * (h + 1)],
                    lambda mj, h=half: xnl[:, TOK * mj + 512 * h:
                                           TOK * mj + 512 * (h + 1)])

        X5b = ExitStack()
        s5b = X5b.enter_context
        v1h = s5b(nc.sbuf_tensor("v1h", [128, 4 * TOK], BF16))
        v1l = s5b(nc.sbuf_tensor("v1l", [128, 4 * TOK], BF16))
        for vc in range(4):
            pr = vc % 2
            wchunk2(vwT.ap()[128 * vc:128 * (vc + 1), :], 2048, pr)
            whb, wlb = wpr[pr]
            b.wait("tensor", s_gp)
            for half in range(2):
                ps = PWH[pr][half]
                b.wait_release("tensor", ("pw", pr, half))
                fh, fl = xn_of(half)
                mm3(ps[:], whb, wlb, fh, fl, 16)
                b.inc_last("tensor", s_te)
                if half == 1:
                    b.release(("wpair", pr), s_te)
                col = TOK * vc + 512 * half
                b.wait("scalar", s_te)
                b.cp("scalar", v1h[:, col:col + 512], ps[:], s_sc)
                b.wait("vector", s_sc)
                b.tt(v1l[:, col:col + 512], ps[:], v1h[:, col:col + 512],
                     ALU.subtract, sem=s_ve)
                b.release(("pw", pr, half), s_ve)
        for oc in range(16):
            pr = oc % 2
            wchunk2(woT1.ap()[128 * oc:128 * (oc + 1), :], 2048, pr)
            whb, wlb = wpr[pr]
            b.wait("tensor", s_gp)
            for half in range(2):
                ps = PWH[pr][half]
                b.wait_release("tensor", ("pw", pr, half))
                mm3(ps[:], whb, wlb,
                    lambda mj, h=half: v1h[:, TOK * (mj // 4) + 512 * h:
                                           TOK * (mj // 4) + 512 * (h + 1)],
                    lambda mj, h=half: v1l[:, TOK * (mj // 4) + 512 * h:
                                           TOK * (mj // 4) + 512 * (h + 1)],
                    16)
                b.inc_last("tensor", s_te)
                if half == 1:
                    b.release(("wpair", pr), s_te)
                b.wait("vector", s_te)
                rsl = resid[:, TOK * oc + 512 * half:
                            TOK * oc + 512 * (half + 1)]
                b.tt(rsl, rsl, ps[:], ALU.add, sem=s_ve)
                b.release(("pw", pr, half), s_ve)
        b.barrier(s_bar)
        X5b.close()

        X6 = ExitStack()
        s6 = X6.enter_context
        comb = s6(nc.sbuf_tensor("comb", [8, TOK], F32))
        g1h = s6(nc.sbuf_tensor("g1h", [128, 16 * 8], BF16))
        g1l = s6(nc.sbuf_tensor("g1l", [128, 16 * 8], BF16))
        sg8 = s6(nc.sbuf_tensor("sg8", [128, 8], F32))
        t8a = s6(nc.sbuf_tensor("t8a", [128, 8], F32))
        t8b = s6(nc.sbuf_tensor("t8b", [128, 8], F32))
        m81 = s6(nc.sbuf_tensor("m81", [128, 1], F32))
        m82 = s6(nc.sbuf_tensor("m82", [128, 1], F32))
        wchunk2(gwT1.ap(), 128, 0, hb=g1h, lb=g1l)
        b.wait("tensor", s_gp)
        for t in range(8):
            b.wait("tensor", s_ve)
            b.wait_release("tensor", ("pg8",))
            for mj in range(16):
                xh = xnh[:, TOK * mj + 128 * t:TOK * mj + 128 * (t + 1)]
                xl = xnl[:, TOK * mj + 128 * t:TOK * mj + 128 * (t + 1)]
                gc = slice(8 * mj, 8 * (mj + 1))
                b.mm(pG, xh, g1h[:, gc], mj == 0, False)
                b.mm(pG, xl, g1h[:, gc], False, False)
                b.mm(pG, xh, g1l[:, gc], False, mj == 15)
            b.inc_last("tensor", s_te)
            b.wait("scalar", s_te)
            b.act(sg8[:], pG, AF.Sigmoid, sem=s_sc)
            b.wait("vector", s_sc)
            def _vs(fn):
                b.op("vector", fn, s_ve)
                b.wait("vector", s_ve)
            b.red(m81[:], sg8[:], ALU.max, sem=s_ve)
            b.wait("vector", s_ve)
            _vs(lambda E, o=t8a[:], i=sg8[:], s=m81[:]:
                E.tensor_scalar(out=o, in0=i, scalar1=s, scalar2=None,
                                op0=ALU.subtract))
            _vs(lambda E, o=t8a[:], i=t8a[:]:
                E.tensor_scalar(out=o, in0=i, scalar1=0.0, scalar2=None,
                                op0=ALU.is_ge))
            _vs(lambda E, o=t8b[:], i=t8a[:], j=sg8[:]:
                E.scalar_tensor_tensor(out=o, in0=i, scalar=-1e30, in1=j,
                                       op0=ALU.mult, op1=ALU.add))
            b.red(m82[:], t8b[:], ALU.max, sem=s_ve)
            b.wait("vector", s_ve)
            _vs(lambda E, o=t8a[:], i=sg8[:], s=m82[:]:
                E.tensor_scalar(out=o, in0=i, scalar1=s, scalar2=None,
                                op0=ALU.subtract))
            _vs(lambda E, o=t8a[:], i=t8a[:]:
                E.tensor_scalar(out=o, in0=i, scalar1=0.0, scalar2=None,
                                op0=ALU.is_ge))
            _vs(lambda E, o=t8a[:], i=t8a[:], j=sg8[:]:
                E.tensor_tensor(o, i, j, ALU.mult))
            _vs(lambda E, o=m82[:], i=m82[:], j=m81[:]:
                E.tensor_tensor(o, i, j, ALU.add))
            _vs(lambda E, o=m82[:], i=m82[:]:
                E.tensor_scalar(out=o, in0=i, scalar1=1e-20, scalar2=None,
                                op0=ALU.add))
            _vs(lambda E, o=m81[:], i=m82[:]: E.reciprocal(o, i))
            b.ts(t8b[:], t8a[:], m81[:], ALU.mult, s2=2.5, op1=ALU.mult,
                 sem=s_ve)
            b.release(("pg8",), s_ve)
            tp = pT[0]
            b.wait_release("tensor", ("pT", 0))
            b.wait("tensor", s_ve)
            b.tr(tp[0:8, :], t8b[:], id32_s[:], s_te)
            b.wait("vector", s_te)
            b.cp("vector", comb[0:8, 128 * t:128 * (t + 1)], tp[0:8, 0:128])
            b.inc_last("vector", s_ve)
            b.release(("pT", 0), s_ve)
        b.barrier(s_bar)

        X7 = ExitStack()
        s7 = X7.enter_context
        GRPS = [list(range(0, 6)), list(range(6, HC))]
        GMAX = max(len(g) for g in GRPS)
        huh = s7(nc.sbuf_tensor("huh7", [128, GMAX * TOK], BF16))
        hul = s7(nc.sbuf_tensor("hul7", [128, GMAX * TOK], BF16))
        gact = s7(nc.sbuf_tensor("gact7", [128, 512], F32))
        uact = s7(nc.sbuf_tensor("uact7", [128, 512], F32))
        humul = s7(nc.sbuf_tensor("humul7", [128, 512], F32))
        cbc = s7(nc.sbuf_tensor("cbc", [128, TOK], F32))
        crow = s7(nc.sbuf_tensor("crow", [1, TOK], F32))
        evb = s7(nc.sbuf_tensor("evb", [128, 512], F32))

        def silu_mul_split2(pa, pb, hucol, kpa, kpb):
            b.wait("scalar", s_te)
            b.wait_release("scalar", ("gact",))
            b.act(gact[:], pa[:], AF.Silu, sem=s_sc)
            b.release(kpa, s_sc)
            b.wait("vector", s_te)
            b.wait_release("vector", ("uact",))
            b.cp("vector", uact[:], pb[:], sem=s_ve)
            b.release(kpb, s_ve)
            b.wait("vector", s_ve)
            b.wait("vector", s_sc)
            b.wait_release("vector", ("humul",))
            b.tt(humul[:], gact[:], uact[:], ALU.mult, sem=s_ve)
            b.release(("gact",), s_ve)
            b.release(("uact",), s_ve)
            b.wait("scalar", s_ve)
            b.wait_release("scalar", ("hu",))
            b.cp("scalar", huh[:, hucol:hucol + 512], humul[:], s_sc)
            b.release(("humul",), s_sc)
            b.wait("vector", s_sc)
            b.wait_release("vector", ("hu",))
            b.tt(hul[:, hucol:hucol + 512], humul[:],
                 huh[:, hucol:hucol + 512], ALU.subtract, sem=s_ve)

        def expert_hu(w1ap, w3ap, grp):
            for i_hc, hc in enumerate(grp):
                par = i_hc % 2
                wchunk2(w1ap(hc), 2048, 0)
                whb, wlb = wpr[0]
                b.wait("tensor", s_gp)
                for half in range(2):
                    b.wait_release("tensor", ("pw", par, half))
                    fh, fl = xn_of(half)
                    mm3(PWH[par][half][:], whb, wlb, fh, fl, 16)
                    b.inc_last("tensor", s_te)
                b.release(("wpair", 0), s_te)
                wchunk2(w3ap(hc), 2048, 1)
                whb2, wlb2 = wpr[1]
                b.wait("tensor", s_gp)
                for half in range(2):
                    b.wait_release("tensor", ("pw3", half))
                    fh, fl = xn_of(half)
                    mm3((pC, pD)[half][:], whb2, wlb2, fh, fl, 16)
                    b.inc_last("tensor", s_te)
                b.release(("wpair", 1), s_te)
                for half in range(2):
                    silu_mul_split2(PWH[par][half], (pC, pD)[half],
                                    TOK * i_hc + 512 * half,
                                    ("pw", par, half), ("pw3", half))

        def expert_down(w2colap, grp, use_cbc):
            nij = len(grp)
            for oc in range(16):
                par = oc % 2
                wchunk2(w2colap(oc), nij * 128, par)
                whb, wlb = wpr[par]
                b.wait("tensor", s_gp)
                b.wait("tensor", s_ve)
                b.wait("tensor", s_sc)
                for half in range(2):
                    ps = PWH[par][half]
                    b.wait_release("tensor", ("pw", par, half))
                    mm3(ps[:], whb, wlb,
                        lambda ij, h=half: huh[:, TOK * ij + 512 * h:
                                               TOK * ij + 512 * (h + 1)],
                        lambda ij, h=half: hul[:, TOK * ij + 512 * h:
                                               TOK * ij + 512 * (h + 1)],
                        nij)
                    b.inc_last("tensor", s_te)
                    if half == 1:
                        b.release(("wpair", par), s_te)
                    b.wait("vector", s_te)
                    rsl = resid[:, TOK * oc + 512 * half:
                                TOK * oc + 512 * (half + 1)]
                    if use_cbc:
                        b.wait_release("vector", ("evb",))
                        b.tt(evb[:], ps[:],
                             cbc[:, 512 * half:512 * (half + 1)],
                             ALU.mult, sem=s_ve)
                        b.release(("pw", par, half), s_ve)
                        b.wait("vector", s_ve)
                        b.tt(rsl, rsl, evb[:], ALU.add, sem=s_ve)
                        b.release(("evb",), s_ve)
                    else:
                        b.tt(rsl, rsl, ps[:], ALU.add, sem=s_ve)
                        b.release(("pw", par, half), s_ve)
            b.release(("hu",), s_te)

        for e in range(E):
            b.wait_release("sync", ("crow",))
            b.dma("sync", crow[:], comb[e:e + 1, :], s_w)
            b.wait("tensor", s_w)
            for half in range(2):
                ps = PWH[0][half]
                b.wait_release("tensor", ("pw", 0, half))
                b.mm(ps[:], ones_s[:],
                     crow[0:1, 512 * half:512 * (half + 1)], True, True)
                b.inc_last("tensor", s_te)
            b.release(("crow",), s_te)
            b.wait("vector", s_te)
            for half in range(2):
                ps = PWH[0][half]
                b.cp("vector", cbc[:, 512 * half:512 * (half + 1)], ps[:])
                b.inc_last("vector", s_ve)
                b.release(("pw", 0, half), s_ve)
            for grp in GRPS:
                g0, ng = grp[0], len(grp)
                expert_hu(
                    lambda hc, e=e: w1.ap()[(e * HC + hc) * 128:
                                            (e * HC + hc + 1) * 128, :],
                    lambda hc, e=e: w3.ap()[(e * HC + hc) * 128:
                                            (e * HC + hc + 1) * 128, :],
                    grp)
                expert_down(
                    lambda oc, e=e, g0=g0, n=ng:
                    w2.ap()[(e * 16 + oc) * 128:(e * 16 + oc + 1) * 128,
                            128 * g0:128 * (g0 + n)],
                    grp, True)

        for grp in GRPS:
            g0, ng = grp[0], len(grp)
            expert_hu(
                lambda hc: sw1.ap()[128 * hc:128 * (hc + 1), :],
                lambda hc: sw3.ap()[128 * hc:128 * (hc + 1), :],
                grp)
            expert_down(
                lambda oc, g0=g0, n=ng:
                sw2.ap()[128 * oc:128 * (oc + 1), 128 * g0:128 * (g0 + n)],
                grp, False)
        b.barrier(s_bar)
        X7.close()
        X6.close()
        X5.close()

        X9 = ExitStack()
        xb = X9.enter_context(nc.sbuf_tensor("xb9", [128, D], F32))
        for t in range(8):
            b.wait_release("vector", ("yst",))
            for j in range(16):
                r = j % 6
                tp = pT6[r]
                b.wait_release("tensor", ("pT", r))
                b.tr(tp,
                     resid[:, TOK * j + 128 * t:TOK * j + 128 * (t + 1)],
                     id32_s[:], s_te)
                b.wait("vector", s_te)
                b.cp("vector", xb[:, 128 * j:128 * (j + 1)], tp)
                b.inc_last("vector", s_ve)
                b.release(("pT", r), s_ve)
            b.wait("sync", s_ve)
            b.dma("sync", y.ap()[128 * t:128 * (t + 1), :], xb[:], s_out)
            b.release(("yst",), s_out)
        b.wait("sync", s_out)
        X9.close()


def build_fused():
    nc = bass.Bass("TRN2", target_bir_lowering=False,
                   detect_race_conditions=False)
    id32 = nc.dram_tensor("id32", [128, 128], F32, kind="ExternalInput")
    with ExitStack() as XS:
        sems = {k: Sem(XS.enter_context(nc.semaphore(k))) for k in
                ("s_ld", "s_x0", "s_x1", "s_cast", "s_te", "s_ve", "s_sc",
                 "s_stash", "s_qb", "s_kv", "s_pd", "s_cc", "s_out", "s_bar",
                 "s_w", "s_np", "s_gp")}
        b = Bld(nc)
        rs_d = emit_p1(nc, b, sems, id32)
        emit_p2(nc, b, sems, id32, rs_d)
        b.emit()
    return nc



def _t128(a):
    a = np.ascontiguousarray(np.asarray(a, np.float32))
    k, n = a.shape
    return np.ascontiguousarray(
        a.reshape(k // 128, 128, n).swapaxes(0, 1).reshape(128, -1))


def _tcol(a, cw=128):
    a = np.ascontiguousarray(np.asarray(a, np.float32))
    k, n = a.shape
    return np.ascontiguousarray(
        a.reshape(k // 128, 128, n // cw, cw).transpose(2, 1, 0, 3)
        .reshape((n // cw) * 128, (k // 128) * cw))


def _alibi_slopes():
    ratio = 2.0 ** (-8.0 / 16.0)
    return np.array([ratio ** (h + 1) for h in range(16)], dtype=np.float64)


def _p1_inputs(x, Wq, Wk, Wv, Wo, bq, bk, bv):
    f32 = np.float32
    slopes = _alibi_slopes()
    pcol = np.arange(128, dtype=np.float64)[:, None]
    ucol = np.arange(896, dtype=np.float64)[None, :]
    mstrip = np.where(ucol >= pcol + 384, 0.0, -1e9).astype(f32)
    id32 = np.eye(128, dtype=f32)
    idbf = np.eye(128, dtype=BF16NP)
    onesbf = np.ones((1, 128), dtype=BF16NP)
    qi = np.arange(S, dtype=np.float64)
    kt = np.arange(64, dtype=np.float64)[None, :]
    base = 128.0 * kt + pcol - 8191.0

    ins = []
    for c in range(NCOR):
        hsl = slice(256 * c, 256 * (c + 1))
        bqkv = np.zeros((128, 6), f32)
        for i3, bb in enumerate((bq, bk, bv)):
            for oc in range(2):
                bqkv[:, 2 * i3 + oc] = bb[hsl].reshape(2, 128)[oc]
        biask = np.zeros((128, 128), np.float64)
        for oc in range(2):
            biask[:, 2 * np.arange(64) + oc] = slopes[2 * c + oc] * base
        mr = np.stack([-SQ128 * slopes[2 * c + oc] * (qi - 8191.0)
                       for oc in range(2)])
        ins.append({
            "x": x,
            "wq": _t128(Wq[:, hsl]),
            "wk": _t128(Wk[:, hsl]),
            "wv": _t128(Wv[:, hsl]),
            "bqkv": bqkv,
            "wo": _t128(Wo[hsl, :]),
            "biask": biask.astype(f32),
            "mrow": mr.astype(BF16NP),
            "mstrip": mstrip,
            "id32": id32, "idbf": idbf, "onesbf": onesbf,
        })
    return ins


def _p2_shared(L0_attn_bo, L0_ffn_gate_w, L0_ffn_up_w, L0_ffn_down_w,
               L0_ffn_gate_s, L0_ffn_up_s, L0_ffn_down_s,
               L1_attn_wqkv, L1_attn_wo, L1_ffn_gate_w,
               L1_ffn_w1, L1_ffn_w2, L1_ffn_w3,
               L1_ffn_sw1, L1_ffn_sw2, L1_ffn_sw3):
    f32 = np.float32
    wqkv = np.asarray(L1_attn_wqkv, f32)
    v_rows = np.concatenate([np.arange(2176 + 256 * g, 2304 + 256 * g)
                             for g in range(4)])
    w1a = np.asarray(L1_ffn_w1, f32)
    w2a = np.asarray(L1_ffn_w2, f32)
    w3a = np.asarray(L1_ffn_w3, f32)
    return {
        "bo_r": np.ascontiguousarray(
            np.asarray(L0_attn_bo, f32).reshape(16, 128).T),
        "guwT": np.ascontiguousarray(np.concatenate(
            [_tcol(np.asarray(L0_ffn_gate_w, f32).T),
             _tcol(np.asarray(L0_ffn_up_w, f32).T)], axis=1)),
        "dwT": _tcol(np.asarray(L0_ffn_down_w, f32).T),
        "gsc": np.ascontiguousarray(
            np.asarray(L0_ffn_gate_s, f32).reshape(IC0, 128).T),
        "usc": np.ascontiguousarray(
            np.asarray(L0_ffn_up_s, f32).reshape(IC0, 128).T),
        "dsc": np.ascontiguousarray(
            np.asarray(L0_ffn_down_s, f32).reshape(16, 128).T),
        "vwT": _tcol(wqkv[v_rows, :].T),
        "woT1": _tcol(np.asarray(L1_attn_wo, f32).T),
        "gwT1": _t128(np.asarray(L1_ffn_gate_w, f32).T),
        "w1": np.ascontiguousarray(
            w1a.reshape(E, 16, 128, HC, 128).transpose(0, 3, 2, 1, 4)
            .reshape(E * HC * 128, D)),
        "w2": np.ascontiguousarray(
            w2a.reshape(E, HC, 128, 16, 128).transpose(0, 3, 2, 1, 4)
            .reshape(E * 16 * 128, HC * 128)),
        "w3": np.ascontiguousarray(
            w3a.reshape(E, 16, 128, HC, 128).transpose(0, 3, 2, 1, 4)
            .reshape(E * HC * 128, D)),
        "sw1": _tcol(np.asarray(L1_ffn_sw1, f32)),
        "sw2": _tcol(np.asarray(L1_ffn_sw2, f32)),
        "sw3": _tcol(np.asarray(L1_ffn_sw3, f32)),
        "ones32": np.ones((1, 128), dtype=f32),
    }


def kernel(L0_attn_x, L0_attn_Wq, L0_attn_bq, L0_attn_Wk, L0_attn_bk,
           L0_attn_Wv, L0_attn_bv, L0_attn_Wo, L0_attn_bo,
           L0_ffn_gate_w, L0_ffn_up_w, L0_ffn_down_w,
           L0_ffn_gate_s, L0_ffn_up_s, L0_ffn_down_s,
           L1_attn_wqkv, L1_attn_wo,
           L1_ffn_gate_w, L1_ffn_w1, L1_ffn_w2, L1_ffn_w3,
           L1_ffn_sw1, L1_ffn_sw2, L1_ffn_sw3):
    f32 = np.float32
    x = np.ascontiguousarray(np.asarray(L0_attn_x, dtype=f32).reshape(S, D))

    in1 = _p1_inputs(x, np.asarray(L0_attn_Wq, f32), np.asarray(L0_attn_Wk, f32),
                     np.asarray(L0_attn_Wv, f32), np.asarray(L0_attn_Wo, f32),
                     np.asarray(L0_attn_bq, f32), np.asarray(L0_attn_bk, f32),
                     np.asarray(L0_attn_bv, f32))
    shared = _p2_shared(L0_attn_bo, L0_ffn_gate_w, L0_ffn_up_w,
                        L0_ffn_down_w, L0_ffn_gate_s, L0_ffn_up_s,
                        L0_ffn_down_s, L1_attn_wqkv, L1_attn_wo,
                        L1_ffn_gate_w, L1_ffn_w1, L1_ffn_w2, L1_ffn_w3,
                        L1_ffn_sw1, L1_ffn_sw2, L1_ffn_sw3)
    ins = []
    for c in range(NCOR):
        d = dict(in1[c])
        d.update(shared)
        d["x_own"] = np.ascontiguousarray(x[tok_index(c)])
        ins.append(d)
    nc = build_fused()
    res = run_bass_kernel_spmd(nc, ins, core_ids=list(range(NCOR)))

    out = np.zeros((S, D), dtype=f32)
    for c in range(NCOR):
        out[tok_index(c)] = res.results[c]["y"]
    return out.reshape(1, S, D)
